# version 19
# baseline (speedup 1.0000x reference)
import numpy as np
from contextlib import ExitStack
import ml_dtypes
import concourse.bass as bass
import concourse.mybir as mybir
from concourse.bass_utils import run_bass_kernel_spmd

F32 = mybir.dt.float32
BF16 = mybir.dt.bfloat16
AF = mybir.ActivationFunctionType
ALU = mybir.AluOpType
AX = mybir.AxisListType

D = 4096
ND = 32
DH = 128
H = 32
SCALE = float(1.0 / np.sqrt(128.0))
EPS = 1e-5
TT = 512


class Ct:
    __slots__ = ("h", "n")
    def __init__(self, h):
        self.h = h
        self.n = 0


def build_nc(n_cores, core_t):
    ext = core_t + 512
    n_tiles = core_t // TT
    n_ext = ext // TT
    nqc = core_t // 128
    nqb = core_t // 256
    nek = ext // 128
    jpc = 24576 // (128 * n_cores)
    n_jb = (jpc + 23) // 24

    nc = bass.Bass("TRN2", target_bir_lowering=False, debug=False)

    ap = lambda t: t.ap()
    xT = ap(nc.dram_tensor("xT", [D, ext], F32, kind="ExternalInput"))
    wqT = ap(nc.dram_tensor("wqT", [D, D], F32, kind="ExternalInput"))
    woT = ap(nc.dram_tensor("woT", [D, D], F32, kind="ExternalInput"))
    wfT = ap(nc.dram_tensor("wfT", [D, D], F32, kind="ExternalInput"))
    wkT = ap(nc.dram_tensor("wkT", [D, DH], F32, kind="ExternalInput"))
    wvT = ap(nc.dram_tensor("wvT", [D, DH], F32, kind="ExternalInput"))
    wmodS = ap(nc.dram_tensor("wmodS", [D, 128 * jpc], F32, kind="ExternalInput"))
    cvec = ap(nc.dram_tensor("cvec", [1, D], F32, kind="ExternalInput"))
    gwT = ap(nc.dram_tensor("gwT", [D, 8], F32, kind="ExternalInput"))
    icsM = ap(nc.dram_tensor("icsM", [8, D], F32, kind="ExternalInput"))
    ocsM = ap(nc.dram_tensor("ocsM", [8, D], F32, kind="ExternalInput"))
    biasM = ap(nc.dram_tensor("biasM", [128, 32], F32, kind="ExternalInput"))
    cosE = ap(nc.dram_tensor("cosE", [128, ext], BF16, kind="ExternalInput"))
    sinE = ap(nc.dram_tensor("sinE", [128, ext], BF16, kind="ExternalInput"))
    aABm = ap(nc.dram_tensor("aABm", [128, 2 * nqc], F32, kind="ExternalInput"))
    m3m = ap(nc.dram_tensor("m3m", [128, nqb], F32, kind="ExternalInput"))
    vmkm = ap(nc.dram_tensor("vmkm", [128, nek], F32, kind="ExternalInput"))
    idf = ap(nc.dram_tensor("idf", [128, 128], F32, kind="ExternalInput"))
    onesbI = ap(nc.dram_tensor("onesbI", [1, 128], BF16, kind="ExternalInput"))
    onescI = ap(nc.dram_tensor("onescI", [128, 1], F32, kind="ExternalInput"))
    idbI = ap(nc.dram_tensor("idbI", [128, 128], BF16, kind="ExternalInput"))

    scrq = ap(nc.dram_tensor("scrq", [32, 128, 32, 128], BF16, kind="Internal"))
    scro = ap(nc.dram_tensor("scro", [32, 128, 32, 128], BF16, kind="Internal"))
    scrf = ap(nc.dram_tensor("scrf", [32, 128, 32, 128], BF16, kind="Internal"))
    ccin = ap(nc.dram_tensor("ccin", [jpc, 128], F32, kind="Internal"))
    ccout = ap(nc.dram_tensor("ccout", [jpc * n_cores, 128], F32, kind="Internal"))

    outT = ap(nc.dram_tensor("outT", [D, core_t], F32, kind="ExternalOutput"))

    sched = {k: [] for k in ("sync", "tensor", "vector", "scalar", "gpsimd")}

    def em(eng, fn):
        sched[eng].append(fn)

    def WAIT(eng, ct, v):
        if v > 0:
            em(eng, lambda e, h=ct.h, v=v: e.wait_ge(h, v))

    def WAITV(eng, tv):
        WAIT(eng, tv[0], tv[1])

    def DMA(eng, ct, out, in_, **kw):
        ct.n += 16
        em(eng, lambda e, o=out, i=in_, h=ct.h, kw=dict(kw): e.dma_start(out=o, in_=i, **kw).then_inc(h, 16))
        return ct.n

    def TDMA(ct, out, in_):
        ct.n += 16
        em("scalar", lambda e, o=out, i=in_, h=ct.h: e.dma_start(out=o, in_=i, transpose=True).then_inc(h, 16))
        return ct.n

    ENG_CT = {}

    def OP(eng, fn):
        ct = ENG_CT[eng]
        if eng != "tensor" and ct.n > 0:
            em(eng, lambda e, h=ct.h, v=ct.n: e.wait_ge(h, v))
        ct.n += 1
        em(eng, lambda e, fn=fn, h=ct.h: fn(e).then_inc(h, 1))
        return ct.n

    def MM(fn):
        em("tensor", lambda e, fn=fn: fn(e))

    with ExitStack() as _st:
        ent = _st.enter_context
        xsb = ent(nc.sbuf_tensor("xsb", [128, ND * TT], F32))
        hsb = ent(nc.sbuf_tensor("hsb", [128, ND * TT], BF16))
        aosb = ent(nc.sbuf_tensor("aosb", [128, ND * TT], BF16))
        kres = ent(nc.sbuf_tensor("kres", [128, ext], BF16))
        vtm = ent(nc.sbuf_tensor("vtm", [128, ext], BF16))
        cossb = ent(nc.sbuf_tensor("cossb", [128, TT], BF16))
        sinsb = ent(nc.sbuf_tensor("sinsb", [128, TT], BF16))
        qslot = ent(nc.sbuf_tensor("qslot", [128, 1024], BF16))
        sig = ent(nc.sbuf_tensor("sig", [128, 1536], BF16))
        psiT = ent(nc.sbuf_tensor("psiT", [128, 1536], BF16, is_transpose_dest=True))
        scrA = ent(nc.sbuf_tensor("scrA", [128, 512], F32))
        scrB = ent(nc.sbuf_tensor("scrB", [128, 512], F32))
        scrC = ent(nc.sbuf_tensor("scrC", [128, 512], F32))
        scrD = ent(nc.sbuf_tensor("scrD", [128, 512], F32))
        mrow = ent(nc.sbuf_tensor("mrow", [1, core_t], BF16))
        rrow = ent(nc.sbuf_tensor("rrow", [1, core_t], BF16))
        rowsb = ent(nc.sbuf_tensor("rowsb", [1, 1024], BF16))
        ostg = ent(nc.sbuf_tensor("ostg", [128, 512], F32))
        wslA = ent(nc.sbuf_tensor("wslA", [128, 2048], BF16))
        wslB = ent(nc.sbuf_tensor("wslB", [128, 2048], BF16))
        pp0 = ent(nc.sbuf_tensor("pp0", [128, 1024], BF16))
        pp1 = ent(nc.sbuf_tensor("pp1", [128, 1024], BF16))
        wslC = ent(nc.sbuf_tensor("wslC", [128, 2048], BF16))
        wslD = ent(nc.sbuf_tensor("wslD", [128, 2048], BF16))
        icsb = ent(nc.sbuf_tensor("icsb", [8, D], BF16))
        ocsb = ent(nc.sbuf_tensor("ocsb", [8, D], BF16))
        gwb = ent(nc.sbuf_tensor("gwb", [128, 256], BF16))
        idfs = ent(nc.sbuf_tensor("idfs", [128, 128], F32))
        idbs = ent(nc.sbuf_tensor("idbs", [128, 128], BF16))
        onesb = ent(nc.sbuf_tensor("onesb", [1, 128], BF16))
        onesc = ent(nc.sbuf_tensor("onesc", [128, 1], F32))
        aAB = ent(nc.sbuf_tensor("aAB", [128, 2 * nqc], F32))
        m3s = ent(nc.sbuf_tensor("m3s", [128, nqb], F32))
        vmk = ent(nc.sbuf_tensor("vmk", [128, nek], F32))
        biasC = ent(nc.sbuf_tensor("biasC", [128, 32], F32))
        scol32 = ent(nc.sbuf_tensor("scol32", [128, 32], F32))
        camod = ent(nc.sbuf_tensor("camod", [128, 224], F32))
        sums = ent(nc.sbuf_tensor("sums", [128, 16], F32))
        coefs = ent(nc.sbuf_tensor("coefs", [128, 8], F32))
        nmx = ent(nc.sbuf_tensor("nmx", [128, 4], F32))
        pQ0 = ent(nc.psum_tensor("pQ0", [128, 512], F32))
        pQ1 = ent(nc.psum_tensor("pQ1", [128, 512], F32))
        pS = ent(nc.psum_tensor("pS", [128, 1024], F32))
        pVo0 = ent(nc.psum_tensor("pVo0", [128, 512], F32))
        pVo1 = ent(nc.psum_tensor("pVo1", [128, 512], F32))
        pTb = ent(nc.psum_tensor("pTb", [128, 1024], BF16))
        pTb2 = ent(nc.psum_tensor("pTb2", [128, 1024], BF16))
        _sC = ent(nc.semaphore("smC"))
        _sX = ent(nc.semaphore("smX"))
        _sCS = ent(nc.semaphore("smCS"))
        _sSlA = ent(nc.semaphore("smSlA"))
        _sSlB = ent(nc.semaphore("smSlB"))
        _sGA = ent(nc.semaphore("smGA"))
        _sGB = ent(nc.semaphore("smGB"))
        _sScr = ent(nc.semaphore("smScr"))
        _sScrB = ent(nc.semaphore("smScrB"))
        _sCc = ent(nc.semaphore("smCc"))
        _sCo = ent(nc.semaphore("smCo"))
        _sOutB = ent(nc.semaphore("smOutB"))
        _sGc = ent(nc.semaphore("smGc"))
        _sPpA = ent(nc.semaphore("smPpA"))
        _sPpB = ent(nc.semaphore("smPpB"))
        _sGP = ent(nc.semaphore("smGP"))
        _sOgA = ent(nc.semaphore("smOgA"))
        _sOgB = ent(nc.semaphore("smOgB"))
        _sXg = ent(nc.semaphore("smXg"))
        _sCSg = ent(nc.semaphore("smCSg"))
        _sSlC = ent(nc.semaphore("smSlC"))
        _sSlD = ent(nc.semaphore("smSlD"))
        _sTpA = ent(nc.semaphore("smTpA"))
        _sTpB = ent(nc.semaphore("smTpB"))
        _sOut = ent(nc.semaphore("smOut"))
        _sCC = ent(nc.semaphore("smCC"))
        _sPE = ent(nc.semaphore("smPE"))
        _sACT = ent(nc.semaphore("smACT"))
        _sDVE = ent(nc.semaphore("smDVE"))
        block = ent(nc.Block())
        sC, sX, sCS = Ct(_sC), Ct(_sX), Ct(_sCS)
        sSl = [Ct(_sSlA), Ct(_sSlB)]
        sG = [Ct(_sGA), Ct(_sGB)]
        sScr = [Ct(_sScr), Ct(_sScrB)]
        sCc, sCo, sGc = Ct(_sCc), Ct(_sCo), Ct(_sGc)
        sPp = [Ct(_sPpA), Ct(_sPpB)]
        sGP = Ct(_sGP)
        sOg = [Ct(_sOgA), Ct(_sOgB)]
        sXg, sCSg = Ct(_sXg), Ct(_sCSg)
        sSl3 = [sSl[0], sSl[1], Ct(_sSlC), Ct(_sSlD)]
        sTp = [Ct(_sTpA), Ct(_sTpB)]
        sOut = [Ct(_sOut), Ct(_sOutB)]
        sCC = Ct(_sCC)
        sPE, sACT, sDVE = Ct(_sPE), Ct(_sACT), Ct(_sDVE)
        ENG_CT.update({"tensor": sPE, "vector": sDVE, "scalar": sACT, "gpsimd": sGP})

        CA1, SAv, GAC, CB1, SBv, GBC, GBIAS = (camod[:, 32 * i:32 * i + 32] for i in range(7))

        modpart = scrA[:, 384:408]
        routb = sig[:, 0:8]
        routT = psiT[0:8, 0:512]
        modcol = scrB[:, 0:192]
        routf = coefs[:, 0:8]

        scrUse = {"a": 0, "v": 0, "t": 0}

        def mark_scratch():
            scrUse["a"], scrUse["v"], scrUse["t"] = sACT.n, sDVE.n, sPE.n

        def wait_scratch(eng):
            if eng != "scalar":
                WAIT(eng, sACT, scrUse["a"])
            if eng != "vector":
                WAIT(eng, sDVE, scrUse["v"])
            if eng != "tensor":
                WAIT(eng, sPE, scrUse["t"])

        DMA("sync", sC, idfs[:], idf)
        DMA("sync", sC, idbs[:], idbI)
        DMA("sync", sC, onesb[:], onesbI)
        DMA("sync", sC, onesc[:], onescI)
        DMA("sync", sC, aAB[:], aABm)
        DMA("sync", sC, m3s[:], m3m)
        DMA("sync", sC, vmk[:], vmkm)
        DMA("sync", sC, biasC[:], biasM)
        DMA("sync", sC, scrB[0:32, 0:128], cvec.rearrange("o (a b) -> (o a) b", b=128))
        vC = sC.n
        DMA("gpsimd", sGc, gwb[:].rearrange("p (b e) -> p b e", e=8),
            gwT.rearrange("(b p) e -> p b e", p=128))
        DMA("gpsimd", sGc, icsb[:], icsM)
        DMA("gpsimd", sGc, ocsb[:], ocsM)
        DMA("gpsimd", sGc, aosb[:, 0:4096].rearrange("p (b f) -> p b f", f=128),
            wkT.rearrange("(b p) f -> p b f", p=128))
        DMA("gpsimd", sGc, aosb[:, 4096:8192].rearrange("p (b f) -> p b f", f=128),
            wvT.rearrange("(b p) f -> p b f", p=128))
        vGsmall = sGc.n
        DMA("sync", sX, xsb[:, 0:8192].rearrange("p (b t) -> p b t", t=TT),
            xT[0:2048, 0:TT].rearrange("(b p) t -> p b t", p=128))
        DMA("sync", sX, xsb[:, 14336:16384].rearrange("p (b t) -> p b t", t=TT),
            xT[3584:4096, 0:TT].rearrange("(b p) t -> p b t", p=128))
        DMA("sync", sCS, cossb[:], cosE[:, 0:TT])
        DMA("sync", sCS, sinsb[:], sinE[:, 0:TT])
        vCSl0 = sCS.n

        WAIT("scalar", sC, vC)
        vSilu = OP("scalar", lambda e: e.activation(out=scrB[0:32, 128:256], in_=scrB[0:32, 0:128], func=AF.Silu))
        WAIT("tensor", sACT, vSilu)
        WAIT("tensor", sC, vC)
        vTs = OP("tensor", lambda e: e.transpose(pQ1[:, 0:32], scrB[0:32, 128:256], idfs[0:32, 0:32]))
        WAIT("vector", sPE, vTs)
        vScol = OP("vector", lambda e: e.tensor_copy(out=scol32[:], in_=pQ1[:, 0:32]))

        mod_evac = 0
        peW = {0: 0, 1: 0}
        for jb in range(n_jb):
            j0 = jb * 24
            jn = min(24, jpc - j0)
            for i in range(ND):
                sl = i % 2
                xbuf = xsb[:, 8192 + 3072 * sl:8192 + 3072 * sl + 3072]
                WAIT("gpsimd", sPE, peW[sl])
                DMA("gpsimd", sG[sl], xbuf[:, 0:128 * jn],
                    wmodS[128 * i:128 * i + 128, 128 * j0:128 * (j0 + jn)])
                if jb == 0 and i == 0:
                    WAIT("tensor", sDVE, vScol)
                WAIT("tensor", sG[sl], sG[sl].n)
                for j in range(jn):
                    fn = (lambda e, xbuf=xbuf, j=j, i=i:
                          e.matmul(pS[:, (32 * j + i):(32 * j + i + 1)],
                                   xbuf[:, 128 * j:128 * j + 128],
                                   scol32[:, i:i + 1], start=True, stop=True))
                    if j == jn - 1:
                        vmm = OP("tensor", fn)
                    else:
                        MM(fn)
                peW[sl] = vmm
            for j in range(jn):
                if j == 0:
                    WAIT("vector", sPE, vmm)
                vred = OP("vector", lambda e, j=j: e.tensor_reduce(
                    out=modpart[:, j:j + 1], in_=pS[:, 32 * j:32 * j + 32], axis=AX.X, op=ALU.add))
            WAIT("tensor", sDVE, vred)
            vTp = OP("tensor", lambda e, jn=jn: e.transpose(pQ0[0:jn, 0:128], modpart[:, 0:jn], idfs[:]))
            WAIT("vector", sPE, vTp)
            WAIT("vector", sCc, mod_evac)
            vEv = OP("vector", lambda e, jn=jn: e.tensor_copy(out=scrA[0:jn, 256:384], in_=pQ0[0:jn, 0:128]))
            WAIT("sync", sDVE, vEv)
            DMA("sync", sCc, ccin[j0:j0 + jn, :], scrA[0:jn, 256:384])
            mod_evac = sCc.n

        WAIT("sync", sPE, vmm)
        DMA("sync", sX, xsb[:, 8192:14336].rearrange("p (b t) -> p b t", t=TT),
            xT[2048:3584, 0:TT].rearrange("(b p) t -> p b t", p=128))
        vXl0 = sX.n
        WAIT("gpsimd", sCc, mod_evac)
        em("gpsimd", lambda e: e.collective_compute(
            "AllGather", ALU.bypass, replica_groups=[list(range(n_cores))],
            ins=[ccin], outs=[ccout]).then_inc(sCC.h, 1))
        sCC.n += 1
        WAIT("sync", sCC, sCC.n)
        R = jpc * n_cores
        g0, tI = 0, 0
        vT2 = 0
        while g0 < R:
            gn = min(96, R - g0)
            WAIT("sync", sPE, vT2)
            DMA("sync", sCo, scrA[0:gn, 0:128], ccout[g0:g0 + gn, :])
            WAIT("tensor", sCo, sCo.n)
            vT2 = OP("tensor", lambda e, gn=gn, tI=tI: e.transpose(
                pQ1[0:128, 128 * tI:128 * tI + gn], scrA[0:gn, 0:128], idfs[0:gn, 0:gn]))
            WAIT("vector", sPE, vT2)
            vMc = OP("vector", lambda e, gn=gn, g0=g0, tI=tI: e.tensor_copy(
                out=modcol[:, g0:g0 + gn], in_=pQ1[:, 128 * tI:128 * tI + gn]))
            g0 += gn
            tI += 1
        OP("vector", lambda e: e.tensor_copy(out=SAv, in_=modcol[:, 0:32]))
        OP("vector", lambda e: e.tensor_scalar(out=CA1, in0=modcol[:, 32:64], scalar1=1.0, scalar2=None, op0=ALU.add))
        OP("vector", lambda e: e.tensor_copy(out=SBv, in_=modcol[:, 96:128]))
        OP("vector", lambda e: e.tensor_scalar(out=CB1, in0=modcol[:, 128:160], scalar1=1.0, scalar2=None, op0=ALU.add))
        for (ga_s, dst) in ((64, GAC), (160, GBC)):
            g = modcol[:, ga_s:ga_s + 32]
            OP("vector", lambda e, g=g: e.tensor_scalar(out=scrC[:, 0:32], in0=g, scalar1=0.0, scalar2=None, op0=ALU.is_ge))
            OP("vector", lambda e, g=g: e.tensor_scalar(out=scrC[:, 32:64], in0=g, scalar1=0.25, scalar2=None, op0=ALU.max))
            OP("vector", lambda e, g=g: e.tensor_scalar(out=scrC[:, 64:96], in0=g, scalar1=-0.25, scalar2=None, op0=ALU.min))
            OP("vector", lambda e: e.tensor_sub(scrC[:, 96:128], scrC[:, 32:64], scrC[:, 64:96]))
            OP("vector", lambda e: e.tensor_mul(scrC[:, 96:128], scrC[:, 96:128], scrC[:, 0:32]))
            OP("vector", lambda e, dst=dst: e.tensor_add(dst, scrC[:, 96:128], scrC[:, 64:96]))
        WAIT("vector", sC, vC)
        vMod = OP("vector", lambda e: e.tensor_mul(GBIAS, GBC, biasC[:]))
        mark_scratch()

        pp = [pp0, pp1]
        prep_state = {"prevWr": {0: 0, 1: 0}, "qidx": 0}
        prep_items = []
        scr_done = {}

        def mk_prep_item(wsrc, scr, do_sign, i, q, last):
            def emit():
                sl = prep_state["qidx"] % 2
                prep_state["qidx"] += 1
                WAIT("gpsimd", sScr[sl], prep_state["prevWr"][sl])
                DMA("gpsimd", sPp[sl], pp[sl][:], wsrc[128 * i:128 * i + 128, 1024 * q:1024 * q + 1024])
                WAIT("gpsimd", sPp[sl], sPp[sl].n)
                DMA("gpsimd", sScr[sl], scr[8 * q:8 * q + 8, :, i, :].rearrange("c p f -> p c f"),
                    pp[sl][:].rearrange("p (c f) -> p c f", f=128))
                prep_state["prevWr"][sl] = sScr[sl].n
                if last:
                    scr_done[id(scr)] = (sScr[0].n, sScr[1].n)
            return emit

        def mat_items(wsrc, scr, do_sign):
            return [mk_prep_item(wsrc, scr, do_sign, i, q, i == ND - 1 and q == 3)
                    for i in range(ND) for q in range(4)]

        _wq_items = mat_items(wqT, scrq, False)
        _wff_items = mat_items(wfT, scrf, False)
        _wo_items = mat_items(woT, scro, False)
        prep_items.extend(_wq_items)
        prep_items.extend(_wo_items)
        prep_items.extend(_wff_items)

        def weave(n):
            for _ in range(min(n, len(prep_items))):
                prep_items.pop(0)()

        def scr_gate(scr):
            v = scr_done.get(id(scr))
            if v is None:
                while prep_items and id(scr) not in scr_done:
                    prep_items.pop(0)()
                v = scr_done[id(scr)]
            return v

        def ln_stats(src_f32, mdst, rdst, extra_rows=None):
            wait_scratch("vector")
            wait_scratch("scalar")
            v = OP("vector", lambda e: e.tensor_copy(out=scrA[:], in_=src_f32[:, 0:512]))
            aq = OP("scalar", lambda e: e.activation(out=scrB[:], in_=src_f32[:, 0:512], func=AF.Square))
            vadd2 = 0
            for cix in range(1, ND):
                ch = src_f32[:, 512 * cix:512 * cix + 512]
                OP("vector", lambda e, ch=ch: e.tensor_add(scrA[:], scrA[:], ch))
                if vadd2:
                    WAIT("scalar", sDVE, vadd2)
                aq = OP("scalar", lambda e, ch=ch: e.activation(out=scrC[:], in_=ch, func=AF.Square))
                WAIT("vector", sACT, aq)
                vadd2 = OP("vector", lambda e: e.tensor_add(scrB[:], scrB[:], scrC[:]))
            WAIT("tensor", sDVE, vadd2)
            return ln_tail(mdst, rdst, extra_rows=extra_rows)

        def ln_tail(mdst, rdst, vdep=0, extra_rows=None):
            wait_scratch("tensor")
            WAIT("tensor", sDVE, vdep)
            OP("tensor", lambda e: e.matmul(pVo1[0:1, 0:512], onesc[:], scrA[:], start=True, stop=True))
            vmm = OP("tensor", lambda e: e.matmul(pVo0[0:1, 0:512], onesc[:], scrB[:], start=True, stop=True))
            WAIT("vector", sPE, vmm)
            OP("vector", lambda e: e.tensor_scalar(out=scrC[0:1, 0:512], in0=pVo1[0:1, 0:512],
                                                   scalar1=1.0 / D, scalar2=None, op0=ALU.mult))
            OP("vector", lambda e: e.tensor_scalar(out=scrB[0:1, 0:512], in0=pVo0[0:1, 0:512],
                                                   scalar1=1.0 / D, scalar2=None, op0=ALU.mult))
            OP("vector", lambda e: e.tensor_mul(scrA[0:1, 0:512], scrC[0:1, 0:512], scrC[0:1, 0:512]))
            v = OP("vector", lambda e: e.scalar_tensor_tensor(
                out=scrB[0:1, 0:512], in0=scrB[0:1, 0:512], scalar=EPS, in1=scrA[0:1, 0:512],
                op0=ALU.add, op1=ALU.subtract))
            WAIT("scalar", sDVE, v)
            aS = OP("scalar", lambda e: e.activation(out=scrA[0:1, 0:512], in_=scrB[0:1, 0:512], func=AF.Sqrt))
            WAIT("vector", sACT, aS)
            OP("vector", lambda e: e.reciprocal(out=scrB[0:1, 0:512], in_=scrA[0:1, 0:512]))
            OP("vector", lambda e, mdst=mdst: e.tensor_copy(out=mdst, in_=scrC[0:1, 0:512]))
            v = OP("vector", lambda e, rdst=rdst: e.tensor_copy(out=rdst, in_=scrB[0:1, 0:512]))
            if extra_rows is not None:
                lo, hi, own0 = extra_rows
                OP("vector", lambda e, lo=lo, hi=hi, own0=own0: e.tensor_copy(
                    out=mrow[0:1, own0:own0 + hi - lo], in_=scrC[0:1, lo:hi]))
                v = OP("vector", lambda e, lo=lo, hi=hi, own0=own0: e.tensor_copy(
                    out=rrow[0:1, own0:own0 + hi - lo], in_=scrB[0:1, lo:hi]))
            return v

        def ln_bcast(mr, rr, vrows=0):
            wait_scratch("tensor")
            WAIT("tensor", sDVE, vrows)
            OP("tensor", lambda e, mr=mr: e.matmul(pS[:, 0:512], onesb[:], mr, start=True, stop=True))
            return OP("tensor", lambda e, rr=rr: e.matmul(pS[:, 512:1024], onesb[:], rr, start=True, stop=True))

        def ln_apply(src_f32, dst_bf, c1col, shcol, vdep_pe):
            lastAs = []
            for cix in range(ND):
                ch = src_f32[:, 512 * cix:512 * cix + 512]
                dch = dst_bf[:, 512 * cix:512 * cix + 512]
                pa, pb = (scrA, scrB) if cix % 2 == 0 else (scrC, scrD)
                if cix == 0:
                    WAIT("vector", sPE, vdep_pe)
                    wait_scratch("vector")
                if cix >= 2:
                    WAIT("vector", sACT, lastAs[cix - 2])
                OP("vector", lambda e, ch=ch, pa=pa: e.tensor_sub(pa[:], ch, pS[:, 0:512]))
                vv = OP("vector", lambda e, pa=pa, pb=pb: e.tensor_mul(pb[:], pa[:], pS[:, 512:1024]))
                WAIT("scalar", sDVE, vv)
                lastAs.append(OP("scalar", lambda e, dch=dch, cix=cix, pb=pb: e.activation(
                    out=dch, in_=pb[:], func=AF.Identity,
                    bias=shcol[:, cix:cix + 1], scale=c1col[:, cix:cix + 1])))
            return lastAs

        vFree = [0, 0, 0, 0]
        lastVevac = 0
        lastHprev = 0
        for et in range(n_ext):
            t0 = et * TT
            if et > 0:
                WAIT("sync", sACT, lastHprev)
                DMA("sync", sX, xsb[:].rearrange("p (b t) -> p b t", t=TT),
                    xT[:, t0:t0 + TT].rearrange("(b p) t -> p b t", p=128))
                vXl = sX.n
                WAIT("sync", sDVE, sDVE.n)
                DMA("sync", sCS, cossb[:], cosE[:, t0:t0 + TT])
                DMA("sync", sCS, sinsb[:], sinE[:, t0:t0 + TT])
                vCSl = sCS.n
            else:
                vXl = vXl0
                vCSl = vCSl0
            WAIT("vector", sX, vXl)
            WAIT("scalar", sX, vXl)
            own_lo = max(0, 256 - t0)
            own_hi = min(512, 256 + core_t - t0)
            vRows = ln_stats(xsb, rowsb[0:1, 0:512], rowsb[0:1, 512:1024],
                             extra_rows=(own_lo, own_hi, t0 - 256 + own_lo))
            vB = ln_bcast(rowsb[0:1, 0:512], rowsb[0:1, 512:1024], vRows)
            lastHs = ln_apply(xsb, hsb, CA1, SAv, vB)
            lastH = lastHs[-1]
            lastHprev = lastH
            vGrp = {}
            if et == 0:
                WAIT("tensor", sGc, vGsmall)
            for which, wbase, pdst in (("k", 0, pQ0), ("v", 4096, pQ1)):
                waited = 0
                for i in range(ND):
                    if lastHs[i] > waited:
                        waited = lastHs[i]
                        WAIT("tensor", sACT, waited)
                    fn = (lambda e, wbase=wbase, i=i, pdst=pdst: e.matmul(
                        pdst[:], aosb[:, wbase + 128 * i:wbase + 128 * i + 128],
                        hsb[:, 512 * i:512 * i + 512], start=(i == 0), stop=(i == ND - 1)))
                    if i == ND - 1:
                        vg = OP("tensor", fn)
                    else:
                        MM(fn)
                vGrp[which] = vg
            WAIT("vector", sPE, vGrp["k"])
            WAIT("vector", sCS, vCSl)
            OP("vector", lambda e: e.tensor_mul(scrA[:], pQ0[:], cossb[:]))
            OP("vector", lambda e: e.tensor_mul(scrB[0:64, :], pQ0[64:128, :], sinsb[0:64, :]))
            OP("vector", lambda e: e.tensor_mul(scrB[64:128, :], pQ0[0:64, :], sinsb[64:128, :]))
            OP("vector", lambda e, t0=t0: e.tensor_sub(kres[0:64, t0:t0 + TT], scrA[0:64, :], scrB[0:64, :]))
            OP("vector", lambda e, t0=t0: e.tensor_add(kres[64:128, t0:t0 + TT], scrA[64:128, :], scrB[64:128, :]))
            WAIT("scalar", sPE, vGrp["v"])
            vVc = OP("scalar", lambda e: e.activation(out=sig[:, 0:512], in_=pQ1[:], func=AF.Copy))
            WAIT("tensor", sACT, vVc)
            WAIT("tensor", sDVE, lastVevac)
            for j in range(4):
                vTt = OP("tensor", lambda e, j=j: e.transpose(
                    pTb[:, 128 * j:128 * j + 128], sig[:, 128 * j:128 * j + 128], idbs[:]))
            WAIT("vector", sPE, vTt)
            for j in range(4):
                kb = 4 * et + j
                lastVevac = OP("vector", lambda e, j=j, kb=kb: e.tensor_scalar(
                    out=vtm[:, 128 * kb:128 * kb + 128], in0=pTb[:, 128 * j:128 * j + 128],
                    scalar1=vmk[:, kb:kb + 1], scalar2=None, op0=ALU.mult))
            mark_scratch()
            weave(52)

        vSigACT = {}
        vPsi = {}
        vTpPE = {}
        vTbEvac = {}
        vVout = {}
        vEvac = {}
        uG = 0
        uTile = {}

        def emit_trans(u):
            slot = u % 2
            pT = pTb if slot == 0 else pTb2
            WAIT("tensor", sDVE, vPsi[u])
            if u >= 2:
                WAITV("tensor", vTbEvac[u - 2])
            for cx in range(6):
                vTpPE[u] = OP("tensor", lambda e, pT=pT, cx=cx, slot=slot: e.transpose(
                    pT[:, 128 * cx:128 * cx + 128],
                    sig[:, 768 * slot + 128 * cx:768 * slot + 128 * cx + 128], idbs[:]))

        def emit_tbevac(u):
            slot = u % 2
            pT = pTb if slot == 0 else pTb2
            eng = "vector" if slot == 0 else "scalar"
            ct = sDVE if slot == 0 else sACT
            WAIT(eng, sPE, vTpPE[u])
            if u >= 2:
                WAIT(eng, sPE, vVout[u - 2])
            if slot == 0:
                v = OP(eng, lambda e, pT=pT, slot=slot: e.tensor_copy(
                    out=psiT[:, 768 * slot:768 * slot + 768], in_=pT[:, 0:768]))
            else:
                v = OP(eng, lambda e, pT=pT, slot=slot: e.activation(
                    out=psiT[:, 768 * slot:768 * slot + 768], in_=pT[:, 0:768], func=AF.Copy))
            vTbEvac[u] = (ct, v)

        def emit_V(u):
            h_ = (u % 128) // 4
            qc_ = u % 4
            tl = uTile[u]
            lqb = 2 * tl + qc_ // 2
            slot = u % 2
            WAITV("tensor", vTbEvac[u])
            if u >= 2:
                WAITV("tensor", vEvac[u - 2])
            pV = pVo0 if slot == 0 else pVo1
            for cx in range(6):
                kb = 2 * lqb + cx
                fn = (lambda e, pV=pV, cx=cx, kb=kb, slot=slot: e.matmul(
                    pV[:, 0:128], vtm[:, 128 * kb:128 * kb + 128],
                    psiT[:, 768 * slot + 128 * cx:768 * slot + 128 * cx + 128],
                    start=(cx == 0), stop=(cx == 5)))
                if cx == 5:
                    vVout[u] = OP("tensor", fn)
                else:
                    MM(fn)
            WAIT("scalar", sPE, vVout[u])
            vEvac[u] = (sACT, OP("scalar", lambda e, h_=h_, qc_=qc_, pV=pV: e.activation(
                out=aosb[:, 512 * h_ + 128 * qc_:512 * h_ + 128 * qc_ + 128],
                in_=pV[:, 0:128], func=AF.Copy)))

        WSL3 = [wslA, wslB, wslC, wslD]
        mmg_state = {"hidx": 0}

        def mm_group(wsrc, c, rhs_sb, pdst, extra_pre=None, rhs_ready=None, sign=False):
            if extra_pre:
                extra_pre()
            vh = []
            slots = []
            vsg = [0, 0]
            for half in range(2):
                sl = mmg_state["hidx"] % 4
                mmg_state["hidx"] += 1
                slots.append(sl)
                WAIT("sync", sPE, vFree[sl])
                DMA("sync", sSl3[sl], WSL3[sl][:].rearrange("p (b f) -> p b f", f=128),
                    wsrc[c, :, 16 * half:16 * half + 16, :])
                vh.append(sSl3[sl].n)
                if sign:
                    WAIT("vector", sSl3[sl], vh[half])
                    WAIT("vector", sPE, vFree[sl])
                    OP("vector", lambda e, sl=sl: e.tensor_scalar(
                        out=WSL3[sl][:], in0=WSL3[sl][:], scalar1=1e38, scalar2=None, op0=ALU.mult))
                    vsg[half] = OP("vector", lambda e, sl=sl: e.tensor_scalar(
                        out=WSL3[sl][:], in0=WSL3[sl][:], scalar1=-1.0, scalar2=1.0, op0=ALU.max, op1=ALU.min))
            out = None
            waited = 0
            for half in range(2):
                sl = slots[half]
                wsl = WSL3[sl]
                WAIT("tensor", sSl3[sl], vh[half])
                if sign:
                    WAIT("tensor", sDVE, vsg[half])
                for b in range(16):
                    i = 16 * half + b
                    if rhs_ready is not None and rhs_ready[i] > waited:
                        waited = rhs_ready[i]
                        WAIT("tensor", sACT, waited)
                    fn = (lambda e, wsl=wsl, b=b, i=i, pdst=pdst: e.matmul(
                        pdst[:], wsl[:, 128 * b:128 * b + 128],
                        rhs_sb[:, 512 * i:512 * i + 512], start=(i == 0), stop=(i == ND - 1)))
                    if i == 15 or i == ND - 1:
                        out = OP("tensor", fn)
                        vFree[sl] = out
                    else:
                        MM(fn)
            return out

        vXl_pref = None
        vCSl_pref = None
        vStSlot = {0: 0, 1: 0}
        for tile in range(n_tiles):
            t0 = tile * TT
            e0 = t0 + 256
            if vXl_pref is None:
                WAIT("sync", sACT, lastHprev)
                DMA("sync", sX, xsb[:].rearrange("p (b t) -> p b t", t=TT),
                    xT[:, e0:e0 + TT].rearrange("(b p) t -> p b t", p=128))
                vXl = (sX, sX.n)
                WAIT("sync", sDVE, sDVE.n)
                DMA("sync", sCS, cossb[:], cosE[:, e0:e0 + TT])
                DMA("sync", sCS, sinsb[:], sinE[:, e0:e0 + TT])
                vCSl = (sCS, sCS.n)
            else:
                vXl, vCSl = vXl_pref, vCSl_pref
            WAITV("vector", vXl)
            WAITV("scalar", vXl)
            vB = ln_bcast(mrow[0:1, t0:t0 + 512], rrow[0:1, t0:t0 + 512])
            lastHs = ln_apply(xsb, hsb, CA1, SAv, vB)
            lastH = lastHs[-1]

            vRope = {}
            vLastScores = sPE.n
            vQmm = {}
            qsub_state = {}

            def q_load(h):
                if tile == 0:
                    if h == 0:
                        vWq = scr_gate(scrq)
                        WAIT("sync", sScr[0], vWq[0])
                        WAIT("sync", sScr[1], vWq[1])
                    weave(4)
                vh, slots = [], []
                for half in range(2):
                    sl = mmg_state["hidx"] % 4
                    mmg_state["hidx"] += 1
                    slots.append(sl)
                    WAIT("sync", sPE, vFree[sl])
                    DMA("sync", sSl3[sl], WSL3[sl][:].rearrange("p (b f) -> p b f", f=128),
                        scrq[h, :, 16 * half:16 * half + 16, :])
                    vh.append(sSl3[sl].n)
                qsub_state[h] = (slots, vh)

            def q_sub(h, j, rhs_ready=None):
                slots, vh = qsub_state[h]
                half = j // 2
                sl = slots[half]
                wsl = WSL3[sl]
                if j == 0:
                    if h >= 2:
                        WAITV("tensor", vRope[h - 2])
                    WAIT("tensor", sACT, lastH)
                if j % 2 == 0:
                    WAIT("tensor", sSl3[sl], vh[half])
                out = None
                waited = 0
                for b in range(8):
                    i = 8 * j + b
                    if rhs_ready is not None and rhs_ready[i] > waited:
                        waited = rhs_ready[i]
                        WAIT("tensor", sACT, waited)
                    fn = (lambda e, wsl=wsl, i=i, h=h: e.matmul(
                        (pQ0 if h % 2 == 0 else pQ1)[:], wsl[:, 128 * (i % 16):128 * (i % 16) + 128],
                        hsb[:, 512 * i:512 * i + 512], start=(i == 0), stop=(i == ND - 1)))
                    if i in (15, ND - 1):
                        out = OP("tensor", fn)
                        vFree[sl] = out
                    else:
                        MM(fn)
                if j == 3:
                    vQmm[h] = out
                return out

            def emit_rope(h):
                qsl2 = h % 2
                WAIT("vector", sPE, vQmm[h])
                WAITV("vector", vCSl)
                if h >= 2:
                    WAIT("vector", sPE, vLastScores)
                pQh = pQ0 if qsl2 == 0 else pQ1
                qs = qslot[:, 512 * qsl2:512 * qsl2 + 512]
                OP("vector", lambda e, pQh=pQh: e.tensor_mul(scrA[:], pQh[:], cossb[:]))
                OP("vector", lambda e, pQh=pQh: e.tensor_mul(scrB[0:64, :], pQh[64:128, :], sinsb[0:64, :]))
                OP("vector", lambda e, pQh=pQh: e.tensor_mul(scrB[64:128, :], pQh[0:64, :], sinsb[64:128, :]))
                OP("vector", lambda e, qs=qs: e.tensor_sub(qs[0:64, :], scrA[0:64, :], scrB[0:64, :]))
                vRope[h] = (sDVE, OP("vector", lambda e, qs=qs: e.tensor_add(qs[64:128, :], scrA[64:128, :], scrB[64:128, :])))

            SUBLAG = 5
            q_load(0)
            for j4 in range(4):
                q_sub(0, j4, rhs_ready=lastHs)
            emit_rope(0)
            q_load(1)
            q_sub(1, 0)
            for h in range(H):
                qsl = h % 2
                for qc in range(4):
                    u = uG
                    uG += 1
                    uTile[u] = tile
                    lqc = 4 * tile + qc
                    lqb = 2 * tile + qc // 2
                    slot = u % 2
                    sm = sums[:, 8 * slot:8 * slot + 8]
                    cf = coefs[:, 4 * slot:4 * slot + 4]
                    qs = qslot[:, 512 * qsl:512 * qsl + 512]
                    s = 4 * h + qc + SUBLAG
                    if s < 4 * H:
                        h2, j2 = divmod(s, 4)
                        if j2 == 0 and h2 >= 2:
                            q_load(h2)
                        q_sub(h2, j2)
                        if j2 == 3:
                            emit_rope(h2)
                    WAITV("tensor", vRope[h])
                    if u >= 1:
                        WAIT("tensor", sACT, vSigACT[u - 1])
                    k0 = 256 * lqb
                    for (ks, kn) in ((0, 512), (512, 256)):
                        vSc = OP("tensor", lambda e, qs=qs, qc=qc, k0=k0, ks=ks, kn=kn: e.matmul(
                            pS[:, ks:ks + kn], qs[:, 128 * qc:128 * qc + 128],
                            kres[:, k0 + ks:k0 + ks + kn], start=True, stop=True))
                    vLastScores = vSc
                    WAIT("scalar", sPE, vSc)
                    if u >= 2:
                        WAIT("scalar", sPE, vTpPE[u - 2])
                    for j3 in range(3):
                        vSigACT[u] = OP("scalar", lambda e, slot=slot, j3=j3, sm=sm: e.activation(
                            out=sig[:, 768 * slot + 256 * j3:768 * slot + 256 * j3 + 256],
                            in_=pS[:, 256 * j3:256 * j3 + 256], func=AF.Sigmoid, scale=SCALE,
                            accum_out=sm[:, j3:j3 + 1]))
                    WAIT("vector", sACT, vSigACT[u])
                    OP("vector", lambda e, sm=sm: e.tensor_add(sm[:, 4:5], sm[:, 0:1], sm[:, 1:2]))
                    OP("vector", lambda e, sm=sm, lqb=lqb: e.scalar_tensor_tensor(
                        out=sm[:, 5:6], in0=sm[:, 2:3], scalar=m3s[:, lqb:lqb + 1],
                        in1=sm[:, 1:2], op0=ALU.mult, op1=ALU.add))
                    OP("vector", lambda e, sm=sm: e.reciprocal(out=sm[:, 6:8], in_=sm[:, 4:6]))
                    OP("vector", lambda e, sm=sm, cf=cf, lqc=lqc: e.tensor_mul(
                        cf[:, 0:2], sm[:, 6:8], aAB[:, 2 * lqc:2 * lqc + 2]))
                    OP("vector", lambda e, cf=cf: e.tensor_add(cf[:, 2:3], cf[:, 0:1], cf[:, 1:2]))
                    for (j3, cc) in ((0, 0), (1, 2), (2, 1)):
                        vPsi[u] = OP("vector", lambda e, slot=slot, j3=j3, cc=cc, cf=cf: e.tensor_scalar(
                            out=sig[:, 768 * slot + 256 * j3:768 * slot + 256 * j3 + 256],
                            in0=sig[:, 768 * slot + 256 * j3:768 * slot + 256 * j3 + 256],
                            scalar1=cf[:, cc:cc + 1], scalar2=None, op0=ALU.mult))
                    if u >= 2 and (u - 2) not in vVout:
                        emit_V(u - 2)
                    if u >= 1 and (u - 1) not in vTpPE:
                        emit_trans(u - 1)
                        emit_tbevac(u - 1)
            uL = uG - 1
            if uL not in vTpPE:
                emit_trans(uL)
                emit_tbevac(uL)
            if (uL - 1) not in vVout:
                emit_V(uL - 1)
            emit_V(uL)
            vAttnDone = vEvac[uL]
            mark_scratch()

            vRopeLast = vRope[H - 1][1] if vRope[H - 1][0] is sDVE else None
            vX2 = {}
            vp2 = 0
            for c in range(ND):
                def pre_wo(c=c):
                    if c >= 2:
                        WAIT("tensor", sDVE, vX2[c - 2])
                    if c == 0:
                        WAITV("tensor", vAttnDone)
                if tile == 0:
                    if c == 0:
                        vWo = scr_gate(scro)
                        WAIT("sync", sScr[0], vWo[0])
                        WAIT("sync", sScr[1], vWo[1])
                    weave(3)
                vW = mm_group(scro, c, aosb, pQ0 if c % 2 == 0 else pQ1, extra_pre=pre_wo)
                WAIT("vector", sPE, vW)
                pQc = pQ0 if c % 2 == 0 else pQ1
                vX2[c] = OP("vector", lambda e, c=c, pQc=pQc: e.scalar_tensor_tensor(
                    out=xsb[:, 512 * c:512 * c + 512], in0=pQc[:], scalar=GAC[:, c:c + 1],
                    in1=xsb[:, 512 * c:512 * c + 512], op0=ALU.mult, op1=ALU.add))
                ch = xsb[:, 512 * c:512 * c + 512]
                if c == 0:
                    wait_scratch("vector")
                    OP("vector", lambda e, ch=ch: e.tensor_copy(out=scrA[:], in_=ch))
                    WAIT("scalar", sDVE, vX2[0])
                    aq = OP("scalar", lambda e, ch=ch: e.activation(out=scrB[:], in_=ch, func=AF.Square))
                else:
                    OP("vector", lambda e, ch=ch: e.tensor_add(scrA[:], scrA[:], ch))
                    tmp = scrC if c % 2 == 0 else scrD
                    WAIT("scalar", sDVE, vp2 if vp2 else vX2[c])
                    WAIT("scalar", sDVE, vX2[c])
                    aq = OP("scalar", lambda e, ch=ch, tmp=tmp: e.activation(out=tmp[:], in_=ch, func=AF.Square))
                    WAIT("vector", sACT, aq)
                    vp2 = OP("vector", lambda e, tmp=tmp: e.tensor_add(scrB[:], scrB[:], tmp[:]))

            vTail = ln_tail(rowsb[0:1, 0:512], rowsb[0:1, 512:1024], vp2)
            vB2 = ln_bcast(rowsb[0:1, 0:512], rowsb[0:1, 512:1024], vTail)
            lastH2s = ln_apply(xsb, hsb, CB1, SBv, vB2)
            lastH2 = lastH2s[-1]

            WAIT("tensor", sACT, vOsc0 if tile > 0 else 0)
            vRt = 0
            vLgs = {}
            for tc in range(4):
                if tc == 0:
                    if tile == 0:
                        WAIT("tensor", sGc, vGsmall)
                    waited = 0
                else:
                    WAIT("tensor", sACT, vExp)
                for i in range(ND):
                    if tc == 0 and lastH2s[i] > waited:
                        waited = lastH2s[i]
                        WAIT("tensor", sACT, waited)
                    fn = (lambda e, tc=tc, i=i: e.matmul(
                        pVo0[:, 0:8], hsb[:, 512 * i + 128 * tc:512 * i + 128 * tc + 128],
                        gwb[:, 8 * i:8 * i + 8], start=(i == 0), stop=(i == ND - 1)))
                    if i == ND - 1:
                        vLg = OP("tensor", fn)
                    else:
                        MM(fn)
                WAIT("vector", sPE, vLg)
                OP("vector", lambda e: e.tensor_reduce(out=nmx[:, 0:1], in_=pVo0[:, 0:8], axis=AX.X, op=ALU.max))
                OP("vector", lambda e: e.tensor_scalar(out=nmx[:, 1:2], in0=nmx[:, 0:1],
                                                       scalar1=-1.0, scalar2=None, op0=ALU.mult))
                WAIT("scalar", sDVE, sDVE.n)
                vExp = OP("scalar", lambda e: e.activation(
                    out=routf[:], in_=pVo0[:, 0:8], func=AF.Exp, bias=nmx[:, 1:2],
                    accum_out=nmx[:, 2:3]))
                if tc == 3:
                    vOsc0 = vExp
                WAIT("vector", sACT, vExp)
                OP("vector", lambda e: e.reciprocal(out=nmx[:, 3:4], in_=nmx[:, 2:3]))
                vRb = OP("vector", lambda e: e.tensor_scalar(
                    out=routb[:], in0=routf[:], scalar1=nmx[:, 3:4], scalar2=None, op0=ALU.mult))
                WAIT("tensor", sDVE, vRb)
                if vRt:
                    WAIT("tensor", sDVE, vRt)
                vTr = OP("tensor", lambda e: e.transpose(pTb[0:8, 512:640], routb[:], idbs[:]))
                WAIT("vector", sPE, vTr)
                vRt = OP("vector", lambda e, tc=tc: e.tensor_copy(
                    out=routT[:, 128 * tc:128 * tc + 128], in_=pTb[0:8, 512:640]))

            vXs = {}
            for c in range(ND):
                psl = c % 2
                if c == 0:
                    WAIT("tensor", sDVE, vRt)
                if c >= 2:
                    WAIT("tensor", sDVE, vXs[c - 2])
                vIs = OP("tensor", lambda e, c=c, psl=psl: e.matmul(
                    pS[:, 512 * psl:512 * psl + 512], icsb[:, 128 * c:128 * c + 128],
                    routT[:], start=True, stop=True))
                WAIT("vector", sPE, vIs)
                vXs[c] = OP("vector", lambda e, c=c, psl=psl: e.tensor_mul(
                    aosb[:, 512 * c:512 * c + 512], hsb[:, 512 * c:512 * c + 512],
                    pS[:, 512 * psl:512 * psl + 512]))

            vEp = {}
            vSt = {}
            for c in range(ND):
                if tile == 0 and c == 0:
                    vWf = scr_gate(scrf)
                    WAIT("sync", sScr[0], vWf[0])
                    WAIT("sync", sScr[1], vWf[1])
                def pre_ff(c=c):
                    if c >= 2:
                        WAIT("tensor", sDVE, vEp[c - 2])
                    if c == 0:
                        WAIT("tensor", sDVE, vXs[ND - 1])
                psl = c % 2
                vF = mm_group(scrf, c, aosb, pQ0 if psl == 0 else pQ1, extra_pre=pre_ff, sign=True)
                if c >= 2:
                    WAIT("tensor", sACT, vOsc)
                vOs = OP("tensor", lambda e, c=c, psl=psl: e.matmul(
                    pS[:, 512 * psl:512 * psl + 512], ocsb[:, 128 * c:128 * c + 128],
                    routT[:], start=True, stop=True))
                WAIT("scalar", sPE, vOs)
                if c >= 2:
                    WAIT("scalar", sDVE, vEp[c - 2])
                oscr = scrC if psl == 0 else scrB
                vOsc = OP("scalar", lambda e, psl=psl, oscr=oscr: e.activation(
                    out=oscr[:], in_=pS[:, 512 * psl:512 * psl + 512], func=AF.Copy))
                WAIT("vector", sPE, vF)
                WAIT("vector", sACT, vOsc)
                WAIT("vector", sOg[0], vStSlot[0])
                OP("vector", lambda e, c=c, psl=psl, oscr=oscr: e.scalar_tensor_tensor(
                    out=scrA[:], in0=(pQ0 if psl == 0 else pQ1)[:], scalar=GBC[:, c:c + 1],
                    in1=oscr[:], op0=ALU.mult, op1=ALU.mult))
                vEp[c] = OP("vector", lambda e, c=c: e.scalar_tensor_tensor(
                    out=ostg[:, 0:512], in0=scrA[:], scalar=GBIAS[:, c:c + 1],
                    in1=xsb[:, 512 * c:512 * c + 512], op0=ALU.add, op1=ALU.add))
                WAIT("gpsimd", sDVE, vEp[c])
                DMA("gpsimd", sOg[0], outT[128 * c:128 * c + 128, t0:t0 + TT], ostg[:, 0:512])
                vSt[c] = sOg[0].n
                vStSlot[0] = sOg[0].n
                if tile < n_tiles - 1:
                    e0n = e0 + TT
                    if c == 0:
                        if vRopeLast is not None:
                            WAIT("gpsimd", sDVE, vRopeLast)
                        DMA("gpsimd", sCSg, cossb[:], cosE[:, e0n:e0n + TT])
                        DMA("gpsimd", sCSg, sinsb[:], sinE[:, e0n:e0n + TT])
                        vCSl_pref = (sCSg, sCSg.n)
                    DMA("gpsimd", sXg, xsb[:, 512 * c:512 * c + 512],
                        xT[128 * c:128 * c + 128, e0n:e0n + TT])
                    vXl_pref = (sXg, sXg.n)
            mark_scratch()

        WAIT("gpsimd", sOg[0], sOg[0].n)

        @block.sync
        def _(e):
            for fn in sched["sync"]:
                fn(e)

        @block.tensor
        def _(e):
            for fn in sched["tensor"]:
                fn(e)

        @block.vector
        def _(e):
            for fn in sched["vector"]:
                fn(e)

        @block.scalar
        def _(e):
            for fn in sched["scalar"]:
                fn(e)

        @block.gpsimd
        def _(e):
            for fn in sched["gpsimd"]:
                fn(e)

    return nc



def _rope_cache(S):
    inv_freq = (1.0 / 10000.0 ** (np.arange(0, DH, 2, dtype=np.float32) / DH)).astype(np.float32)
    t = np.arange(S, dtype=np.float32)
    f = np.outer(t, inv_freq)
    cos = np.zeros((S, DH), dtype=np.float32)
    sin = np.zeros((S, DH), dtype=np.float32)
    cos[:, 0::2] = np.cos(f)
    cos[:, 1::2] = np.cos(f)
    sin[:, 0::2] = np.sin(f)
    sin[:, 1::2] = np.sin(f)
    return cos, sin


def make_in_maps(inputs, n_cores, core_t):
    ext = core_t + 512
    nqc = core_t // 128
    nqb = core_t // 256
    nek = ext // 128
    jpc = 24576 // (128 * n_cores)
    S = n_cores * core_t

    f32 = np.float32
    x = np.asarray(inputs["L0_attn_x"], f32)[0]
    assert x.shape == (S, D), x.shape
    xTfull = np.zeros((D, S + 512), f32)
    xTfull[:, 256:256 + S] = np.ascontiguousarray(x.T)
    wq = np.ascontiguousarray(np.asarray(inputs["L0_attn_Wq"], f32).T)
    wk = np.ascontiguousarray(np.asarray(inputs["L0_attn_Wk"], f32).T)
    wv = np.ascontiguousarray(np.asarray(inputs["L0_attn_Wv"], f32).T)
    wo = np.ascontiguousarray(np.asarray(inputs["L0_attn_Wo"], f32).T)
    wf = np.ascontiguousarray(np.asarray(inputs["L0_ffn_weight"], f32).T)
    wmod = np.asarray(inputs["L0_wmod"], f32)
    cvec = np.ascontiguousarray(np.asarray(inputs["L0_c"], f32).reshape(1, D))
    gw = np.ascontiguousarray(np.asarray(inputs["L0_ffn_gate_w"], f32).T)
    ics = np.ascontiguousarray(np.asarray(inputs["L0_ffn_in_channel_scale"], f32))
    ocs = np.ascontiguousarray(np.asarray(inputs["L0_ffn_out_channel_scale"], f32))
    biasM = np.ascontiguousarray(np.asarray(inputs["L0_ffn_bias"], f32).reshape(32, 128).T)

    cos, sin = _rope_cache(S)
    idfA = np.eye(128, dtype=f32)
    idbA = np.eye(128, dtype=np.float32).astype(ml_dtypes.bfloat16)
    onesbA = np.ones((1, 128), dtype=ml_dtypes.bfloat16)
    onescA = np.ones((128, 1), dtype=f32)
    alpha = np.linspace(0.0, 1.0, 256, dtype=f32)

    maps = []
    for r in range(n_cores):
        r0 = r * core_t
        pos = np.arange(r0 - 256, r0 + core_t + 256)
        posc = np.clip(pos, 0, S - 1)
        cosEa = np.ascontiguousarray(cos[posc].T)
        sinEa = np.ascontiguousarray(sin[posc].T)
        aABa = np.zeros((128, 2 * nqc), f32)
        for qc in range(nqc):
            gqb = (r0 // 256) + qc // 2
            half = qc % 2
            asl = alpha[128 * half:128 * half + 128]
            if gqb == 0:
                aABa[:, 2 * qc] = 0.0
                aABa[:, 2 * qc + 1] = 1.0
            else:
                aABa[:, 2 * qc] = 1.0 - asl
                aABa[:, 2 * qc + 1] = asl
        m3a = np.ones((128, nqb), f32)
        for qb in range(nqb):
            if (r0 // 256) + qb == S // 256 - 1:
                m3a[:, qb] = 0.0
        vma = np.ones((128, nek), f32)
        for kb in range(nek):
            glob = r0 - 256 + 128 * kb
            if glob < 0 or glob >= S:
                vma[:, kb] = 0.0
        maps.append({
            "xT": np.ascontiguousarray(xTfull[:, r0:r0 + ext]),
            "wqT": wq, "woT": wo, "wfT": wf, "wkT": wk, "wvT": wv,
            "wmodS": np.ascontiguousarray(wmod[:, 128 * jpc * r:128 * jpc * (r + 1)]),
            "cvec": cvec, "gwT": gw, "icsM": ics, "ocsM": ocs, "biasM": biasM,
            "cosE": cosEa.astype(ml_dtypes.bfloat16), "sinE": sinEa.astype(ml_dtypes.bfloat16), "aABm": aABa, "m3m": m3a,
            "vmkm": vma, "idf": idfA, "onesbI": onesbA, "onescI": onescA, "idbI": idbA,
        })
    return maps


def kernel(**inputs):
    n_cores, core_t = 8, 2048
    maps = make_in_maps(inputs, n_cores, core_t)
    nc = build_nc(n_cores, core_t)
    res = run_bass_kernel_spmd(nc, maps, core_ids=list(range(n_cores)))
    full = np.concatenate([np.asarray(r["outT"]) for r in res.results], axis=1)
    return np.ascontiguousarray(full.T).reshape(1, n_cores * core_t, D).astype(np.float32)


# version 20
# speedup vs baseline: 1.1020x; 1.1020x over previous
import numpy as np
from contextlib import ExitStack
import ml_dtypes
import concourse.bass as bass
import concourse.mybir as mybir
from concourse.bass_utils import run_bass_kernel_spmd

F32 = mybir.dt.float32
BF16 = mybir.dt.bfloat16
AF = mybir.ActivationFunctionType
ALU = mybir.AluOpType
AX = mybir.AxisListType

D = 4096
ND = 32
DH = 128
H = 32
SCALE = float(1.0 / np.sqrt(128.0))
EPS = 1e-5
TT = 512


class Ct:
    __slots__ = ("h", "n")
    def __init__(self, h):
        self.h = h
        self.n = 0


def build_nc(n_cores, core_t):
    ext = core_t + 512
    n_tiles = core_t // TT
    n_ext = ext // TT
    nqc = core_t // 128
    nqb = core_t // 256
    nek = ext // 128
    jpc = 24576 // (128 * n_cores)
    n_jb = (jpc + 23) // 24

    nc = bass.Bass("TRN2", target_bir_lowering=False, debug=False)

    ap = lambda t: t.ap()
    xT = ap(nc.dram_tensor("xT", [D, ext], F32, kind="ExternalInput"))
    wqT = ap(nc.dram_tensor("wqT", [D, D], F32, kind="ExternalInput"))
    woT = ap(nc.dram_tensor("woT", [D, D], F32, kind="ExternalInput"))
    wfT = ap(nc.dram_tensor("wfT", [D, D], F32, kind="ExternalInput"))
    wkT = ap(nc.dram_tensor("wkT", [D, DH], F32, kind="ExternalInput"))
    wvT = ap(nc.dram_tensor("wvT", [D, DH], F32, kind="ExternalInput"))
    wmodS = ap(nc.dram_tensor("wmodS", [D, 128 * jpc], F32, kind="ExternalInput"))
    cvec = ap(nc.dram_tensor("cvec", [1, D], F32, kind="ExternalInput"))
    gwT = ap(nc.dram_tensor("gwT", [D, 8], F32, kind="ExternalInput"))
    icsM = ap(nc.dram_tensor("icsM", [8, D], F32, kind="ExternalInput"))
    ocsM = ap(nc.dram_tensor("ocsM", [8, D], F32, kind="ExternalInput"))
    biasM = ap(nc.dram_tensor("biasM", [128, 32], F32, kind="ExternalInput"))
    cosE = ap(nc.dram_tensor("cosE", [128, ext], BF16, kind="ExternalInput"))
    sinE = ap(nc.dram_tensor("sinE", [128, ext], BF16, kind="ExternalInput"))
    aABm = ap(nc.dram_tensor("aABm", [128, 2 * nqc], F32, kind="ExternalInput"))
    m3m = ap(nc.dram_tensor("m3m", [128, nqb], F32, kind="ExternalInput"))
    vmkm = ap(nc.dram_tensor("vmkm", [128, nek], F32, kind="ExternalInput"))
    idf = ap(nc.dram_tensor("idf", [128, 128], F32, kind="ExternalInput"))
    onesbI = ap(nc.dram_tensor("onesbI", [1, 128], BF16, kind="ExternalInput"))
    onescI = ap(nc.dram_tensor("onescI", [128, 1], F32, kind="ExternalInput"))
    idbI = ap(nc.dram_tensor("idbI", [128, 128], BF16, kind="ExternalInput"))

    scrq = ap(nc.dram_tensor("scrq", [32, 128, 32, 128], BF16, kind="Internal"))
    scro = ap(nc.dram_tensor("scro", [32, 128, 32, 128], BF16, kind="Internal"))
    scrf = ap(nc.dram_tensor("scrf", [32, 128, 32, 128], BF16, kind="Internal"))
    ccin = ap(nc.dram_tensor("ccin", [jpc, 128], F32, kind="Internal"))
    ccout = ap(nc.dram_tensor("ccout", [jpc * n_cores, 128], F32, kind="Internal"))

    outT = ap(nc.dram_tensor("outT", [D, core_t], F32, kind="ExternalOutput"))

    sched = {k: [] for k in ("sync", "tensor", "vector", "scalar", "gpsimd")}

    def em(eng, fn):
        sched[eng].append(fn)

    def WAIT(eng, ct, v):
        if v > 0:
            em(eng, lambda e, h=ct.h, v=v: e.wait_ge(h, v))

    def WAITV(eng, tv):
        WAIT(eng, tv[0], tv[1])

    def DMA(eng, ct, out, in_, **kw):
        ct.n += 16
        em(eng, lambda e, o=out, i=in_, h=ct.h, kw=dict(kw): e.dma_start(out=o, in_=i, **kw).then_inc(h, 16))
        return ct.n

    def TDMA(ct, out, in_):
        ct.n += 16
        em("scalar", lambda e, o=out, i=in_, h=ct.h: e.dma_start(out=o, in_=i, transpose=True).then_inc(h, 16))
        return ct.n

    ENG_CT = {}

    def OP(eng, fn):
        ct = ENG_CT[eng]
        if eng != "tensor" and ct.n > 0:
            em(eng, lambda e, h=ct.h, v=ct.n: e.wait_ge(h, v))
        ct.n += 1
        em(eng, lambda e, fn=fn, h=ct.h: fn(e).then_inc(h, 1))
        return ct.n

    def MM(fn):
        em("tensor", lambda e, fn=fn: fn(e))

    with ExitStack() as _st:
        ent = _st.enter_context
        xsb = ent(nc.sbuf_tensor("xsb", [128, ND * TT], F32))
        hsb = ent(nc.sbuf_tensor("hsb", [128, ND * TT], BF16))
        aosb = ent(nc.sbuf_tensor("aosb", [128, ND * TT], BF16))
        kres = ent(nc.sbuf_tensor("kres", [128, ext], BF16))
        vtm = ent(nc.sbuf_tensor("vtm", [128, ext], BF16))
        cossb = ent(nc.sbuf_tensor("cossb", [128, TT], BF16))
        sinsb = ent(nc.sbuf_tensor("sinsb", [128, TT], BF16))
        qslot = ent(nc.sbuf_tensor("qslot", [128, 1024], BF16))
        sig = ent(nc.sbuf_tensor("sig", [128, 1536], BF16))
        psiT = ent(nc.sbuf_tensor("psiT", [128, 1536], BF16, is_transpose_dest=True))
        scrA = ent(nc.sbuf_tensor("scrA", [128, 512], F32))
        scrB = ent(nc.sbuf_tensor("scrB", [128, 512], F32))
        scrC = ent(nc.sbuf_tensor("scrC", [128, 512], F32))
        scrD = ent(nc.sbuf_tensor("scrD", [128, 512], F32))
        mrow = ent(nc.sbuf_tensor("mrow", [1, core_t], BF16))
        rrow = ent(nc.sbuf_tensor("rrow", [1, core_t], BF16))
        rowsb = ent(nc.sbuf_tensor("rowsb", [1, 1024], BF16))
        ostg = ent(nc.sbuf_tensor("ostg", [128, 512], F32))
        wslA = ent(nc.sbuf_tensor("wslA", [128, 2048], BF16))
        wslB = ent(nc.sbuf_tensor("wslB", [128, 2048], BF16))
        pp0 = ent(nc.sbuf_tensor("pp0", [128, 1024], BF16))
        pp1 = ent(nc.sbuf_tensor("pp1", [128, 1024], BF16))
        wslC = ent(nc.sbuf_tensor("wslC", [128, 2048], BF16))
        wslD = ent(nc.sbuf_tensor("wslD", [128, 2048], BF16))
        icsb = ent(nc.sbuf_tensor("icsb", [8, D], BF16))
        ocsb = ent(nc.sbuf_tensor("ocsb", [8, D], BF16))
        gwb = ent(nc.sbuf_tensor("gwb", [128, 256], BF16))
        idfs = ent(nc.sbuf_tensor("idfs", [128, 128], F32))
        idbs = ent(nc.sbuf_tensor("idbs", [128, 128], BF16))
        onesb = ent(nc.sbuf_tensor("onesb", [1, 128], BF16))
        onesc = ent(nc.sbuf_tensor("onesc", [128, 1], F32))
        aAB = ent(nc.sbuf_tensor("aAB", [128, 2 * nqc], F32))
        m3s = ent(nc.sbuf_tensor("m3s", [128, nqb], F32))
        vmk = ent(nc.sbuf_tensor("vmk", [128, nek], F32))
        biasC = ent(nc.sbuf_tensor("biasC", [128, 32], F32))
        scol32 = ent(nc.sbuf_tensor("scol32", [128, 32], F32))
        camod = ent(nc.sbuf_tensor("camod", [128, 224], F32))
        sums = ent(nc.sbuf_tensor("sums", [128, 16], F32))
        coefs = ent(nc.sbuf_tensor("coefs", [128, 8], F32))
        nmx = ent(nc.sbuf_tensor("nmx", [128, 4], F32))
        pQ0 = ent(nc.psum_tensor("pQ0", [128, 512], F32))
        pQ1 = ent(nc.psum_tensor("pQ1", [128, 512], F32))
        pS = ent(nc.psum_tensor("pS", [128, 1024], F32))
        pVo0 = ent(nc.psum_tensor("pVo0", [128, 512], F32))
        pVo1 = ent(nc.psum_tensor("pVo1", [128, 512], F32))
        pTb = ent(nc.psum_tensor("pTb", [128, 1024], BF16))
        pTb2 = ent(nc.psum_tensor("pTb2", [128, 1024], BF16))
        _sC = ent(nc.semaphore("smC"))
        _sX = ent(nc.semaphore("smX"))
        _sCS = ent(nc.semaphore("smCS"))
        _sSlA = ent(nc.semaphore("smSlA"))
        _sSlB = ent(nc.semaphore("smSlB"))
        _sGA = ent(nc.semaphore("smGA"))
        _sGB = ent(nc.semaphore("smGB"))
        _sScr = ent(nc.semaphore("smScr"))
        _sScrB = ent(nc.semaphore("smScrB"))
        _sCc = ent(nc.semaphore("smCc"))
        _sCo = ent(nc.semaphore("smCo"))
        _sOutB = ent(nc.semaphore("smOutB"))
        _sGc = ent(nc.semaphore("smGc"))
        _sPpA = ent(nc.semaphore("smPpA"))
        _sPpB = ent(nc.semaphore("smPpB"))
        _sGP = ent(nc.semaphore("smGP"))
        _sOgA = ent(nc.semaphore("smOgA"))
        _sOgB = ent(nc.semaphore("smOgB"))
        _sXg = ent(nc.semaphore("smXg"))
        _sCSg = ent(nc.semaphore("smCSg"))
        _sSlC = ent(nc.semaphore("smSlC"))
        _sSlD = ent(nc.semaphore("smSlD"))
        _sGq0 = ent(nc.semaphore("smGq0"))
        _sGq1 = ent(nc.semaphore("smGq1"))
        _sGq2 = ent(nc.semaphore("smGq2"))
        _sGq3 = ent(nc.semaphore("smGq3"))
        _sTpA = ent(nc.semaphore("smTpA"))
        _sTpB = ent(nc.semaphore("smTpB"))
        _sOut = ent(nc.semaphore("smOut"))
        _sCC = ent(nc.semaphore("smCC"))
        _sPE = ent(nc.semaphore("smPE"))
        _sACT = ent(nc.semaphore("smACT"))
        _sDVE = ent(nc.semaphore("smDVE"))
        block = ent(nc.Block())
        sC, sX, sCS = Ct(_sC), Ct(_sX), Ct(_sCS)
        sSl = [Ct(_sSlA), Ct(_sSlB)]
        sG = [Ct(_sGA), Ct(_sGB)]
        sScr = [Ct(_sScr), Ct(_sScrB)]
        sCc, sCo, sGc = Ct(_sCc), Ct(_sCo), Ct(_sGc)
        sPp = [Ct(_sPpA), Ct(_sPpB)]
        sGP = Ct(_sGP)
        sOg = [Ct(_sOgA), Ct(_sOgB)]
        sXg, sCSg = Ct(_sXg), Ct(_sCSg)
        sSl3 = [sSl[0], sSl[1], Ct(_sSlC), Ct(_sSlD)]
        sGq = [Ct(_sGq0), Ct(_sGq1), Ct(_sGq2), Ct(_sGq3)]
        sTp = [Ct(_sTpA), Ct(_sTpB)]
        sOut = [Ct(_sOut), Ct(_sOutB)]
        sCC = Ct(_sCC)
        sPE, sACT, sDVE = Ct(_sPE), Ct(_sACT), Ct(_sDVE)
        ENG_CT.update({"tensor": sPE, "vector": sDVE, "scalar": sACT, "gpsimd": sGP})

        CA1, SAv, GAC, CB1, SBv, GBC, GBIAS = (camod[:, 32 * i:32 * i + 32] for i in range(7))

        modpart = scrA[:, 384:408]
        routb = sig[:, 0:8]
        routT = psiT[0:8, 0:512]
        modcol = scrB[:, 0:192]
        routf = coefs[:, 0:8]

        scrUse = {"a": 0, "v": 0, "t": 0}

        def mark_scratch():
            scrUse["a"], scrUse["v"], scrUse["t"] = sACT.n, sDVE.n, sPE.n

        def wait_scratch(eng):
            if eng != "scalar":
                WAIT(eng, sACT, scrUse["a"])
            if eng != "vector":
                WAIT(eng, sDVE, scrUse["v"])
            if eng != "tensor":
                WAIT(eng, sPE, scrUse["t"])

        DMA("sync", sC, idfs[:], idf)
        DMA("sync", sC, idbs[:], idbI)
        DMA("sync", sC, onesb[:], onesbI)
        DMA("sync", sC, onesc[:], onescI)
        DMA("sync", sC, aAB[:], aABm)
        DMA("sync", sC, m3s[:], m3m)
        DMA("sync", sC, vmk[:], vmkm)
        DMA("sync", sC, biasC[:], biasM)
        DMA("sync", sC, scrB[0:32, 0:128], cvec.rearrange("o (a b) -> (o a) b", b=128))
        vC = sC.n
        DMA("gpsimd", sGc, gwb[:].rearrange("p (b e) -> p b e", e=8),
            gwT.rearrange("(b p) e -> p b e", p=128))
        DMA("gpsimd", sGc, icsb[:], icsM)
        DMA("gpsimd", sGc, ocsb[:], ocsM)
        DMA("gpsimd", sGc, aosb[:, 0:4096].rearrange("p (b f) -> p b f", f=128),
            wkT.rearrange("(b p) f -> p b f", p=128))
        DMA("gpsimd", sGc, aosb[:, 4096:8192].rearrange("p (b f) -> p b f", f=128),
            wvT.rearrange("(b p) f -> p b f", p=128))
        vGsmall = sGc.n
        DMA("sync", sX, xsb[:, 0:8192].rearrange("p (b t) -> p b t", t=TT),
            xT[0:2048, 0:TT].rearrange("(b p) t -> p b t", p=128))
        DMA("sync", sX, xsb[:, 14336:16384].rearrange("p (b t) -> p b t", t=TT),
            xT[3584:4096, 0:TT].rearrange("(b p) t -> p b t", p=128))
        DMA("sync", sCS, cossb[:], cosE[:, 0:TT])
        DMA("sync", sCS, sinsb[:], sinE[:, 0:TT])
        vCSl0 = sCS.n

        WAIT("scalar", sC, vC)
        vSilu = OP("scalar", lambda e: e.activation(out=scrB[0:32, 128:256], in_=scrB[0:32, 0:128], func=AF.Silu))
        WAIT("tensor", sACT, vSilu)
        WAIT("tensor", sC, vC)
        vTs = OP("tensor", lambda e: e.transpose(pQ1[:, 0:32], scrB[0:32, 128:256], idfs[0:32, 0:32]))
        WAIT("vector", sPE, vTs)
        vScol = OP("vector", lambda e: e.tensor_copy(out=scol32[:], in_=pQ1[:, 0:32]))

        mod_evac = 0
        peW = {0: 0, 1: 0}
        for jb in range(n_jb):
            j0 = jb * 24
            jn = min(24, jpc - j0)
            for i in range(ND):
                sl = i % 2
                xbuf = xsb[:, 8192 + 3072 * sl:8192 + 3072 * sl + 3072]
                WAIT("gpsimd", sPE, peW[sl])
                DMA("gpsimd", sG[sl], xbuf[:, 0:128 * jn],
                    wmodS[128 * i:128 * i + 128, 128 * j0:128 * (j0 + jn)])
                if jb == 0 and i == 0:
                    WAIT("tensor", sDVE, vScol)
                WAIT("tensor", sG[sl], sG[sl].n)
                for j in range(jn):
                    fn = (lambda e, xbuf=xbuf, j=j, i=i:
                          e.matmul(pS[:, (32 * j + i):(32 * j + i + 1)],
                                   xbuf[:, 128 * j:128 * j + 128],
                                   scol32[:, i:i + 1], start=True, stop=True))
                    if j == jn - 1:
                        vmm = OP("tensor", fn)
                    else:
                        MM(fn)
                peW[sl] = vmm
            for j in range(jn):
                if j == 0:
                    WAIT("vector", sPE, vmm)
                vred = OP("vector", lambda e, j=j: e.tensor_reduce(
                    out=modpart[:, j:j + 1], in_=pS[:, 32 * j:32 * j + 32], axis=AX.X, op=ALU.add))
            WAIT("tensor", sDVE, vred)
            vTp = OP("tensor", lambda e, jn=jn: e.transpose(pQ0[0:jn, 0:128], modpart[:, 0:jn], idfs[:]))
            WAIT("vector", sPE, vTp)
            WAIT("vector", sCc, mod_evac)
            vEv = OP("vector", lambda e, jn=jn: e.tensor_copy(out=scrA[0:jn, 256:384], in_=pQ0[0:jn, 0:128]))
            WAIT("sync", sDVE, vEv)
            DMA("sync", sCc, ccin[j0:j0 + jn, :], scrA[0:jn, 256:384])
            mod_evac = sCc.n

        WAIT("sync", sPE, vmm)
        DMA("sync", sX, xsb[:, 8192:14336].rearrange("p (b t) -> p b t", t=TT),
            xT[2048:3584, 0:TT].rearrange("(b p) t -> p b t", p=128))
        vXl0 = sX.n
        WAIT("gpsimd", sCc, mod_evac)
        em("gpsimd", lambda e: e.collective_compute(
            "AllGather", ALU.bypass, replica_groups=[list(range(n_cores))],
            ins=[ccin], outs=[ccout]).then_inc(sCC.h, 1))
        sCC.n += 1
        WAIT("sync", sCC, sCC.n)
        R = jpc * n_cores
        g0, tI = 0, 0
        vT2 = 0
        while g0 < R:
            gn = min(96, R - g0)
            WAIT("sync", sPE, vT2)
            DMA("sync", sCo, scrA[0:gn, 0:128], ccout[g0:g0 + gn, :])
            WAIT("tensor", sCo, sCo.n)
            vT2 = OP("tensor", lambda e, gn=gn, tI=tI: e.transpose(
                pQ1[0:128, 128 * tI:128 * tI + gn], scrA[0:gn, 0:128], idfs[0:gn, 0:gn]))
            WAIT("vector", sPE, vT2)
            vMc = OP("vector", lambda e, gn=gn, g0=g0, tI=tI: e.tensor_copy(
                out=modcol[:, g0:g0 + gn], in_=pQ1[:, 128 * tI:128 * tI + gn]))
            g0 += gn
            tI += 1
        OP("vector", lambda e: e.tensor_copy(out=SAv, in_=modcol[:, 0:32]))
        OP("vector", lambda e: e.tensor_scalar(out=CA1, in0=modcol[:, 32:64], scalar1=1.0, scalar2=None, op0=ALU.add))
        OP("vector", lambda e: e.tensor_copy(out=SBv, in_=modcol[:, 96:128]))
        OP("vector", lambda e: e.tensor_scalar(out=CB1, in0=modcol[:, 128:160], scalar1=1.0, scalar2=None, op0=ALU.add))
        for (ga_s, dst) in ((64, GAC), (160, GBC)):
            g = modcol[:, ga_s:ga_s + 32]
            OP("vector", lambda e, g=g: e.tensor_scalar(out=scrC[:, 0:32], in0=g, scalar1=0.0, scalar2=None, op0=ALU.is_ge))
            OP("vector", lambda e, g=g: e.tensor_scalar(out=scrC[:, 32:64], in0=g, scalar1=0.25, scalar2=None, op0=ALU.max))
            OP("vector", lambda e, g=g: e.tensor_scalar(out=scrC[:, 64:96], in0=g, scalar1=-0.25, scalar2=None, op0=ALU.min))
            OP("vector", lambda e: e.tensor_sub(scrC[:, 96:128], scrC[:, 32:64], scrC[:, 64:96]))
            OP("vector", lambda e: e.tensor_mul(scrC[:, 96:128], scrC[:, 96:128], scrC[:, 0:32]))
            OP("vector", lambda e, dst=dst: e.tensor_add(dst, scrC[:, 96:128], scrC[:, 64:96]))
        WAIT("vector", sC, vC)
        vMod = OP("vector", lambda e: e.tensor_mul(GBIAS, GBC, biasC[:]))
        mark_scratch()

        pp = [pp0, pp1]
        prep_state = {"prevWr": {0: 0, 1: 0}, "qidx": 0}
        prep_items = []
        scr_done = {}

        def mk_prep_item(wsrc, scr, do_sign, i, q, last):
            def emit():
                sl = prep_state["qidx"] % 2
                prep_state["qidx"] += 1
                WAIT("gpsimd", sScr[sl], prep_state["prevWr"][sl])
                DMA("gpsimd", sPp[sl], pp[sl][:], wsrc[128 * i:128 * i + 128, 1024 * q:1024 * q + 1024])
                WAIT("gpsimd", sPp[sl], sPp[sl].n)
                DMA("gpsimd", sScr[sl], scr[8 * q:8 * q + 8, :, i, :].rearrange("c p f -> p c f"),
                    pp[sl][:].rearrange("p (c f) -> p c f", f=128))
                prep_state["prevWr"][sl] = sScr[sl].n
                if last:
                    scr_done[id(scr)] = (sScr[0].n, sScr[1].n)
            return emit

        def mat_items(wsrc, scr, do_sign):
            return [mk_prep_item(wsrc, scr, do_sign, i, q, i == ND - 1 and q == 3)
                    for i in range(ND) for q in range(4)]

        _wff_items = mat_items(wfT, scrf, False)
        _wo_items = mat_items(woT, scro, False)
        prep_items.extend(_wo_items)
        prep_items.extend(_wff_items)

        def weave(n):
            for _ in range(min(n, len(prep_items))):
                prep_items.pop(0)()

        def scr_gate(scr):
            v = scr_done.get(id(scr))
            if v is None:
                while prep_items and id(scr) not in scr_done:
                    prep_items.pop(0)()
                v = scr_done[id(scr)]
            return v

        def ln_stats(src_f32, mdst, rdst, extra_rows=None):
            wait_scratch("vector")
            wait_scratch("scalar")
            v = OP("vector", lambda e: e.tensor_copy(out=scrA[:], in_=src_f32[:, 0:512]))
            aq = OP("scalar", lambda e: e.activation(out=scrB[:], in_=src_f32[:, 0:512], func=AF.Square))
            vadd2 = 0
            for cix in range(1, ND):
                ch = src_f32[:, 512 * cix:512 * cix + 512]
                OP("vector", lambda e, ch=ch: e.tensor_add(scrA[:], scrA[:], ch))
                if vadd2:
                    WAIT("scalar", sDVE, vadd2)
                aq = OP("scalar", lambda e, ch=ch: e.activation(out=scrC[:], in_=ch, func=AF.Square))
                WAIT("vector", sACT, aq)
                vadd2 = OP("vector", lambda e: e.tensor_add(scrB[:], scrB[:], scrC[:]))
            WAIT("tensor", sDVE, vadd2)
            return ln_tail(mdst, rdst, extra_rows=extra_rows)

        def ln_tail(mdst, rdst, vdep=0, extra_rows=None):
            wait_scratch("tensor")
            WAIT("tensor", sDVE, vdep)
            OP("tensor", lambda e: e.matmul(pVo1[0:1, 0:512], onesc[:], scrA[:], start=True, stop=True))
            vmm = OP("tensor", lambda e: e.matmul(pVo0[0:1, 0:512], onesc[:], scrB[:], start=True, stop=True))
            WAIT("vector", sPE, vmm)
            OP("vector", lambda e: e.tensor_scalar(out=scrC[0:1, 0:512], in0=pVo1[0:1, 0:512],
                                                   scalar1=1.0 / D, scalar2=None, op0=ALU.mult))
            OP("vector", lambda e: e.tensor_scalar(out=scrB[0:1, 0:512], in0=pVo0[0:1, 0:512],
                                                   scalar1=1.0 / D, scalar2=None, op0=ALU.mult))
            OP("vector", lambda e: e.tensor_mul(scrA[0:1, 0:512], scrC[0:1, 0:512], scrC[0:1, 0:512]))
            v = OP("vector", lambda e: e.scalar_tensor_tensor(
                out=scrB[0:1, 0:512], in0=scrB[0:1, 0:512], scalar=EPS, in1=scrA[0:1, 0:512],
                op0=ALU.add, op1=ALU.subtract))
            WAIT("scalar", sDVE, v)
            aS = OP("scalar", lambda e: e.activation(out=scrA[0:1, 0:512], in_=scrB[0:1, 0:512], func=AF.Sqrt))
            WAIT("vector", sACT, aS)
            OP("vector", lambda e: e.reciprocal(out=scrB[0:1, 0:512], in_=scrA[0:1, 0:512]))
            OP("vector", lambda e, mdst=mdst: e.tensor_copy(out=mdst, in_=scrC[0:1, 0:512]))
            v = OP("vector", lambda e, rdst=rdst: e.tensor_copy(out=rdst, in_=scrB[0:1, 0:512]))
            if extra_rows is not None:
                lo, hi, own0 = extra_rows
                OP("vector", lambda e, lo=lo, hi=hi, own0=own0: e.tensor_copy(
                    out=mrow[0:1, own0:own0 + hi - lo], in_=scrC[0:1, lo:hi]))
                v = OP("vector", lambda e, lo=lo, hi=hi, own0=own0: e.tensor_copy(
                    out=rrow[0:1, own0:own0 + hi - lo], in_=scrB[0:1, lo:hi]))
            return v

        def ln_bcast(mr, rr, vrows=0):
            wait_scratch("tensor")
            WAIT("tensor", sDVE, vrows)
            OP("tensor", lambda e, mr=mr: e.matmul(pS[:, 0:512], onesb[:], mr, start=True, stop=True))
            return OP("tensor", lambda e, rr=rr: e.matmul(pS[:, 512:1024], onesb[:], rr, start=True, stop=True))

        def ln_apply(src_f32, dst_bf, c1col, shcol, vdep_pe):
            lastAs = []
            for cix in range(ND):
                ch = src_f32[:, 512 * cix:512 * cix + 512]
                dch = dst_bf[:, 512 * cix:512 * cix + 512]
                pa, pb = (scrA, scrB) if cix % 2 == 0 else (scrC, scrD)
                if cix == 0:
                    WAIT("vector", sPE, vdep_pe)
                    wait_scratch("vector")
                if cix >= 2:
                    WAIT("vector", sACT, lastAs[cix - 2])
                OP("vector", lambda e, ch=ch, pa=pa: e.tensor_sub(pa[:], ch, pS[:, 0:512]))
                vv = OP("vector", lambda e, pa=pa, pb=pb: e.tensor_mul(pb[:], pa[:], pS[:, 512:1024]))
                WAIT("scalar", sDVE, vv)
                lastAs.append(OP("scalar", lambda e, dch=dch, cix=cix, pb=pb: e.activation(
                    out=dch, in_=pb[:], func=AF.Identity,
                    bias=shcol[:, cix:cix + 1], scale=c1col[:, cix:cix + 1])))
            return lastAs

        vFree = [0, 0, 0, 0]
        lastVevac = 0
        lastHprev = 0
        for et in range(n_ext):
            t0 = et * TT
            if et > 0:
                WAIT("sync", sACT, lastHprev)
                DMA("sync", sX, xsb[:].rearrange("p (b t) -> p b t", t=TT),
                    xT[:, t0:t0 + TT].rearrange("(b p) t -> p b t", p=128))
                vXl = sX.n
                WAIT("sync", sDVE, sDVE.n)
                DMA("sync", sCS, cossb[:], cosE[:, t0:t0 + TT])
                DMA("sync", sCS, sinsb[:], sinE[:, t0:t0 + TT])
                vCSl = sCS.n
            else:
                vXl = vXl0
                vCSl = vCSl0
            WAIT("vector", sX, vXl)
            WAIT("scalar", sX, vXl)
            own_lo = max(0, 256 - t0)
            own_hi = min(512, 256 + core_t - t0)
            vRows = ln_stats(xsb, rowsb[0:1, 0:512], rowsb[0:1, 512:1024],
                             extra_rows=(own_lo, own_hi, t0 - 256 + own_lo))
            vB = ln_bcast(rowsb[0:1, 0:512], rowsb[0:1, 512:1024], vRows)
            lastHs = ln_apply(xsb, hsb, CA1, SAv, vB)
            lastH = lastHs[-1]
            lastHprev = lastH
            vGrp = {}
            if et == 0:
                WAIT("tensor", sGc, vGsmall)
            for which, wbase, pdst in (("k", 0, pQ0), ("v", 4096, pQ1)):
                waited = 0
                for i in range(ND):
                    if lastHs[i] > waited:
                        waited = lastHs[i]
                        WAIT("tensor", sACT, waited)
                    fn = (lambda e, wbase=wbase, i=i, pdst=pdst: e.matmul(
                        pdst[:], aosb[:, wbase + 128 * i:wbase + 128 * i + 128],
                        hsb[:, 512 * i:512 * i + 512], start=(i == 0), stop=(i == ND - 1)))
                    if i == ND - 1:
                        vg = OP("tensor", fn)
                    else:
                        MM(fn)
                vGrp[which] = vg
            WAIT("vector", sPE, vGrp["k"])
            WAIT("vector", sCS, vCSl)
            OP("vector", lambda e: e.tensor_mul(scrA[:], pQ0[:], cossb[:]))
            OP("vector", lambda e: e.tensor_mul(scrB[0:64, :], pQ0[64:128, :], sinsb[0:64, :]))
            OP("vector", lambda e: e.tensor_mul(scrB[64:128, :], pQ0[0:64, :], sinsb[64:128, :]))
            OP("vector", lambda e, t0=t0: e.tensor_sub(kres[0:64, t0:t0 + TT], scrA[0:64, :], scrB[0:64, :]))
            OP("vector", lambda e, t0=t0: e.tensor_add(kres[64:128, t0:t0 + TT], scrA[64:128, :], scrB[64:128, :]))
            WAIT("scalar", sPE, vGrp["v"])
            vVc = OP("scalar", lambda e: e.activation(out=sig[:, 0:512], in_=pQ1[:], func=AF.Copy))
            WAIT("tensor", sACT, vVc)
            WAIT("tensor", sDVE, lastVevac)
            for j in range(4):
                vTt = OP("tensor", lambda e, j=j: e.transpose(
                    pTb[:, 128 * j:128 * j + 128], sig[:, 128 * j:128 * j + 128], idbs[:]))
            WAIT("vector", sPE, vTt)
            for j in range(4):
                kb = 4 * et + j
                lastVevac = OP("vector", lambda e, j=j, kb=kb: e.tensor_scalar(
                    out=vtm[:, 128 * kb:128 * kb + 128], in0=pTb[:, 128 * j:128 * j + 128],
                    scalar1=vmk[:, kb:kb + 1], scalar2=None, op0=ALU.mult))
            mark_scratch()
            weave(52)

        vSigACT = {}
        vPsi = {}
        vTpPE = {}
        vTbEvac = {}
        vVout = {}
        vEvac = {}
        uG = 0
        uTile = {}

        def emit_trans(u):
            slot = u % 2
            pT = pTb if slot == 0 else pTb2
            WAIT("tensor", sDVE, vPsi[u])
            if u >= 2:
                WAITV("tensor", vTbEvac[u - 2])
            for cx in range(6):
                vTpPE[u] = OP("tensor", lambda e, pT=pT, cx=cx, slot=slot: e.transpose(
                    pT[:, 128 * cx:128 * cx + 128],
                    sig[:, 768 * slot + 128 * cx:768 * slot + 128 * cx + 128], idbs[:]))

        def emit_tbevac(u):
            slot = u % 2
            pT = pTb if slot == 0 else pTb2
            eng = "vector" if slot == 0 else "scalar"
            ct = sDVE if slot == 0 else sACT
            WAIT(eng, sPE, vTpPE[u])
            if u >= 2:
                WAIT(eng, sPE, vVout[u - 2])
            if slot == 0:
                v = OP(eng, lambda e, pT=pT, slot=slot: e.tensor_copy(
                    out=psiT[:, 768 * slot:768 * slot + 768], in_=pT[:, 0:768]))
            else:
                v = OP(eng, lambda e, pT=pT, slot=slot: e.activation(
                    out=psiT[:, 768 * slot:768 * slot + 768], in_=pT[:, 0:768], func=AF.Copy))
            vTbEvac[u] = (ct, v)

        def emit_V(u):
            h_ = (u % 128) // 4
            qc_ = u % 4
            tl = uTile[u]
            lqb = 2 * tl + qc_ // 2
            slot = u % 2
            WAITV("tensor", vTbEvac[u])
            if u >= 2:
                WAITV("tensor", vEvac[u - 2])
            pV = pVo0 if slot == 0 else pVo1
            for cx in range(6):
                kb = 2 * lqb + cx
                fn = (lambda e, pV=pV, cx=cx, kb=kb, slot=slot: e.matmul(
                    pV[:, 0:128], vtm[:, 128 * kb:128 * kb + 128],
                    psiT[:, 768 * slot + 128 * cx:768 * slot + 128 * cx + 128],
                    start=(cx == 0), stop=(cx == 5)))
                if cx == 5:
                    vVout[u] = OP("tensor", fn)
                else:
                    MM(fn)
            WAIT("scalar", sPE, vVout[u])
            vEvac[u] = (sACT, OP("scalar", lambda e, h_=h_, qc_=qc_, pV=pV: e.activation(
                out=aosb[:, 512 * h_ + 128 * qc_:512 * h_ + 128 * qc_ + 128],
                in_=pV[:, 0:128], func=AF.Copy)))

        WSL3 = [wslA, wslB, wslC, wslD]
        mmg_state = {"hidx": 0}

        def mm_group(wsrc, c, rhs_sb, pdst, extra_pre=None, rhs_ready=None, sign=False):
            if extra_pre:
                extra_pre()
            vh = []
            slots = []
            vsg = [0, 0]
            for half in range(2):
                sl = mmg_state["hidx"] % 4
                mmg_state["hidx"] += 1
                slots.append(sl)
                WAIT("sync", sPE, vFree[sl])
                DMA("sync", sSl3[sl], WSL3[sl][:].rearrange("p (b f) -> p b f", f=128),
                    wsrc[c, :, 16 * half:16 * half + 16, :])
                vh.append(sSl3[sl].n)
                if sign:
                    WAIT("vector", sSl3[sl], vh[half])
                    WAIT("vector", sPE, vFree[sl])
                    OP("vector", lambda e, sl=sl: e.tensor_scalar(
                        out=WSL3[sl][:], in0=WSL3[sl][:], scalar1=1e38, scalar2=None, op0=ALU.mult))
                    vsg[half] = OP("vector", lambda e, sl=sl: e.tensor_scalar(
                        out=WSL3[sl][:], in0=WSL3[sl][:], scalar1=-1.0, scalar2=1.0, op0=ALU.max, op1=ALU.min))
            out = None
            waited = 0
            for half in range(2):
                sl = slots[half]
                wsl = WSL3[sl]
                WAIT("tensor", sSl3[sl], vh[half])
                if sign:
                    WAIT("tensor", sDVE, vsg[half])
                for b in range(16):
                    i = 16 * half + b
                    if rhs_ready is not None and rhs_ready[i] > waited:
                        waited = rhs_ready[i]
                        WAIT("tensor", sACT, waited)
                    fn = (lambda e, wsl=wsl, b=b, i=i, pdst=pdst: e.matmul(
                        pdst[:], wsl[:, 128 * b:128 * b + 128],
                        rhs_sb[:, 512 * i:512 * i + 512], start=(i == 0), stop=(i == ND - 1)))
                    if i == 15 or i == ND - 1:
                        out = OP("tensor", fn)
                        vFree[sl] = out
                    else:
                        MM(fn)
            return out

        vXl_pref = None
        vCSl_pref = None
        vStSlot = {0: 0, 1: 0}
        for tile in range(n_tiles):
            t0 = tile * TT
            e0 = t0 + 256
            if vXl_pref is None:
                WAIT("sync", sACT, lastHprev)
                DMA("sync", sX, xsb[:].rearrange("p (b t) -> p b t", t=TT),
                    xT[:, e0:e0 + TT].rearrange("(b p) t -> p b t", p=128))
                vXl = (sX, sX.n)
                WAIT("sync", sDVE, sDVE.n)
                DMA("sync", sCS, cossb[:], cosE[:, e0:e0 + TT])
                DMA("sync", sCS, sinsb[:], sinE[:, e0:e0 + TT])
                vCSl = (sCS, sCS.n)
            else:
                vXl, vCSl = vXl_pref, vCSl_pref
            WAITV("vector", vXl)
            WAITV("scalar", vXl)
            vB = ln_bcast(mrow[0:1, t0:t0 + 512], rrow[0:1, t0:t0 + 512])
            lastHs = ln_apply(xsb, hsb, CA1, SAv, vB)
            lastH = lastHs[-1]

            vRope = {}
            vLastScores = sPE.n
            vQmm = {}
            qsub_state = {}

            def q_load(h):
                if tile == 0:
                    weave(4)
                vh, slots = [], []
                for half in range(2):
                    sl = mmg_state["hidx"] % 4
                    mmg_state["hidx"] += 1
                    slots.append(sl)
                    WAIT("gpsimd", sPE, vFree[sl])
                    DMA("gpsimd", sGq[sl], WSL3[sl][:].rearrange("p (b f) -> p b f", f=128),
                        wqT[2048 * half:2048 * half + 2048, 128 * h:128 * h + 128]
                        .rearrange("(b p) f -> p b f", p=128))
                    vh.append((sGq[sl], sGq[sl].n))
                qsub_state[h] = (slots, vh)

            def q_sub(h, j, rhs_ready=None):
                slots, vh = qsub_state[h]
                half = j // 2
                sl = slots[half]
                wsl = WSL3[sl]
                if j == 0:
                    if h >= 2:
                        WAITV("tensor", vRope[h - 2])
                    WAIT("tensor", sACT, lastH)
                if j % 2 == 0:
                    WAITV("tensor", vh[half])
                out = None
                waited = 0
                for b in range(8):
                    i = 8 * j + b
                    if rhs_ready is not None and rhs_ready[i] > waited:
                        waited = rhs_ready[i]
                        WAIT("tensor", sACT, waited)
                    fn = (lambda e, wsl=wsl, i=i, h=h: e.matmul(
                        (pQ0 if h % 2 == 0 else pQ1)[:], wsl[:, 128 * (i % 16):128 * (i % 16) + 128],
                        hsb[:, 512 * i:512 * i + 512], start=(i == 0), stop=(i == ND - 1)))
                    if i in (15, ND - 1):
                        out = OP("tensor", fn)
                        vFree[sl] = out
                    else:
                        MM(fn)
                if j == 3:
                    vQmm[h] = out
                return out

            def emit_rope(h):
                qsl2 = h % 2
                WAIT("vector", sPE, vQmm[h])
                WAITV("vector", vCSl)
                if h >= 2:
                    WAIT("vector", sPE, vLastScores)
                pQh = pQ0 if qsl2 == 0 else pQ1
                qs = qslot[:, 512 * qsl2:512 * qsl2 + 512]
                OP("vector", lambda e, pQh=pQh: e.tensor_mul(scrA[:], pQh[:], cossb[:]))
                OP("vector", lambda e, pQh=pQh: e.tensor_mul(scrB[0:64, :], pQh[64:128, :], sinsb[0:64, :]))
                OP("vector", lambda e, pQh=pQh: e.tensor_mul(scrB[64:128, :], pQh[0:64, :], sinsb[64:128, :]))
                OP("vector", lambda e, qs=qs: e.tensor_sub(qs[0:64, :], scrA[0:64, :], scrB[0:64, :]))
                vRope[h] = (sDVE, OP("vector", lambda e, qs=qs: e.tensor_add(qs[64:128, :], scrA[64:128, :], scrB[64:128, :])))

            SUBLAG = 5
            q_load(0)
            for j4 in range(4):
                q_sub(0, j4, rhs_ready=lastHs)
            emit_rope(0)
            q_load(1)
            q_sub(1, 0)
            for h in range(H):
                qsl = h % 2
                for qc in range(4):
                    u = uG
                    uG += 1
                    uTile[u] = tile
                    lqc = 4 * tile + qc
                    lqb = 2 * tile + qc // 2
                    slot = u % 2
                    sm = sums[:, 8 * slot:8 * slot + 8]
                    cf = coefs[:, 4 * slot:4 * slot + 4]
                    qs = qslot[:, 512 * qsl:512 * qsl + 512]
                    s = 4 * h + qc + SUBLAG
                    if s < 4 * H:
                        h2, j2 = divmod(s, 4)
                        if j2 == 0 and h2 >= 2:
                            q_load(h2)
                        q_sub(h2, j2)
                        if j2 == 3:
                            emit_rope(h2)
                    WAITV("tensor", vRope[h])
                    if u >= 1:
                        WAIT("tensor", sACT, vSigACT[u - 1])
                    k0 = 256 * lqb
                    for (ks, kn) in ((0, 512), (512, 256)):
                        vSc = OP("tensor", lambda e, qs=qs, qc=qc, k0=k0, ks=ks, kn=kn: e.matmul(
                            pS[:, ks:ks + kn], qs[:, 128 * qc:128 * qc + 128],
                            kres[:, k0 + ks:k0 + ks + kn], start=True, stop=True))
                    vLastScores = vSc
                    WAIT("scalar", sPE, vSc)
                    if u >= 2:
                        WAIT("scalar", sPE, vTpPE[u - 2])
                    for j3 in range(3):
                        vSigACT[u] = OP("scalar", lambda e, slot=slot, j3=j3, sm=sm: e.activation(
                            out=sig[:, 768 * slot + 256 * j3:768 * slot + 256 * j3 + 256],
                            in_=pS[:, 256 * j3:256 * j3 + 256], func=AF.Sigmoid, scale=SCALE,
                            accum_out=sm[:, j3:j3 + 1]))
                    WAIT("vector", sACT, vSigACT[u])
                    OP("vector", lambda e, sm=sm: e.tensor_add(sm[:, 4:5], sm[:, 0:1], sm[:, 1:2]))
                    OP("vector", lambda e, sm=sm, lqb=lqb: e.scalar_tensor_tensor(
                        out=sm[:, 5:6], in0=sm[:, 2:3], scalar=m3s[:, lqb:lqb + 1],
                        in1=sm[:, 1:2], op0=ALU.mult, op1=ALU.add))
                    OP("vector", lambda e, sm=sm: e.reciprocal(out=sm[:, 6:8], in_=sm[:, 4:6]))
                    OP("vector", lambda e, sm=sm, cf=cf, lqc=lqc: e.tensor_mul(
                        cf[:, 0:2], sm[:, 6:8], aAB[:, 2 * lqc:2 * lqc + 2]))
                    OP("vector", lambda e, cf=cf: e.tensor_add(cf[:, 2:3], cf[:, 0:1], cf[:, 1:2]))
                    for (j3, cc) in ((0, 0), (1, 2), (2, 1)):
                        vPsi[u] = OP("vector", lambda e, slot=slot, j3=j3, cc=cc, cf=cf: e.tensor_scalar(
                            out=sig[:, 768 * slot + 256 * j3:768 * slot + 256 * j3 + 256],
                            in0=sig[:, 768 * slot + 256 * j3:768 * slot + 256 * j3 + 256],
                            scalar1=cf[:, cc:cc + 1], scalar2=None, op0=ALU.mult))
                    if u >= 2 and (u - 2) not in vVout:
                        emit_V(u - 2)
                    if u >= 1 and (u - 1) not in vTpPE:
                        emit_trans(u - 1)
                        emit_tbevac(u - 1)
            uL = uG - 1
            if uL not in vTpPE:
                emit_trans(uL)
                emit_tbevac(uL)
            if (uL - 1) not in vVout:
                emit_V(uL - 1)
            emit_V(uL)
            vAttnDone = vEvac[uL]
            mark_scratch()

            vRopeLast = vRope[H - 1][1] if vRope[H - 1][0] is sDVE else None
            vX2 = {}
            vp2 = 0
            for c in range(ND):
                def pre_wo(c=c):
                    if c >= 2:
                        WAIT("tensor", sDVE, vX2[c - 2])
                    if c == 0:
                        WAITV("tensor", vAttnDone)
                if tile == 0:
                    if c == 0:
                        vWo = scr_gate(scro)
                        WAIT("sync", sScr[0], vWo[0])
                        WAIT("sync", sScr[1], vWo[1])
                    weave(3)
                vW = mm_group(scro, c, aosb, pQ0 if c % 2 == 0 else pQ1, extra_pre=pre_wo)
                WAIT("vector", sPE, vW)
                pQc = pQ0 if c % 2 == 0 else pQ1
                vX2[c] = OP("vector", lambda e, c=c, pQc=pQc: e.scalar_tensor_tensor(
                    out=xsb[:, 512 * c:512 * c + 512], in0=pQc[:], scalar=GAC[:, c:c + 1],
                    in1=xsb[:, 512 * c:512 * c + 512], op0=ALU.mult, op1=ALU.add))
                ch = xsb[:, 512 * c:512 * c + 512]
                if c == 0:
                    wait_scratch("vector")
                    OP("vector", lambda e, ch=ch: e.tensor_copy(out=scrA[:], in_=ch))
                    WAIT("scalar", sDVE, vX2[0])
                    aq = OP("scalar", lambda e, ch=ch: e.activation(out=scrB[:], in_=ch, func=AF.Square))
                else:
                    OP("vector", lambda e, ch=ch: e.tensor_add(scrA[:], scrA[:], ch))
                    tmp = scrC if c % 2 == 0 else scrD
                    WAIT("scalar", sDVE, vp2 if vp2 else vX2[c])
                    WAIT("scalar", sDVE, vX2[c])
                    aq = OP("scalar", lambda e, ch=ch, tmp=tmp: e.activation(out=tmp[:], in_=ch, func=AF.Square))
                    WAIT("vector", sACT, aq)
                    vp2 = OP("vector", lambda e, tmp=tmp: e.tensor_add(scrB[:], scrB[:], tmp[:]))

            vTail = ln_tail(rowsb[0:1, 0:512], rowsb[0:1, 512:1024], vp2)
            vB2 = ln_bcast(rowsb[0:1, 0:512], rowsb[0:1, 512:1024], vTail)
            lastH2s = ln_apply(xsb, hsb, CB1, SBv, vB2)
            lastH2 = lastH2s[-1]

            WAIT("tensor", sACT, vOsc0 if tile > 0 else 0)
            vRt = 0
            vLgs = {}
            for tc in range(4):
                if tc == 0:
                    if tile == 0:
                        WAIT("tensor", sGc, vGsmall)
                    waited = 0
                else:
                    WAIT("tensor", sACT, vExp)
                for i in range(ND):
                    if tc == 0 and lastH2s[i] > waited:
                        waited = lastH2s[i]
                        WAIT("tensor", sACT, waited)
                    fn = (lambda e, tc=tc, i=i: e.matmul(
                        pVo0[:, 0:8], hsb[:, 512 * i + 128 * tc:512 * i + 128 * tc + 128],
                        gwb[:, 8 * i:8 * i + 8], start=(i == 0), stop=(i == ND - 1)))
                    if i == ND - 1:
                        vLg = OP("tensor", fn)
                    else:
                        MM(fn)
                WAIT("vector", sPE, vLg)
                OP("vector", lambda e: e.tensor_reduce(out=nmx[:, 0:1], in_=pVo0[:, 0:8], axis=AX.X, op=ALU.max))
                OP("vector", lambda e: e.tensor_scalar(out=nmx[:, 1:2], in0=nmx[:, 0:1],
                                                       scalar1=-1.0, scalar2=None, op0=ALU.mult))
                WAIT("scalar", sDVE, sDVE.n)
                vExp = OP("scalar", lambda e: e.activation(
                    out=routf[:], in_=pVo0[:, 0:8], func=AF.Exp, bias=nmx[:, 1:2],
                    accum_out=nmx[:, 2:3]))
                if tc == 3:
                    vOsc0 = vExp
                WAIT("vector", sACT, vExp)
                OP("vector", lambda e: e.reciprocal(out=nmx[:, 3:4], in_=nmx[:, 2:3]))
                vRb = OP("vector", lambda e: e.tensor_scalar(
                    out=routb[:], in0=routf[:], scalar1=nmx[:, 3:4], scalar2=None, op0=ALU.mult))
                WAIT("tensor", sDVE, vRb)
                if vRt:
                    WAIT("tensor", sDVE, vRt)
                vTr = OP("tensor", lambda e: e.transpose(pTb[0:8, 512:640], routb[:], idbs[:]))
                WAIT("vector", sPE, vTr)
                vRt = OP("vector", lambda e, tc=tc: e.tensor_copy(
                    out=routT[:, 128 * tc:128 * tc + 128], in_=pTb[0:8, 512:640]))

            vXs = {}
            for c in range(ND):
                psl = c % 2
                if c == 0:
                    WAIT("tensor", sDVE, vRt)
                if c >= 2:
                    WAIT("tensor", sDVE, vXs[c - 2])
                vIs = OP("tensor", lambda e, c=c, psl=psl: e.matmul(
                    pS[:, 512 * psl:512 * psl + 512], icsb[:, 128 * c:128 * c + 128],
                    routT[:], start=True, stop=True))
                WAIT("vector", sPE, vIs)
                vXs[c] = OP("vector", lambda e, c=c, psl=psl: e.tensor_mul(
                    aosb[:, 512 * c:512 * c + 512], hsb[:, 512 * c:512 * c + 512],
                    pS[:, 512 * psl:512 * psl + 512]))

            vEp = {}
            vSt = {}
            for c in range(ND):
                if tile == 0 and c == 0:
                    vWf = scr_gate(scrf)
                    WAIT("sync", sScr[0], vWf[0])
                    WAIT("sync", sScr[1], vWf[1])
                def pre_ff(c=c):
                    if c >= 2:
                        WAIT("tensor", sDVE, vEp[c - 2])
                    if c == 0:
                        WAIT("tensor", sDVE, vXs[ND - 1])
                psl = c % 2
                vF = mm_group(scrf, c, aosb, pQ0 if psl == 0 else pQ1, extra_pre=pre_ff, sign=True)
                if c >= 2:
                    WAIT("tensor", sACT, vOsc)
                vOs = OP("tensor", lambda e, c=c, psl=psl: e.matmul(
                    pS[:, 512 * psl:512 * psl + 512], ocsb[:, 128 * c:128 * c + 128],
                    routT[:], start=True, stop=True))
                WAIT("scalar", sPE, vOs)
                if c >= 2:
                    WAIT("scalar", sDVE, vEp[c - 2])
                oscr = scrC if psl == 0 else scrB
                vOsc = OP("scalar", lambda e, psl=psl, oscr=oscr: e.activation(
                    out=oscr[:], in_=pS[:, 512 * psl:512 * psl + 512], func=AF.Copy))
                WAIT("vector", sPE, vF)
                WAIT("vector", sACT, vOsc)
                WAIT("vector", sOg[0], vStSlot[0])
                OP("vector", lambda e, c=c, psl=psl, oscr=oscr: e.scalar_tensor_tensor(
                    out=scrA[:], in0=(pQ0 if psl == 0 else pQ1)[:], scalar=GBC[:, c:c + 1],
                    in1=oscr[:], op0=ALU.mult, op1=ALU.mult))
                vEp[c] = OP("vector", lambda e, c=c: e.scalar_tensor_tensor(
                    out=ostg[:, 0:512], in0=scrA[:], scalar=GBIAS[:, c:c + 1],
                    in1=xsb[:, 512 * c:512 * c + 512], op0=ALU.add, op1=ALU.add))
                WAIT("gpsimd", sDVE, vEp[c])
                DMA("gpsimd", sOg[0], outT[128 * c:128 * c + 128, t0:t0 + TT], ostg[:, 0:512])
                vSt[c] = sOg[0].n
                vStSlot[0] = sOg[0].n
                if tile < n_tiles - 1:
                    e0n = e0 + TT
                    if c == 0:
                        if vRopeLast is not None:
                            WAIT("gpsimd", sDVE, vRopeLast)
                        DMA("gpsimd", sCSg, cossb[:], cosE[:, e0n:e0n + TT])
                        DMA("gpsimd", sCSg, sinsb[:], sinE[:, e0n:e0n + TT])
                        vCSl_pref = (sCSg, sCSg.n)
                    DMA("gpsimd", sXg, xsb[:, 512 * c:512 * c + 512],
                        xT[128 * c:128 * c + 128, e0n:e0n + TT])
                    vXl_pref = (sXg, sXg.n)
            mark_scratch()

        WAIT("gpsimd", sOg[0], sOg[0].n)

        @block.sync
        def _(e):
            for fn in sched["sync"]:
                fn(e)

        @block.tensor
        def _(e):
            for fn in sched["tensor"]:
                fn(e)

        @block.vector
        def _(e):
            for fn in sched["vector"]:
                fn(e)

        @block.scalar
        def _(e):
            for fn in sched["scalar"]:
                fn(e)

        @block.gpsimd
        def _(e):
            for fn in sched["gpsimd"]:
                fn(e)

    return nc



def _rope_cache(S):
    inv_freq = (1.0 / 10000.0 ** (np.arange(0, DH, 2, dtype=np.float32) / DH)).astype(np.float32)
    t = np.arange(S, dtype=np.float32)
    f = np.outer(t, inv_freq)
    cos = np.zeros((S, DH), dtype=np.float32)
    sin = np.zeros((S, DH), dtype=np.float32)
    cos[:, 0::2] = np.cos(f)
    cos[:, 1::2] = np.cos(f)
    sin[:, 0::2] = np.sin(f)
    sin[:, 1::2] = np.sin(f)
    return cos, sin


def make_in_maps(inputs, n_cores, core_t):
    ext = core_t + 512
    nqc = core_t // 128
    nqb = core_t // 256
    nek = ext // 128
    jpc = 24576 // (128 * n_cores)
    S = n_cores * core_t

    f32 = np.float32
    x = np.asarray(inputs["L0_attn_x"], f32)[0]
    assert x.shape == (S, D), x.shape
    xTfull = np.zeros((D, S + 512), f32)
    xTfull[:, 256:256 + S] = np.ascontiguousarray(x.T)
    wq = np.ascontiguousarray(np.asarray(inputs["L0_attn_Wq"], f32).T)
    wk = np.ascontiguousarray(np.asarray(inputs["L0_attn_Wk"], f32).T)
    wv = np.ascontiguousarray(np.asarray(inputs["L0_attn_Wv"], f32).T)
    wo = np.ascontiguousarray(np.asarray(inputs["L0_attn_Wo"], f32).T)
    wf = np.ascontiguousarray(np.asarray(inputs["L0_ffn_weight"], f32).T)
    wmod = np.asarray(inputs["L0_wmod"], f32)
    cvec = np.ascontiguousarray(np.asarray(inputs["L0_c"], f32).reshape(1, D))
    gw = np.ascontiguousarray(np.asarray(inputs["L0_ffn_gate_w"], f32).T)
    ics = np.ascontiguousarray(np.asarray(inputs["L0_ffn_in_channel_scale"], f32))
    ocs = np.ascontiguousarray(np.asarray(inputs["L0_ffn_out_channel_scale"], f32))
    biasM = np.ascontiguousarray(np.asarray(inputs["L0_ffn_bias"], f32).reshape(32, 128).T)

    cos, sin = _rope_cache(S)
    idfA = np.eye(128, dtype=f32)
    idbA = np.eye(128, dtype=np.float32).astype(ml_dtypes.bfloat16)
    onesbA = np.ones((1, 128), dtype=ml_dtypes.bfloat16)
    onescA = np.ones((128, 1), dtype=f32)
    alpha = np.linspace(0.0, 1.0, 256, dtype=f32)

    maps = []
    for r in range(n_cores):
        r0 = r * core_t
        pos = np.arange(r0 - 256, r0 + core_t + 256)
        posc = np.clip(pos, 0, S - 1)
        cosEa = np.ascontiguousarray(cos[posc].T)
        sinEa = np.ascontiguousarray(sin[posc].T)
        aABa = np.zeros((128, 2 * nqc), f32)
        for qc in range(nqc):
            gqb = (r0 // 256) + qc // 2
            half = qc % 2
            asl = alpha[128 * half:128 * half + 128]
            if gqb == 0:
                aABa[:, 2 * qc] = 0.0
                aABa[:, 2 * qc + 1] = 1.0
            else:
                aABa[:, 2 * qc] = 1.0 - asl
                aABa[:, 2 * qc + 1] = asl
        m3a = np.ones((128, nqb), f32)
        for qb in range(nqb):
            if (r0 // 256) + qb == S // 256 - 1:
                m3a[:, qb] = 0.0
        vma = np.ones((128, nek), f32)
        for kb in range(nek):
            glob = r0 - 256 + 128 * kb
            if glob < 0 or glob >= S:
                vma[:, kb] = 0.0
        maps.append({
            "xT": np.ascontiguousarray(xTfull[:, r0:r0 + ext]),
            "wqT": wq, "woT": wo, "wfT": wf, "wkT": wk, "wvT": wv,
            "wmodS": np.ascontiguousarray(wmod[:, 128 * jpc * r:128 * jpc * (r + 1)]),
            "cvec": cvec, "gwT": gw, "icsM": ics, "ocsM": ocs, "biasM": biasM,
            "cosE": cosEa.astype(ml_dtypes.bfloat16), "sinE": sinEa.astype(ml_dtypes.bfloat16), "aABm": aABa, "m3m": m3a,
            "vmkm": vma, "idf": idfA, "onesbI": onesbA, "onescI": onescA, "idbI": idbA,
        })
    return maps


def kernel(**inputs):
    n_cores, core_t = 8, 2048
    maps = make_in_maps(inputs, n_cores, core_t)
    nc = build_nc(n_cores, core_t)
    res = run_bass_kernel_spmd(nc, maps, core_ids=list(range(n_cores)))
    full = np.concatenate([np.asarray(r["outT"]) for r in res.results], axis=1)
    return np.ascontiguousarray(full.T).reshape(1, n_cores * core_t, D).astype(np.float32)


# version 21
# speedup vs baseline: 1.2190x; 1.1062x over previous
import numpy as np
from contextlib import ExitStack
import ml_dtypes
import concourse.bass as bass
import concourse.mybir as mybir
from concourse.bass_utils import run_bass_kernel_spmd

F32 = mybir.dt.float32
BF16 = mybir.dt.bfloat16
AF = mybir.ActivationFunctionType
ALU = mybir.AluOpType
AX = mybir.AxisListType

D = 4096
ND = 32
DH = 128
H = 32
SCALE = float(1.0 / np.sqrt(128.0))
EPS = 1e-5
TT = 512


class Ct:
    __slots__ = ("h", "n")
    def __init__(self, h):
        self.h = h
        self.n = 0


def build_nc(n_cores, core_t):
    ext = core_t + 512
    n_tiles = core_t // TT
    n_ext = ext // TT
    nqc = core_t // 128
    nqb = core_t // 256
    nek = ext // 128
    jpc = 24576 // (128 * n_cores)
    n_jb = (jpc + 23) // 24

    nc = bass.Bass("TRN2", target_bir_lowering=False, debug=False)

    ap = lambda t: t.ap()
    xT = ap(nc.dram_tensor("xT", [D, ext], F32, kind="ExternalInput"))
    wqT = ap(nc.dram_tensor("wqT", [D, D], F32, kind="ExternalInput"))
    woT = ap(nc.dram_tensor("woT", [D, D], F32, kind="ExternalInput"))
    wfT = ap(nc.dram_tensor("wfT", [D, D], F32, kind="ExternalInput"))
    wkT = ap(nc.dram_tensor("wkT", [D, DH], F32, kind="ExternalInput"))
    wvT = ap(nc.dram_tensor("wvT", [D, DH], F32, kind="ExternalInput"))
    wmodS = ap(nc.dram_tensor("wmodS", [D, 128 * jpc], F32, kind="ExternalInput"))
    cvec = ap(nc.dram_tensor("cvec", [1, D], F32, kind="ExternalInput"))
    gwT = ap(nc.dram_tensor("gwT", [D, 8], F32, kind="ExternalInput"))
    icsM = ap(nc.dram_tensor("icsM", [8, D], F32, kind="ExternalInput"))
    ocsM = ap(nc.dram_tensor("ocsM", [8, D], F32, kind="ExternalInput"))
    biasM = ap(nc.dram_tensor("biasM", [128, 32], F32, kind="ExternalInput"))
    cosE = ap(nc.dram_tensor("cosE", [128, ext], BF16, kind="ExternalInput"))
    sinE = ap(nc.dram_tensor("sinE", [128, ext], BF16, kind="ExternalInput"))
    aABm = ap(nc.dram_tensor("aABm", [128, 2 * nqc], F32, kind="ExternalInput"))
    m3m = ap(nc.dram_tensor("m3m", [128, nqb], F32, kind="ExternalInput"))
    vmkm = ap(nc.dram_tensor("vmkm", [128, nek], F32, kind="ExternalInput"))
    idf = ap(nc.dram_tensor("idf", [128, 128], F32, kind="ExternalInput"))
    onesbI = ap(nc.dram_tensor("onesbI", [1, 128], BF16, kind="ExternalInput"))
    onescI = ap(nc.dram_tensor("onescI", [128, 1], F32, kind="ExternalInput"))
    idbI = ap(nc.dram_tensor("idbI", [128, 128], BF16, kind="ExternalInput"))

    scrq = ap(nc.dram_tensor("scrq", [32, 128, 32, 128], BF16, kind="Internal"))
    scro = ap(nc.dram_tensor("scro", [32, 128, 32, 128], BF16, kind="Internal"))
    scrf = ap(nc.dram_tensor("scrf", [32, 128, 32, 128], BF16, kind="Internal"))
    ccin = ap(nc.dram_tensor("ccin", [jpc, 128], F32, kind="Internal"))
    ccout = ap(nc.dram_tensor("ccout", [jpc * n_cores, 128], F32, kind="Internal"))

    outT = ap(nc.dram_tensor("outT", [D, core_t], F32, kind="ExternalOutput"))

    sched = {k: [] for k in ("sync", "tensor", "vector", "scalar", "gpsimd")}

    def em(eng, fn):
        sched[eng].append(fn)

    def WAIT(eng, ct, v):
        if v > 0:
            em(eng, lambda e, h=ct.h, v=v: e.wait_ge(h, v))

    def WAITV(eng, tv):
        WAIT(eng, tv[0], tv[1])

    def DMA(eng, ct, out, in_, **kw):
        ct.n += 16
        em(eng, lambda e, o=out, i=in_, h=ct.h, kw=dict(kw): e.dma_start(out=o, in_=i, **kw).then_inc(h, 16))
        return ct.n

    def TDMA(ct, out, in_):
        ct.n += 16
        em("scalar", lambda e, o=out, i=in_, h=ct.h: e.dma_start(out=o, in_=i, transpose=True).then_inc(h, 16))
        return ct.n

    ENG_CT = {}

    def OP(eng, fn):
        ct = ENG_CT[eng]
        if eng != "tensor" and ct.n > 0:
            em(eng, lambda e, h=ct.h, v=ct.n: e.wait_ge(h, v))
        ct.n += 1
        em(eng, lambda e, fn=fn, h=ct.h: fn(e).then_inc(h, 1))
        return ct.n

    def MM(fn):
        em("tensor", lambda e, fn=fn: fn(e))

    with ExitStack() as _st:
        ent = _st.enter_context
        xsb = ent(nc.sbuf_tensor("xsb", [128, ND * TT], F32))
        hsb = ent(nc.sbuf_tensor("hsb", [128, ND * TT], BF16))
        aosb = ent(nc.sbuf_tensor("aosb", [128, ND * TT], BF16))
        kres = ent(nc.sbuf_tensor("kres", [128, ext], BF16))
        vtm = ent(nc.sbuf_tensor("vtm", [128, ext], BF16))
        cossb = ent(nc.sbuf_tensor("cossb", [128, TT], BF16))
        sinsb = ent(nc.sbuf_tensor("sinsb", [128, TT], BF16))
        qslot = ent(nc.sbuf_tensor("qslot", [128, 1024], BF16))
        sig = ent(nc.sbuf_tensor("sig", [128, 1536], BF16))
        psiT = ent(nc.sbuf_tensor("psiT", [128, 1536], BF16, is_transpose_dest=True))
        scrA = ent(nc.sbuf_tensor("scrA", [128, 512], F32))
        scrB = ent(nc.sbuf_tensor("scrB", [128, 512], F32))
        scrC = ent(nc.sbuf_tensor("scrC", [128, 512], F32))
        scrD = ent(nc.sbuf_tensor("scrD", [128, 512], F32))
        mrow = ent(nc.sbuf_tensor("mrow", [1, core_t], BF16))
        rrow = ent(nc.sbuf_tensor("rrow", [1, core_t], BF16))
        rowsb = ent(nc.sbuf_tensor("rowsb", [1, 1024], BF16))
        ostg = ent(nc.sbuf_tensor("ostg", [128, 512], F32))
        wslA = ent(nc.sbuf_tensor("wslA", [128, 2048], BF16))
        wslB = ent(nc.sbuf_tensor("wslB", [128, 2048], BF16))
        pp0 = ent(nc.sbuf_tensor("pp0", [128, 1024], BF16))
        pp1 = ent(nc.sbuf_tensor("pp1", [128, 1024], BF16))
        wslC = ent(nc.sbuf_tensor("wslC", [128, 2048], BF16))
        wslD = ent(nc.sbuf_tensor("wslD", [128, 2048], BF16))
        icsb = ent(nc.sbuf_tensor("icsb", [8, D], BF16))
        ocsb = ent(nc.sbuf_tensor("ocsb", [8, D], BF16))
        gwb = ent(nc.sbuf_tensor("gwb", [128, 256], BF16))
        idfs = ent(nc.sbuf_tensor("idfs", [128, 128], F32))
        idbs = ent(nc.sbuf_tensor("idbs", [128, 128], BF16))
        onesb = ent(nc.sbuf_tensor("onesb", [1, 128], BF16))
        onesc = ent(nc.sbuf_tensor("onesc", [128, 1], F32))
        aAB = ent(nc.sbuf_tensor("aAB", [128, 2 * nqc], F32))
        m3s = ent(nc.sbuf_tensor("m3s", [128, nqb], F32))
        vmk = ent(nc.sbuf_tensor("vmk", [128, nek], F32))
        biasC = ent(nc.sbuf_tensor("biasC", [128, 32], F32))
        scol32 = ent(nc.sbuf_tensor("scol32", [128, 32], F32))
        camod = ent(nc.sbuf_tensor("camod", [128, 224], F32))
        sums = ent(nc.sbuf_tensor("sums", [128, 16], F32))
        coefs = ent(nc.sbuf_tensor("coefs", [128, 8], F32))
        nmx = ent(nc.sbuf_tensor("nmx", [128, 4], F32))
        pQ0 = ent(nc.psum_tensor("pQ0", [128, 512], F32))
        pQ1 = ent(nc.psum_tensor("pQ1", [128, 512], F32))
        pS = ent(nc.psum_tensor("pS", [128, 1024], F32))
        pVo0 = ent(nc.psum_tensor("pVo0", [128, 512], F32))
        pVo1 = ent(nc.psum_tensor("pVo1", [128, 512], F32))
        pTb = ent(nc.psum_tensor("pTb", [128, 1024], BF16))
        pTb2 = ent(nc.psum_tensor("pTb2", [128, 1024], BF16))
        _sC = ent(nc.semaphore("smC"))
        _sX = ent(nc.semaphore("smX"))
        _sCS = ent(nc.semaphore("smCS"))
        _sSlA = ent(nc.semaphore("smSlA"))
        _sSlB = ent(nc.semaphore("smSlB"))
        _sGA = ent(nc.semaphore("smGA"))
        _sGB = ent(nc.semaphore("smGB"))
        _sScr = ent(nc.semaphore("smScr"))
        _sScrB = ent(nc.semaphore("smScrB"))
        _sCc = ent(nc.semaphore("smCc"))
        _sCo = ent(nc.semaphore("smCo"))
        _sOutB = ent(nc.semaphore("smOutB"))
        _sGc = ent(nc.semaphore("smGc"))
        _sPpA = ent(nc.semaphore("smPpA"))
        _sPpB = ent(nc.semaphore("smPpB"))
        _sGP = ent(nc.semaphore("smGP"))
        _sOgA = ent(nc.semaphore("smOgA"))
        _sOgB = ent(nc.semaphore("smOgB"))
        _sXg = ent(nc.semaphore("smXg"))
        _sCSg = ent(nc.semaphore("smCSg"))
        _sSlC = ent(nc.semaphore("smSlC"))
        _sSlD = ent(nc.semaphore("smSlD"))
        _sGq0 = ent(nc.semaphore("smGq0"))
        _sGq1 = ent(nc.semaphore("smGq1"))
        _sGq2 = ent(nc.semaphore("smGq2"))
        _sGq3 = ent(nc.semaphore("smGq3"))
        _sTpA = ent(nc.semaphore("smTpA"))
        _sTpB = ent(nc.semaphore("smTpB"))
        _sOut = ent(nc.semaphore("smOut"))
        _sCC = ent(nc.semaphore("smCC"))
        _sPE = ent(nc.semaphore("smPE"))
        _sACT = ent(nc.semaphore("smACT"))
        _sDVE = ent(nc.semaphore("smDVE"))
        block = ent(nc.Block())
        sC, sX, sCS = Ct(_sC), Ct(_sX), Ct(_sCS)
        sSl = [Ct(_sSlA), Ct(_sSlB)]
        sG = [Ct(_sGA), Ct(_sGB)]
        sScr = [Ct(_sScr), Ct(_sScrB)]
        sCc, sCo, sGc = Ct(_sCc), Ct(_sCo), Ct(_sGc)
        sPp = [Ct(_sPpA), Ct(_sPpB)]
        sGP = Ct(_sGP)
        sOg = [Ct(_sOgA), Ct(_sOgB)]
        sXg, sCSg = Ct(_sXg), Ct(_sCSg)
        sSl3 = [sSl[0], sSl[1], Ct(_sSlC), Ct(_sSlD)]
        sGq = [Ct(_sGq0), Ct(_sGq1), Ct(_sGq2), Ct(_sGq3)]
        sTp = [Ct(_sTpA), Ct(_sTpB)]
        sOut = [Ct(_sOut), Ct(_sOutB)]
        sCC = Ct(_sCC)
        sPE, sACT, sDVE = Ct(_sPE), Ct(_sACT), Ct(_sDVE)
        ENG_CT.update({"tensor": sPE, "vector": sDVE, "scalar": sACT, "gpsimd": sGP})

        CA1, SAv, GAC, CB1, SBv, GBC, GBIAS = (camod[:, 32 * i:32 * i + 32] for i in range(7))

        modpart = scrA[:, 384:408]
        routb = sig[:, 0:8]
        routT = psiT[0:8, 0:512]
        modcol = scrB[:, 0:192]
        routf = coefs[:, 0:8]

        scrUse = {"a": 0, "v": 0, "t": 0}

        def mark_scratch():
            scrUse["a"], scrUse["v"], scrUse["t"] = sACT.n, sDVE.n, sPE.n

        def wait_scratch(eng):
            if eng != "scalar":
                WAIT(eng, sACT, scrUse["a"])
            if eng != "vector":
                WAIT(eng, sDVE, scrUse["v"])
            if eng != "tensor":
                WAIT(eng, sPE, scrUse["t"])

        DMA("sync", sC, idfs[:], idf)
        DMA("sync", sC, idbs[:], idbI)
        DMA("sync", sC, onesb[:], onesbI)
        DMA("sync", sC, onesc[:], onescI)
        DMA("sync", sC, aAB[:], aABm)
        DMA("sync", sC, m3s[:], m3m)
        DMA("sync", sC, vmk[:], vmkm)
        DMA("sync", sC, biasC[:], biasM)
        DMA("sync", sC, scrB[0:32, 0:128], cvec.rearrange("o (a b) -> (o a) b", b=128))
        vC = sC.n
        DMA("gpsimd", sGc, gwb[:].rearrange("p (b e) -> p b e", e=8),
            gwT.rearrange("(b p) e -> p b e", p=128))
        DMA("gpsimd", sGc, icsb[:], icsM)
        DMA("gpsimd", sGc, ocsb[:], ocsM)
        DMA("gpsimd", sGc, aosb[:, 0:4096].rearrange("p (b f) -> p b f", f=128),
            wkT.rearrange("(b p) f -> p b f", p=128))
        DMA("gpsimd", sGc, aosb[:, 4096:8192].rearrange("p (b f) -> p b f", f=128),
            wvT.rearrange("(b p) f -> p b f", p=128))
        vGsmall = sGc.n
        DMA("sync", sX, xsb[:, 0:8192].rearrange("p (b t) -> p b t", t=TT),
            xT[0:2048, 0:TT].rearrange("(b p) t -> p b t", p=128))
        DMA("sync", sX, xsb[:, 14336:16384].rearrange("p (b t) -> p b t", t=TT),
            xT[3584:4096, 0:TT].rearrange("(b p) t -> p b t", p=128))
        DMA("sync", sCS, cossb[:], cosE[:, 0:TT])
        DMA("sync", sCS, sinsb[:], sinE[:, 0:TT])
        vCSl0 = sCS.n

        WAIT("scalar", sC, vC)
        vSilu = OP("scalar", lambda e: e.activation(out=scrB[0:32, 128:256], in_=scrB[0:32, 0:128], func=AF.Silu))
        WAIT("tensor", sACT, vSilu)
        WAIT("tensor", sC, vC)
        vTs = OP("tensor", lambda e: e.transpose(pQ1[:, 0:32], scrB[0:32, 128:256], idfs[0:32, 0:32]))
        WAIT("vector", sPE, vTs)
        vScol = OP("vector", lambda e: e.tensor_copy(out=scol32[:], in_=pQ1[:, 0:32]))

        mod_evac = 0
        peW = {0: 0, 1: 0}
        for jb in range(n_jb):
            j0 = jb * 24
            jn = min(24, jpc - j0)
            for i in range(ND):
                sl = i % 2
                xbuf = xsb[:, 8192 + 3072 * sl:8192 + 3072 * sl + 3072]
                WAIT("gpsimd", sPE, peW[sl])
                DMA("gpsimd", sG[sl], xbuf[:, 0:128 * jn],
                    wmodS[128 * i:128 * i + 128, 128 * j0:128 * (j0 + jn)])
                if jb == 0 and i == 0:
                    WAIT("tensor", sDVE, vScol)
                WAIT("tensor", sG[sl], sG[sl].n)
                for j in range(jn):
                    fn = (lambda e, xbuf=xbuf, j=j, i=i:
                          e.matmul(pS[:, (32 * j + i):(32 * j + i + 1)],
                                   xbuf[:, 128 * j:128 * j + 128],
                                   scol32[:, i:i + 1], start=True, stop=True))
                    if j == jn - 1:
                        vmm = OP("tensor", fn)
                    else:
                        MM(fn)
                peW[sl] = vmm
            for j in range(jn):
                if j == 0:
                    WAIT("vector", sPE, vmm)
                vred = OP("vector", lambda e, j=j: e.tensor_reduce(
                    out=modpart[:, j:j + 1], in_=pS[:, 32 * j:32 * j + 32], axis=AX.X, op=ALU.add))
            WAIT("tensor", sDVE, vred)
            vTp = OP("tensor", lambda e, jn=jn: e.transpose(pQ0[0:jn, 0:128], modpart[:, 0:jn], idfs[:]))
            WAIT("vector", sPE, vTp)
            WAIT("vector", sCc, mod_evac)
            vEv = OP("vector", lambda e, jn=jn: e.tensor_copy(out=scrA[0:jn, 256:384], in_=pQ0[0:jn, 0:128]))
            WAIT("sync", sDVE, vEv)
            DMA("sync", sCc, ccin[j0:j0 + jn, :], scrA[0:jn, 256:384])
            mod_evac = sCc.n

        WAIT("sync", sPE, vmm)
        DMA("sync", sX, xsb[:, 8192:14336].rearrange("p (b t) -> p b t", t=TT),
            xT[2048:3584, 0:TT].rearrange("(b p) t -> p b t", p=128))
        vXl0 = sX.n
        WAIT("gpsimd", sCc, mod_evac)
        em("gpsimd", lambda e: e.collective_compute(
            "AllGather", ALU.bypass, replica_groups=[list(range(n_cores))],
            ins=[ccin], outs=[ccout]).then_inc(sCC.h, 1))
        sCC.n += 1
        WAIT("sync", sCC, sCC.n)
        R = jpc * n_cores
        g0, tI = 0, 0
        vT2 = 0
        while g0 < R:
            gn = min(96, R - g0)
            WAIT("sync", sPE, vT2)
            DMA("sync", sCo, scrA[0:gn, 0:128], ccout[g0:g0 + gn, :])
            WAIT("tensor", sCo, sCo.n)
            vT2 = OP("tensor", lambda e, gn=gn, tI=tI: e.transpose(
                pQ1[0:128, 128 * tI:128 * tI + gn], scrA[0:gn, 0:128], idfs[0:gn, 0:gn]))
            WAIT("vector", sPE, vT2)
            vMc = OP("vector", lambda e, gn=gn, g0=g0, tI=tI: e.tensor_copy(
                out=modcol[:, g0:g0 + gn], in_=pQ1[:, 128 * tI:128 * tI + gn]))
            g0 += gn
            tI += 1
        OP("vector", lambda e: e.tensor_copy(out=SAv, in_=modcol[:, 0:32]))
        OP("vector", lambda e: e.tensor_scalar(out=CA1, in0=modcol[:, 32:64], scalar1=1.0, scalar2=None, op0=ALU.add))
        OP("vector", lambda e: e.tensor_copy(out=SBv, in_=modcol[:, 96:128]))
        OP("vector", lambda e: e.tensor_scalar(out=CB1, in0=modcol[:, 128:160], scalar1=1.0, scalar2=None, op0=ALU.add))
        for (ga_s, dst) in ((64, GAC), (160, GBC)):
            g = modcol[:, ga_s:ga_s + 32]
            OP("vector", lambda e, g=g: e.tensor_scalar(out=scrC[:, 0:32], in0=g, scalar1=0.0, scalar2=None, op0=ALU.is_ge))
            OP("vector", lambda e, g=g: e.tensor_scalar(out=scrC[:, 32:64], in0=g, scalar1=0.25, scalar2=None, op0=ALU.max))
            OP("vector", lambda e, g=g: e.tensor_scalar(out=scrC[:, 64:96], in0=g, scalar1=-0.25, scalar2=None, op0=ALU.min))
            OP("vector", lambda e: e.tensor_sub(scrC[:, 96:128], scrC[:, 32:64], scrC[:, 64:96]))
            OP("vector", lambda e: e.tensor_mul(scrC[:, 96:128], scrC[:, 96:128], scrC[:, 0:32]))
            OP("vector", lambda e, dst=dst: e.tensor_add(dst, scrC[:, 96:128], scrC[:, 64:96]))
        WAIT("vector", sC, vC)
        vMod = OP("vector", lambda e: e.tensor_mul(GBIAS, GBC, biasC[:]))
        mark_scratch()

        pp = [pp0, pp1]
        prep_state = {"prevWr": {0: 0, 1: 0}, "qidx": 0}
        prep_items = []
        scr_done = {}

        def mk_prep_item(wsrc, scr, do_sign, i, q, last):
            def emit():
                sl = prep_state["qidx"] % 2
                prep_state["qidx"] += 1
                WAIT("gpsimd", sScr[sl], prep_state["prevWr"][sl])
                DMA("gpsimd", sPp[sl], pp[sl][:], wsrc[128 * i:128 * i + 128, 1024 * q:1024 * q + 1024])
                WAIT("gpsimd", sPp[sl], sPp[sl].n)
                DMA("gpsimd", sScr[sl], scr[8 * q:8 * q + 8, :, i, :].rearrange("c p f -> p c f"),
                    pp[sl][:].rearrange("p (c f) -> p c f", f=128))
                prep_state["prevWr"][sl] = sScr[sl].n
                if last:
                    scr_done[id(scr)] = (sScr[0].n, sScr[1].n)
            return emit

        def mat_items(wsrc, scr, do_sign):
            return [mk_prep_item(wsrc, scr, do_sign, i, q, i == ND - 1 and q == 3)
                    for i in range(ND) for q in range(4)]

        prep_items.extend(mat_items(wfT, scrf, False))

        def weave(n):
            for _ in range(min(n, len(prep_items))):
                prep_items.pop(0)()

        def scr_gate(scr):
            v = scr_done.get(id(scr))
            if v is None:
                while prep_items and id(scr) not in scr_done:
                    prep_items.pop(0)()
                v = scr_done[id(scr)]
            return v

        def ln_stats(src_f32, mdst, rdst, extra_rows=None):
            wait_scratch("vector")
            wait_scratch("scalar")
            v = OP("vector", lambda e: e.tensor_copy(out=scrA[:], in_=src_f32[:, 0:512]))
            aq = OP("scalar", lambda e: e.activation(out=scrB[:], in_=src_f32[:, 0:512], func=AF.Square))
            vadd2 = 0
            for cix in range(1, ND):
                ch = src_f32[:, 512 * cix:512 * cix + 512]
                OP("vector", lambda e, ch=ch: e.tensor_add(scrA[:], scrA[:], ch))
                if vadd2:
                    WAIT("scalar", sDVE, vadd2)
                aq = OP("scalar", lambda e, ch=ch: e.activation(out=scrC[:], in_=ch, func=AF.Square))
                WAIT("vector", sACT, aq)
                vadd2 = OP("vector", lambda e: e.tensor_add(scrB[:], scrB[:], scrC[:]))
            WAIT("tensor", sDVE, vadd2)
            return ln_tail(mdst, rdst, extra_rows=extra_rows)

        def ln_tail(mdst, rdst, vdep=0, extra_rows=None):
            wait_scratch("tensor")
            WAIT("tensor", sDVE, vdep)
            OP("tensor", lambda e: e.matmul(pVo1[0:1, 0:512], onesc[:], scrA[:], start=True, stop=True))
            vmm = OP("tensor", lambda e: e.matmul(pVo0[0:1, 0:512], onesc[:], scrB[:], start=True, stop=True))
            WAIT("vector", sPE, vmm)
            OP("vector", lambda e: e.tensor_scalar(out=scrC[0:1, 0:512], in0=pVo1[0:1, 0:512],
                                                   scalar1=1.0 / D, scalar2=None, op0=ALU.mult))
            OP("vector", lambda e: e.tensor_scalar(out=scrB[0:1, 0:512], in0=pVo0[0:1, 0:512],
                                                   scalar1=1.0 / D, scalar2=None, op0=ALU.mult))
            OP("vector", lambda e: e.tensor_mul(scrA[0:1, 0:512], scrC[0:1, 0:512], scrC[0:1, 0:512]))
            v = OP("vector", lambda e: e.scalar_tensor_tensor(
                out=scrB[0:1, 0:512], in0=scrB[0:1, 0:512], scalar=EPS, in1=scrA[0:1, 0:512],
                op0=ALU.add, op1=ALU.subtract))
            WAIT("scalar", sDVE, v)
            aS = OP("scalar", lambda e: e.activation(out=scrA[0:1, 0:512], in_=scrB[0:1, 0:512], func=AF.Sqrt))
            WAIT("vector", sACT, aS)
            OP("vector", lambda e: e.reciprocal(out=scrB[0:1, 0:512], in_=scrA[0:1, 0:512]))
            OP("vector", lambda e, mdst=mdst: e.tensor_copy(out=mdst, in_=scrC[0:1, 0:512]))
            v = OP("vector", lambda e, rdst=rdst: e.tensor_copy(out=rdst, in_=scrB[0:1, 0:512]))
            if extra_rows is not None:
                lo, hi, own0 = extra_rows
                OP("vector", lambda e, lo=lo, hi=hi, own0=own0: e.tensor_copy(
                    out=mrow[0:1, own0:own0 + hi - lo], in_=scrC[0:1, lo:hi]))
                v = OP("vector", lambda e, lo=lo, hi=hi, own0=own0: e.tensor_copy(
                    out=rrow[0:1, own0:own0 + hi - lo], in_=scrB[0:1, lo:hi]))
            return v

        def ln_bcast(mr, rr, vrows=0):
            wait_scratch("tensor")
            WAIT("tensor", sDVE, vrows)
            OP("tensor", lambda e, mr=mr: e.matmul(pS[:, 0:512], onesb[:], mr, start=True, stop=True))
            return OP("tensor", lambda e, rr=rr: e.matmul(pS[:, 512:1024], onesb[:], rr, start=True, stop=True))

        def ln_apply(src_f32, dst_bf, c1col, shcol, vdep_pe):
            lastAs = []
            for cix in range(ND):
                ch = src_f32[:, 512 * cix:512 * cix + 512]
                dch = dst_bf[:, 512 * cix:512 * cix + 512]
                pa, pb = (scrA, scrB) if cix % 2 == 0 else (scrC, scrD)
                if cix == 0:
                    WAIT("vector", sPE, vdep_pe)
                    wait_scratch("vector")
                if cix >= 2:
                    WAIT("vector", sACT, lastAs[cix - 2])
                OP("vector", lambda e, ch=ch, pa=pa: e.tensor_sub(pa[:], ch, pS[:, 0:512]))
                vv = OP("vector", lambda e, pa=pa, pb=pb: e.tensor_mul(pb[:], pa[:], pS[:, 512:1024]))
                WAIT("scalar", sDVE, vv)
                lastAs.append(OP("scalar", lambda e, dch=dch, cix=cix, pb=pb: e.activation(
                    out=dch, in_=pb[:], func=AF.Identity,
                    bias=shcol[:, cix:cix + 1], scale=c1col[:, cix:cix + 1])))
            return lastAs

        vFree = [0, 0, 0, 0]
        lastVevac = 0
        lastHprev = 0
        for et in range(n_ext):
            t0 = et * TT
            if et > 0:
                WAIT("sync", sACT, lastHprev)
                DMA("sync", sX, xsb[:].rearrange("p (b t) -> p b t", t=TT),
                    xT[:, t0:t0 + TT].rearrange("(b p) t -> p b t", p=128))
                vXl = sX.n
                WAIT("sync", sDVE, sDVE.n)
                DMA("sync", sCS, cossb[:], cosE[:, t0:t0 + TT])
                DMA("sync", sCS, sinsb[:], sinE[:, t0:t0 + TT])
                vCSl = sCS.n
            else:
                vXl = vXl0
                vCSl = vCSl0
            WAIT("vector", sX, vXl)
            WAIT("scalar", sX, vXl)
            own_lo = max(0, 256 - t0)
            own_hi = min(512, 256 + core_t - t0)
            vRows = ln_stats(xsb, rowsb[0:1, 0:512], rowsb[0:1, 512:1024],
                             extra_rows=(own_lo, own_hi, t0 - 256 + own_lo))
            vB = ln_bcast(rowsb[0:1, 0:512], rowsb[0:1, 512:1024], vRows)
            lastHs = ln_apply(xsb, hsb, CA1, SAv, vB)
            lastH = lastHs[-1]
            lastHprev = lastH
            vGrp = {}
            if et == 0:
                WAIT("tensor", sGc, vGsmall)
            for which, wbase, pdst in (("k", 0, pQ0), ("v", 4096, pQ1)):
                waited = 0
                for i in range(ND):
                    if lastHs[i] > waited:
                        waited = lastHs[i]
                        WAIT("tensor", sACT, waited)
                    fn = (lambda e, wbase=wbase, i=i, pdst=pdst: e.matmul(
                        pdst[:], aosb[:, wbase + 128 * i:wbase + 128 * i + 128],
                        hsb[:, 512 * i:512 * i + 512], start=(i == 0), stop=(i == ND - 1)))
                    if i == ND - 1:
                        vg = OP("tensor", fn)
                    else:
                        MM(fn)
                vGrp[which] = vg
            WAIT("vector", sPE, vGrp["k"])
            WAIT("vector", sCS, vCSl)
            OP("vector", lambda e: e.tensor_mul(scrA[:], pQ0[:], cossb[:]))
            OP("vector", lambda e: e.tensor_mul(scrB[0:64, :], pQ0[64:128, :], sinsb[0:64, :]))
            OP("vector", lambda e: e.tensor_mul(scrB[64:128, :], pQ0[0:64, :], sinsb[64:128, :]))
            OP("vector", lambda e, t0=t0: e.tensor_sub(kres[0:64, t0:t0 + TT], scrA[0:64, :], scrB[0:64, :]))
            OP("vector", lambda e, t0=t0: e.tensor_add(kres[64:128, t0:t0 + TT], scrA[64:128, :], scrB[64:128, :]))
            WAIT("scalar", sPE, vGrp["v"])
            vVc = OP("scalar", lambda e: e.activation(out=sig[:, 0:512], in_=pQ1[:], func=AF.Copy))
            WAIT("tensor", sACT, vVc)
            WAIT("tensor", sDVE, lastVevac)
            for j in range(4):
                vTt = OP("tensor", lambda e, j=j: e.transpose(
                    pTb[:, 128 * j:128 * j + 128], sig[:, 128 * j:128 * j + 128], idbs[:]))
            WAIT("vector", sPE, vTt)
            for j in range(4):
                kb = 4 * et + j
                lastVevac = OP("vector", lambda e, j=j, kb=kb: e.tensor_scalar(
                    out=vtm[:, 128 * kb:128 * kb + 128], in0=pTb[:, 128 * j:128 * j + 128],
                    scalar1=vmk[:, kb:kb + 1], scalar2=None, op0=ALU.mult))
            mark_scratch()
            weave(26)

        vSigACT = {}
        vPsi = {}
        vTpPE = {}
        vTbEvac = {}
        vVout = {}
        vEvac = {}
        uG = 0
        uTile = {}

        def emit_trans(u):
            slot = u % 2
            pT = pTb if slot == 0 else pTb2
            WAIT("tensor", sDVE, vPsi[u])
            if u >= 2:
                WAITV("tensor", vTbEvac[u - 2])
            for cx in range(6):
                vTpPE[u] = OP("tensor", lambda e, pT=pT, cx=cx, slot=slot: e.transpose(
                    pT[:, 128 * cx:128 * cx + 128],
                    sig[:, 768 * slot + 128 * cx:768 * slot + 128 * cx + 128], idbs[:]))

        def emit_tbevac(u):
            slot = u % 2
            pT = pTb if slot == 0 else pTb2
            eng = "vector" if slot == 0 else "scalar"
            ct = sDVE if slot == 0 else sACT
            WAIT(eng, sPE, vTpPE[u])
            if u >= 2:
                WAIT(eng, sPE, vVout[u - 2])
            if slot == 0:
                v = OP(eng, lambda e, pT=pT, slot=slot: e.tensor_copy(
                    out=psiT[:, 768 * slot:768 * slot + 768], in_=pT[:, 0:768]))
            else:
                v = OP(eng, lambda e, pT=pT, slot=slot: e.activation(
                    out=psiT[:, 768 * slot:768 * slot + 768], in_=pT[:, 0:768], func=AF.Copy))
            vTbEvac[u] = (ct, v)

        def emit_V(u):
            h_ = (u % 128) // 4
            qc_ = u % 4
            tl = uTile[u]
            lqb = 2 * tl + qc_ // 2
            slot = u % 2
            WAITV("tensor", vTbEvac[u])
            if u >= 2:
                WAITV("tensor", vEvac[u - 2])
            pV = pVo0 if slot == 0 else pVo1
            for cx in range(6):
                kb = 2 * lqb + cx
                fn = (lambda e, pV=pV, cx=cx, kb=kb, slot=slot: e.matmul(
                    pV[:, 0:128], vtm[:, 128 * kb:128 * kb + 128],
                    psiT[:, 768 * slot + 128 * cx:768 * slot + 128 * cx + 128],
                    start=(cx == 0), stop=(cx == 5)))
                if cx == 5:
                    vVout[u] = OP("tensor", fn)
                else:
                    MM(fn)
            WAIT("scalar", sPE, vVout[u])
            vEvac[u] = (sACT, OP("scalar", lambda e, h_=h_, qc_=qc_, pV=pV: e.activation(
                out=aosb[:, 512 * h_ + 128 * qc_:512 * h_ + 128 * qc_ + 128],
                in_=pV[:, 0:128], func=AF.Copy)))

        WSL3 = [wslA, wslB, wslC, wslD]
        mmg_state = {"hidx": 0}

        def mm_group(wsrc, c, rhs_sb, pdst, extra_pre=None, rhs_ready=None, sign=False,
                     direct=False):
            if extra_pre:
                extra_pre()
            vh = []
            slots = []
            vsg = [0, 0]
            for half in range(2):
                sl = mmg_state["hidx"] % 4
                mmg_state["hidx"] += 1
                slots.append(sl)
                wdst = WSL3[sl][:].rearrange("p (b f) -> p b f", f=128)
                if direct:
                    WAIT("gpsimd", sPE, vFree[sl])
                    DMA("gpsimd", sGq[sl], wdst,
                        wsrc[2048 * half:2048 * half + 2048, 128 * c:128 * c + 128]
                        .rearrange("(b p) f -> p b f", p=128))
                    vh.append((sGq[sl], sGq[sl].n))
                else:
                    WAIT("sync", sPE, vFree[sl])
                    DMA("sync", sSl3[sl], wdst, wsrc[c, :, 16 * half:16 * half + 16, :])
                    vh.append((sSl3[sl], sSl3[sl].n))
                if sign:
                    WAITV("vector", vh[half])
                    WAIT("vector", sPE, vFree[sl])
                    OP("vector", lambda e, sl=sl: e.tensor_scalar(
                        out=WSL3[sl][:], in0=WSL3[sl][:], scalar1=1e38, scalar2=None, op0=ALU.mult))
                    vsg[half] = OP("vector", lambda e, sl=sl: e.tensor_scalar(
                        out=WSL3[sl][:], in0=WSL3[sl][:], scalar1=-1.0, scalar2=1.0, op0=ALU.max, op1=ALU.min))
            out = None
            waited = 0
            for half in range(2):
                sl = slots[half]
                wsl = WSL3[sl]
                WAITV("tensor", vh[half])
                if sign:
                    WAIT("tensor", sDVE, vsg[half])
                for b in range(16):
                    i = 16 * half + b
                    if rhs_ready is not None and rhs_ready[i] > waited:
                        waited = rhs_ready[i]
                        WAIT("tensor", sACT, waited)
                    fn = (lambda e, wsl=wsl, b=b, i=i, pdst=pdst: e.matmul(
                        pdst[:], wsl[:, 128 * b:128 * b + 128],
                        rhs_sb[:, 512 * i:512 * i + 512], start=(i == 0), stop=(i == ND - 1)))
                    if i == 15 or i == ND - 1:
                        out = OP("tensor", fn)
                        vFree[sl] = out
                    else:
                        MM(fn)
            return out

        vXl_pref = None
        vCSl_pref = None
        vStSlot = {0: 0, 1: 0}
        for tile in range(n_tiles):
            t0 = tile * TT
            e0 = t0 + 256
            if vXl_pref is None:
                WAIT("sync", sACT, lastHprev)
                DMA("sync", sX, xsb[:].rearrange("p (b t) -> p b t", t=TT),
                    xT[:, e0:e0 + TT].rearrange("(b p) t -> p b t", p=128))
                vXl = (sX, sX.n)
                WAIT("sync", sDVE, sDVE.n)
                DMA("sync", sCS, cossb[:], cosE[:, e0:e0 + TT])
                DMA("sync", sCS, sinsb[:], sinE[:, e0:e0 + TT])
                vCSl = (sCS, sCS.n)
            else:
                vXl, vCSl = vXl_pref, vCSl_pref
            WAITV("vector", vXl)
            WAITV("scalar", vXl)
            vB = ln_bcast(mrow[0:1, t0:t0 + 512], rrow[0:1, t0:t0 + 512])
            lastHs = ln_apply(xsb, hsb, CA1, SAv, vB)
            lastH = lastHs[-1]

            vRope = {}
            vLastScores = sPE.n
            vQmm = {}
            qsub_state = {}

            def q_load(h):
                if tile == 0:
                    weave(4)
                vh, slots = [], []
                for half in range(2):
                    sl = mmg_state["hidx"] % 4
                    mmg_state["hidx"] += 1
                    slots.append(sl)
                    WAIT("gpsimd", sPE, vFree[sl])
                    DMA("gpsimd", sGq[sl], WSL3[sl][:].rearrange("p (b f) -> p b f", f=128),
                        wqT[2048 * half:2048 * half + 2048, 128 * h:128 * h + 128]
                        .rearrange("(b p) f -> p b f", p=128))
                    vh.append((sGq[sl], sGq[sl].n))
                qsub_state[h] = (slots, vh)

            def q_sub(h, j, rhs_ready=None):
                slots, vh = qsub_state[h]
                half = j // 2
                sl = slots[half]
                wsl = WSL3[sl]
                if j == 0:
                    if h >= 2:
                        WAITV("tensor", vRope[h - 2])
                    WAIT("tensor", sACT, lastH)
                if j % 2 == 0:
                    WAITV("tensor", vh[half])
                out = None
                waited = 0
                for b in range(8):
                    i = 8 * j + b
                    if rhs_ready is not None and rhs_ready[i] > waited:
                        waited = rhs_ready[i]
                        WAIT("tensor", sACT, waited)
                    fn = (lambda e, wsl=wsl, i=i, h=h: e.matmul(
                        (pQ0 if h % 2 == 0 else pQ1)[:], wsl[:, 128 * (i % 16):128 * (i % 16) + 128],
                        hsb[:, 512 * i:512 * i + 512], start=(i == 0), stop=(i == ND - 1)))
                    if i in (15, ND - 1):
                        out = OP("tensor", fn)
                        vFree[sl] = out
                    else:
                        MM(fn)
                if j == 3:
                    vQmm[h] = out
                return out

            def emit_rope(h):
                qsl2 = h % 2
                WAIT("vector", sPE, vQmm[h])
                WAITV("vector", vCSl)
                if h >= 2:
                    WAIT("vector", sPE, vLastScores)
                pQh = pQ0 if qsl2 == 0 else pQ1
                qs = qslot[:, 512 * qsl2:512 * qsl2 + 512]
                OP("vector", lambda e, pQh=pQh: e.tensor_mul(scrA[:], pQh[:], cossb[:]))
                OP("vector", lambda e, pQh=pQh: e.tensor_mul(scrB[0:64, :], pQh[64:128, :], sinsb[0:64, :]))
                OP("vector", lambda e, pQh=pQh: e.tensor_mul(scrB[64:128, :], pQh[0:64, :], sinsb[64:128, :]))
                OP("vector", lambda e, qs=qs: e.tensor_sub(qs[0:64, :], scrA[0:64, :], scrB[0:64, :]))
                vRope[h] = (sDVE, OP("vector", lambda e, qs=qs: e.tensor_add(qs[64:128, :], scrA[64:128, :], scrB[64:128, :])))

            SUBLAG = 5
            q_load(0)
            for j4 in range(4):
                q_sub(0, j4, rhs_ready=lastHs)
            emit_rope(0)
            q_load(1)
            q_sub(1, 0)
            for h in range(H):
                qsl = h % 2
                for qc in range(4):
                    u = uG
                    uG += 1
                    uTile[u] = tile
                    lqc = 4 * tile + qc
                    lqb = 2 * tile + qc // 2
                    slot = u % 2
                    sm = sums[:, 8 * slot:8 * slot + 8]
                    cf = coefs[:, 4 * slot:4 * slot + 4]
                    qs = qslot[:, 512 * qsl:512 * qsl + 512]
                    s = 4 * h + qc + SUBLAG
                    if s < 4 * H:
                        h2, j2 = divmod(s, 4)
                        if j2 == 0 and h2 >= 2:
                            q_load(h2)
                        q_sub(h2, j2)
                        if j2 == 3:
                            emit_rope(h2)
                    WAITV("tensor", vRope[h])
                    if u >= 1:
                        WAIT("tensor", sACT, vSigACT[u - 1])
                    k0 = 256 * lqb
                    for (ks, kn) in ((0, 512), (512, 256)):
                        vSc = OP("tensor", lambda e, qs=qs, qc=qc, k0=k0, ks=ks, kn=kn: e.matmul(
                            pS[:, ks:ks + kn], qs[:, 128 * qc:128 * qc + 128],
                            kres[:, k0 + ks:k0 + ks + kn], start=True, stop=True))
                    vLastScores = vSc
                    WAIT("scalar", sPE, vSc)
                    if u >= 2:
                        WAIT("scalar", sPE, vTpPE[u - 2])
                    for j3 in range(3):
                        vSigACT[u] = OP("scalar", lambda e, slot=slot, j3=j3, sm=sm: e.activation(
                            out=sig[:, 768 * slot + 256 * j3:768 * slot + 256 * j3 + 256],
                            in_=pS[:, 256 * j3:256 * j3 + 256], func=AF.Sigmoid, scale=SCALE,
                            accum_out=sm[:, j3:j3 + 1]))
                    WAIT("vector", sACT, vSigACT[u])
                    OP("vector", lambda e, sm=sm: e.tensor_add(sm[:, 4:5], sm[:, 0:1], sm[:, 1:2]))
                    OP("vector", lambda e, sm=sm, lqb=lqb: e.scalar_tensor_tensor(
                        out=sm[:, 5:6], in0=sm[:, 2:3], scalar=m3s[:, lqb:lqb + 1],
                        in1=sm[:, 1:2], op0=ALU.mult, op1=ALU.add))
                    OP("vector", lambda e, sm=sm: e.reciprocal(out=sm[:, 6:8], in_=sm[:, 4:6]))
                    OP("vector", lambda e, sm=sm, cf=cf, lqc=lqc: e.tensor_mul(
                        cf[:, 0:2], sm[:, 6:8], aAB[:, 2 * lqc:2 * lqc + 2]))
                    OP("vector", lambda e, cf=cf: e.tensor_add(cf[:, 2:3], cf[:, 0:1], cf[:, 1:2]))
                    for (j3, cc) in ((0, 0), (1, 2), (2, 1)):
                        vPsi[u] = OP("vector", lambda e, slot=slot, j3=j3, cc=cc, cf=cf: e.tensor_scalar(
                            out=sig[:, 768 * slot + 256 * j3:768 * slot + 256 * j3 + 256],
                            in0=sig[:, 768 * slot + 256 * j3:768 * slot + 256 * j3 + 256],
                            scalar1=cf[:, cc:cc + 1], scalar2=None, op0=ALU.mult))
                    if u >= 2 and (u - 2) not in vVout:
                        emit_V(u - 2)
                    if u >= 1 and (u - 1) not in vTpPE:
                        emit_trans(u - 1)
                        emit_tbevac(u - 1)
            uL = uG - 1
            if uL not in vTpPE:
                emit_trans(uL)
                emit_tbevac(uL)
            if (uL - 1) not in vVout:
                emit_V(uL - 1)
            emit_V(uL)
            vAttnDone = vEvac[uL]
            mark_scratch()

            vRopeLast = vRope[H - 1][1] if vRope[H - 1][0] is sDVE else None
            vX2 = {}
            vp2 = 0
            for c in range(ND):
                def pre_wo(c=c):
                    if c >= 2:
                        WAIT("tensor", sDVE, vX2[c - 2])
                    if c == 0:
                        WAITV("tensor", vAttnDone)
                if tile == 0:
                    weave(3)
                vW = mm_group(woT, c, aosb, pQ0 if c % 2 == 0 else pQ1, extra_pre=pre_wo, direct=True)
                WAIT("vector", sPE, vW)
                pQc = pQ0 if c % 2 == 0 else pQ1
                vX2[c] = OP("vector", lambda e, c=c, pQc=pQc: e.scalar_tensor_tensor(
                    out=xsb[:, 512 * c:512 * c + 512], in0=pQc[:], scalar=GAC[:, c:c + 1],
                    in1=xsb[:, 512 * c:512 * c + 512], op0=ALU.mult, op1=ALU.add))
                ch = xsb[:, 512 * c:512 * c + 512]
                if c == 0:
                    wait_scratch("vector")
                    OP("vector", lambda e, ch=ch: e.tensor_copy(out=scrA[:], in_=ch))
                    WAIT("scalar", sDVE, vX2[0])
                    aq = OP("scalar", lambda e, ch=ch: e.activation(out=scrB[:], in_=ch, func=AF.Square))
                else:
                    OP("vector", lambda e, ch=ch: e.tensor_add(scrA[:], scrA[:], ch))
                    tmp = scrC if c % 2 == 0 else scrD
                    WAIT("scalar", sDVE, vp2 if vp2 else vX2[c])
                    WAIT("scalar", sDVE, vX2[c])
                    aq = OP("scalar", lambda e, ch=ch, tmp=tmp: e.activation(out=tmp[:], in_=ch, func=AF.Square))
                    WAIT("vector", sACT, aq)
                    vp2 = OP("vector", lambda e, tmp=tmp: e.tensor_add(scrB[:], scrB[:], tmp[:]))

            vTail = ln_tail(rowsb[0:1, 0:512], rowsb[0:1, 512:1024], vp2)
            vB2 = ln_bcast(rowsb[0:1, 0:512], rowsb[0:1, 512:1024], vTail)
            lastH2s = ln_apply(xsb, hsb, CB1, SBv, vB2)
            lastH2 = lastH2s[-1]

            WAIT("tensor", sACT, vOsc0 if tile > 0 else 0)
            vRt = 0
            vLgs = {}
            for tc in range(4):
                if tc == 0:
                    if tile == 0:
                        WAIT("tensor", sGc, vGsmall)
                    waited = 0
                else:
                    WAIT("tensor", sACT, vExp)
                for i in range(ND):
                    if tc == 0 and lastH2s[i] > waited:
                        waited = lastH2s[i]
                        WAIT("tensor", sACT, waited)
                    fn = (lambda e, tc=tc, i=i: e.matmul(
                        pVo0[:, 0:8], hsb[:, 512 * i + 128 * tc:512 * i + 128 * tc + 128],
                        gwb[:, 8 * i:8 * i + 8], start=(i == 0), stop=(i == ND - 1)))
                    if i == ND - 1:
                        vLg = OP("tensor", fn)
                    else:
                        MM(fn)
                WAIT("vector", sPE, vLg)
                OP("vector", lambda e: e.tensor_reduce(out=nmx[:, 0:1], in_=pVo0[:, 0:8], axis=AX.X, op=ALU.max))
                OP("vector", lambda e: e.tensor_scalar(out=nmx[:, 1:2], in0=nmx[:, 0:1],
                                                       scalar1=-1.0, scalar2=None, op0=ALU.mult))
                WAIT("scalar", sDVE, sDVE.n)
                vExp = OP("scalar", lambda e: e.activation(
                    out=routf[:], in_=pVo0[:, 0:8], func=AF.Exp, bias=nmx[:, 1:2],
                    accum_out=nmx[:, 2:3]))
                if tc == 3:
                    vOsc0 = vExp
                WAIT("vector", sACT, vExp)
                OP("vector", lambda e: e.reciprocal(out=nmx[:, 3:4], in_=nmx[:, 2:3]))
                vRb = OP("vector", lambda e: e.tensor_scalar(
                    out=routb[:], in0=routf[:], scalar1=nmx[:, 3:4], scalar2=None, op0=ALU.mult))
                WAIT("tensor", sDVE, vRb)
                if vRt:
                    WAIT("tensor", sDVE, vRt)
                vTr = OP("tensor", lambda e: e.transpose(pTb[0:8, 512:640], routb[:], idbs[:]))
                WAIT("vector", sPE, vTr)
                vRt = OP("vector", lambda e, tc=tc: e.tensor_copy(
                    out=routT[:, 128 * tc:128 * tc + 128], in_=pTb[0:8, 512:640]))

            vXs = {}
            for c in range(ND):
                psl = c % 2
                if c == 0:
                    WAIT("tensor", sDVE, vRt)
                if c >= 2:
                    WAIT("tensor", sDVE, vXs[c - 2])
                vIs = OP("tensor", lambda e, c=c, psl=psl: e.matmul(
                    pS[:, 512 * psl:512 * psl + 512], icsb[:, 128 * c:128 * c + 128],
                    routT[:], start=True, stop=True))
                WAIT("vector", sPE, vIs)
                vXs[c] = OP("vector", lambda e, c=c, psl=psl: e.tensor_mul(
                    aosb[:, 512 * c:512 * c + 512], hsb[:, 512 * c:512 * c + 512],
                    pS[:, 512 * psl:512 * psl + 512]))

            vEp = {}
            vSt = {}
            for c in range(ND):
                if tile == 0 and c == 0:
                    vWf = scr_gate(scrf)
                    WAIT("sync", sScr[0], vWf[0])
                    WAIT("sync", sScr[1], vWf[1])
                def pre_ff(c=c):
                    if c >= 2:
                        WAIT("tensor", sDVE, vEp[c - 2])
                    if c == 0:
                        WAIT("tensor", sDVE, vXs[ND - 1])
                psl = c % 2
                vF = mm_group(scrf, c, aosb, pQ0 if psl == 0 else pQ1, extra_pre=pre_ff, sign=True)
                if c >= 2:
                    WAIT("tensor", sACT, vOsc)
                vOs = OP("tensor", lambda e, c=c, psl=psl: e.matmul(
                    pS[:, 512 * psl:512 * psl + 512], ocsb[:, 128 * c:128 * c + 128],
                    routT[:], start=True, stop=True))
                WAIT("scalar", sPE, vOs)
                if c >= 2:
                    WAIT("scalar", sDVE, vEp[c - 2])
                oscr = scrC if psl == 0 else scrB
                vOsc = OP("scalar", lambda e, psl=psl, oscr=oscr: e.activation(
                    out=oscr[:], in_=pS[:, 512 * psl:512 * psl + 512], func=AF.Copy))
                WAIT("vector", sPE, vF)
                WAIT("vector", sACT, vOsc)
                WAIT("vector", sOg[0], vStSlot[0])
                OP("vector", lambda e, c=c, psl=psl, oscr=oscr: e.scalar_tensor_tensor(
                    out=scrA[:], in0=(pQ0 if psl == 0 else pQ1)[:], scalar=GBC[:, c:c + 1],
                    in1=oscr[:], op0=ALU.mult, op1=ALU.mult))
                vEp[c] = OP("vector", lambda e, c=c: e.scalar_tensor_tensor(
                    out=ostg[:, 0:512], in0=scrA[:], scalar=GBIAS[:, c:c + 1],
                    in1=xsb[:, 512 * c:512 * c + 512], op0=ALU.add, op1=ALU.add))
                WAIT("gpsimd", sDVE, vEp[c])
                DMA("gpsimd", sOg[0], outT[128 * c:128 * c + 128, t0:t0 + TT], ostg[:, 0:512])
                vSt[c] = sOg[0].n
                vStSlot[0] = sOg[0].n
                if tile < n_tiles - 1:
                    e0n = e0 + TT
                    if c == 0:
                        if vRopeLast is not None:
                            WAIT("gpsimd", sDVE, vRopeLast)
                        DMA("gpsimd", sCSg, cossb[:], cosE[:, e0n:e0n + TT])
                        DMA("gpsimd", sCSg, sinsb[:], sinE[:, e0n:e0n + TT])
                        vCSl_pref = (sCSg, sCSg.n)
                    DMA("gpsimd", sXg, xsb[:, 512 * c:512 * c + 512],
                        xT[128 * c:128 * c + 128, e0n:e0n + TT])
                    vXl_pref = (sXg, sXg.n)
            mark_scratch()

        WAIT("gpsimd", sOg[0], sOg[0].n)

        @block.sync
        def _(e):
            for fn in sched["sync"]:
                fn(e)

        @block.tensor
        def _(e):
            for fn in sched["tensor"]:
                fn(e)

        @block.vector
        def _(e):
            for fn in sched["vector"]:
                fn(e)

        @block.scalar
        def _(e):
            for fn in sched["scalar"]:
                fn(e)

        @block.gpsimd
        def _(e):
            for fn in sched["gpsimd"]:
                fn(e)

    return nc



def _rope_cache(S):
    inv_freq = (1.0 / 10000.0 ** (np.arange(0, DH, 2, dtype=np.float32) / DH)).astype(np.float32)
    t = np.arange(S, dtype=np.float32)
    f = np.outer(t, inv_freq)
    cos = np.zeros((S, DH), dtype=np.float32)
    sin = np.zeros((S, DH), dtype=np.float32)
    cos[:, 0::2] = np.cos(f)
    cos[:, 1::2] = np.cos(f)
    sin[:, 0::2] = np.sin(f)
    sin[:, 1::2] = np.sin(f)
    return cos, sin


def make_in_maps(inputs, n_cores, core_t):
    ext = core_t + 512
    nqc = core_t // 128
    nqb = core_t // 256
    nek = ext // 128
    jpc = 24576 // (128 * n_cores)
    S = n_cores * core_t

    f32 = np.float32
    x = np.asarray(inputs["L0_attn_x"], f32)[0]
    assert x.shape == (S, D), x.shape
    xTfull = np.zeros((D, S + 512), f32)
    xTfull[:, 256:256 + S] = np.ascontiguousarray(x.T)
    wq = np.ascontiguousarray(np.asarray(inputs["L0_attn_Wq"], f32).T)
    wk = np.ascontiguousarray(np.asarray(inputs["L0_attn_Wk"], f32).T)
    wv = np.ascontiguousarray(np.asarray(inputs["L0_attn_Wv"], f32).T)
    wo = np.ascontiguousarray(np.asarray(inputs["L0_attn_Wo"], f32).T)
    wf = np.ascontiguousarray(np.asarray(inputs["L0_ffn_weight"], f32).T)
    wmod = np.asarray(inputs["L0_wmod"], f32)
    cvec = np.ascontiguousarray(np.asarray(inputs["L0_c"], f32).reshape(1, D))
    gw = np.ascontiguousarray(np.asarray(inputs["L0_ffn_gate_w"], f32).T)
    ics = np.ascontiguousarray(np.asarray(inputs["L0_ffn_in_channel_scale"], f32))
    ocs = np.ascontiguousarray(np.asarray(inputs["L0_ffn_out_channel_scale"], f32))
    biasM = np.ascontiguousarray(np.asarray(inputs["L0_ffn_bias"], f32).reshape(32, 128).T)

    cos, sin = _rope_cache(S)
    idfA = np.eye(128, dtype=f32)
    idbA = np.eye(128, dtype=np.float32).astype(ml_dtypes.bfloat16)
    onesbA = np.ones((1, 128), dtype=ml_dtypes.bfloat16)
    onescA = np.ones((128, 1), dtype=f32)
    alpha = np.linspace(0.0, 1.0, 256, dtype=f32)

    maps = []
    for r in range(n_cores):
        r0 = r * core_t
        pos = np.arange(r0 - 256, r0 + core_t + 256)
        posc = np.clip(pos, 0, S - 1)
        cosEa = np.ascontiguousarray(cos[posc].T)
        sinEa = np.ascontiguousarray(sin[posc].T)
        aABa = np.zeros((128, 2 * nqc), f32)
        for qc in range(nqc):
            gqb = (r0 // 256) + qc // 2
            half = qc % 2
            asl = alpha[128 * half:128 * half + 128]
            if gqb == 0:
                aABa[:, 2 * qc] = 0.0
                aABa[:, 2 * qc + 1] = 1.0
            else:
                aABa[:, 2 * qc] = 1.0 - asl
                aABa[:, 2 * qc + 1] = asl
        m3a = np.ones((128, nqb), f32)
        for qb in range(nqb):
            if (r0 // 256) + qb == S // 256 - 1:
                m3a[:, qb] = 0.0
        vma = np.ones((128, nek), f32)
        for kb in range(nek):
            glob = r0 - 256 + 128 * kb
            if glob < 0 or glob >= S:
                vma[:, kb] = 0.0
        maps.append({
            "xT": np.ascontiguousarray(xTfull[:, r0:r0 + ext]),
            "wqT": wq, "woT": wo, "wfT": wf, "wkT": wk, "wvT": wv,
            "wmodS": np.ascontiguousarray(wmod[:, 128 * jpc * r:128 * jpc * (r + 1)]),
            "cvec": cvec, "gwT": gw, "icsM": ics, "ocsM": ocs, "biasM": biasM,
            "cosE": cosEa.astype(ml_dtypes.bfloat16), "sinE": sinEa.astype(ml_dtypes.bfloat16), "aABm": aABa, "m3m": m3a,
            "vmkm": vma, "idf": idfA, "onesbI": onesbA, "onescI": onescA, "idbI": idbA,
        })
    return maps


def kernel(**inputs):
    n_cores, core_t = 8, 2048
    maps = make_in_maps(inputs, n_cores, core_t)
    nc = build_nc(n_cores, core_t)
    res = run_bass_kernel_spmd(nc, maps, core_ids=list(range(n_cores)))
    full = np.concatenate([np.asarray(r["outT"]) for r in res.results], axis=1)
    return np.ascontiguousarray(full.T).reshape(1, n_cores * core_t, D).astype(np.float32)


# version 22
# speedup vs baseline: 1.3049x; 1.0705x over previous
import numpy as np
from contextlib import ExitStack
import ml_dtypes
import concourse.bass as bass
import concourse.mybir as mybir
from concourse.bass_utils import run_bass_kernel_spmd

F32 = mybir.dt.float32
BF16 = mybir.dt.bfloat16
AF = mybir.ActivationFunctionType
ALU = mybir.AluOpType
AX = mybir.AxisListType

D = 4096
ND = 32
DH = 128
H = 32
SCALE = float(1.0 / np.sqrt(128.0))
EPS = 1e-5
TT = 512


class Ct:
    __slots__ = ("h", "n")
    def __init__(self, h):
        self.h = h
        self.n = 0


def build_nc(n_cores, core_t):
    ext = core_t + 512
    n_tiles = core_t // TT
    n_ext = ext // TT
    nqc = core_t // 128
    nqb = core_t // 256
    nek = ext // 128
    jpc = 24576 // (128 * n_cores)
    n_jb = (jpc + 23) // 24

    nc = bass.Bass("TRN2", target_bir_lowering=False, debug=False)

    ap = lambda t: t.ap()
    xT = ap(nc.dram_tensor("xT", [D, ext], F32, kind="ExternalInput"))
    wqT = ap(nc.dram_tensor("wqT", [D, D], F32, kind="ExternalInput"))
    woT = ap(nc.dram_tensor("woT", [D, D], F32, kind="ExternalInput"))
    wfT = ap(nc.dram_tensor("wfT", [D, D], F32, kind="ExternalInput"))
    wkT = ap(nc.dram_tensor("wkT", [D, DH], F32, kind="ExternalInput"))
    wvT = ap(nc.dram_tensor("wvT", [D, DH], F32, kind="ExternalInput"))
    wmodS = ap(nc.dram_tensor("wmodS", [D, 128 * jpc], F32, kind="ExternalInput"))
    cvec = ap(nc.dram_tensor("cvec", [1, D], F32, kind="ExternalInput"))
    gwT = ap(nc.dram_tensor("gwT", [D, 8], F32, kind="ExternalInput"))
    icsM = ap(nc.dram_tensor("icsM", [8, D], F32, kind="ExternalInput"))
    ocsM = ap(nc.dram_tensor("ocsM", [8, D], F32, kind="ExternalInput"))
    biasM = ap(nc.dram_tensor("biasM", [128, 32], F32, kind="ExternalInput"))
    cosE = ap(nc.dram_tensor("cosE", [128, ext], BF16, kind="ExternalInput"))
    sinE = ap(nc.dram_tensor("sinE", [128, ext], BF16, kind="ExternalInput"))
    aABm = ap(nc.dram_tensor("aABm", [128, 2 * nqc], F32, kind="ExternalInput"))
    m3m = ap(nc.dram_tensor("m3m", [128, nqb], F32, kind="ExternalInput"))
    vmkm = ap(nc.dram_tensor("vmkm", [128, nek], F32, kind="ExternalInput"))
    idf = ap(nc.dram_tensor("idf", [128, 128], F32, kind="ExternalInput"))
    onesbI = ap(nc.dram_tensor("onesbI", [1, 128], BF16, kind="ExternalInput"))
    onescI = ap(nc.dram_tensor("onescI", [128, 1], F32, kind="ExternalInput"))
    idbI = ap(nc.dram_tensor("idbI", [128, 128], BF16, kind="ExternalInput"))

    scrq = ap(nc.dram_tensor("scrq", [32, 128, 32, 128], BF16, kind="Internal"))
    scro = ap(nc.dram_tensor("scro", [32, 128, 32, 128], BF16, kind="Internal"))
    scrf = ap(nc.dram_tensor("scrf", [32, 128, 32, 128], BF16, kind="Internal"))
    ccin = ap(nc.dram_tensor("ccin", [jpc, 128], F32, kind="Internal"))
    ccout = ap(nc.dram_tensor("ccout", [jpc * n_cores, 128], F32, kind="Internal"))

    outT = ap(nc.dram_tensor("outT", [D, core_t], F32, kind="ExternalOutput"))

    sched = {k: [] for k in ("sync", "tensor", "vector", "scalar", "gpsimd")}

    def em(eng, fn):
        sched[eng].append(fn)

    def WAIT(eng, ct, v):
        if v > 0:
            em(eng, lambda e, h=ct.h, v=v: e.wait_ge(h, v))

    def WAITV(eng, tv):
        WAIT(eng, tv[0], tv[1])

    def DMA(eng, ct, out, in_, **kw):
        ct.n += 16
        em(eng, lambda e, o=out, i=in_, h=ct.h, kw=dict(kw): e.dma_start(out=o, in_=i, **kw).then_inc(h, 16))
        return ct.n

    def TDMA(ct, out, in_):
        ct.n += 16
        em("scalar", lambda e, o=out, i=in_, h=ct.h: e.dma_start(out=o, in_=i, transpose=True).then_inc(h, 16))
        return ct.n

    ENG_CT = {}

    def OP(eng, fn):
        ct = ENG_CT[eng]
        if eng != "tensor" and ct.n > 0:
            em(eng, lambda e, h=ct.h, v=ct.n: e.wait_ge(h, v))
        ct.n += 1
        em(eng, lambda e, fn=fn, h=ct.h: fn(e).then_inc(h, 1))
        return ct.n

    def MM(fn):
        em("tensor", lambda e, fn=fn: fn(e))

    with ExitStack() as _st:
        ent = _st.enter_context
        xsb = ent(nc.sbuf_tensor("xsb", [128, ND * TT], F32))
        hsb = ent(nc.sbuf_tensor("hsb", [128, ND * TT], BF16))
        aosb = ent(nc.sbuf_tensor("aosb", [128, ND * TT], BF16))
        kres = ent(nc.sbuf_tensor("kres", [128, ext], BF16))
        vtm = ent(nc.sbuf_tensor("vtm", [128, (ext // 128) * 129], BF16))
        cossb = ent(nc.sbuf_tensor("cossb", [128, TT], BF16))
        sinsb = ent(nc.sbuf_tensor("sinsb", [128, TT], BF16))
        qslot = ent(nc.sbuf_tensor("qslot", [128, 1024], BF16))
        sig = ent(nc.sbuf_tensor("sig", [128, 1536], BF16))
        psiT = ent(nc.sbuf_tensor("psiT", [128, 512], BF16, is_transpose_dest=True))
        scrA = ent(nc.sbuf_tensor("scrA", [128, 512], F32))
        scrB = ent(nc.sbuf_tensor("scrB", [128, 512], F32))
        scrC = ent(nc.sbuf_tensor("scrC", [128, 512], F32))
        scrD = ent(nc.sbuf_tensor("scrD", [128, 512], F32))
        mrow = ent(nc.sbuf_tensor("mrow", [1, core_t], BF16))
        rrow = ent(nc.sbuf_tensor("rrow", [1, core_t], BF16))
        rowsb = ent(nc.sbuf_tensor("rowsb", [1, 1024], BF16))
        ostg = ent(nc.sbuf_tensor("ostg", [128, 512], F32))
        wslA = ent(nc.sbuf_tensor("wslA", [128, 2048], BF16))
        wslB = ent(nc.sbuf_tensor("wslB", [128, 2048], BF16))
        pp0 = ent(nc.sbuf_tensor("pp0", [128, 1024], BF16))
        pp1 = ent(nc.sbuf_tensor("pp1", [128, 1024], BF16))
        wslC = ent(nc.sbuf_tensor("wslC", [128, 2048], BF16))
        wslD = ent(nc.sbuf_tensor("wslD", [128, 2048], BF16))
        icsb = ent(nc.sbuf_tensor("icsb", [8, D], BF16))
        ocsb = ent(nc.sbuf_tensor("ocsb", [8, D], BF16))
        gwb = ent(nc.sbuf_tensor("gwb", [128, 256], BF16))
        idfs = ent(nc.sbuf_tensor("idfs", [128, 128], F32))
        idbs = ent(nc.sbuf_tensor("idbs", [128, 128], BF16))
        onesb = ent(nc.sbuf_tensor("onesb", [1, 128], BF16))
        onesc = ent(nc.sbuf_tensor("onesc", [128, 1], F32))
        aAB = ent(nc.sbuf_tensor("aAB", [128, 2 * nqc], F32))
        m3s = ent(nc.sbuf_tensor("m3s", [128, nqb], F32))
        vmk = ent(nc.sbuf_tensor("vmk", [128, nek], F32))
        biasC = ent(nc.sbuf_tensor("biasC", [128, 32], F32))
        scol32 = ent(nc.sbuf_tensor("scol32", [128, 32], F32))
        camod = ent(nc.sbuf_tensor("camod", [128, 224], F32))
        sums = ent(nc.sbuf_tensor("sums", [128, 16], F32))
        coefs = ent(nc.sbuf_tensor("coefs", [128, 8], F32))
        nmx = ent(nc.sbuf_tensor("nmx", [128, 4], F32))
        pQ0 = ent(nc.psum_tensor("pQ0", [128, 512], F32))
        pQ1 = ent(nc.psum_tensor("pQ1", [128, 512], F32))
        pS = ent(nc.psum_tensor("pS", [128, 1024], F32))
        pVo0 = ent(nc.psum_tensor("pVo0", [128, 512], F32))
        pVo1 = ent(nc.psum_tensor("pVo1", [128, 512], F32))
        pTb = ent(nc.psum_tensor("pTb", [128, 1024], BF16))
        pTb2 = ent(nc.psum_tensor("pTb2", [128, 1024], BF16))
        _sC = ent(nc.semaphore("smC"))
        _sX = ent(nc.semaphore("smX"))
        _sCS = ent(nc.semaphore("smCS"))
        _sSlA = ent(nc.semaphore("smSlA"))
        _sSlB = ent(nc.semaphore("smSlB"))
        _sGA = ent(nc.semaphore("smGA"))
        _sGB = ent(nc.semaphore("smGB"))
        _sScr = ent(nc.semaphore("smScr"))
        _sScrB = ent(nc.semaphore("smScrB"))
        _sCc = ent(nc.semaphore("smCc"))
        _sCo = ent(nc.semaphore("smCo"))
        _sOutB = ent(nc.semaphore("smOutB"))
        _sGc = ent(nc.semaphore("smGc"))
        _sPpA = ent(nc.semaphore("smPpA"))
        _sPpB = ent(nc.semaphore("smPpB"))
        _sGP = ent(nc.semaphore("smGP"))
        _sOgA = ent(nc.semaphore("smOgA"))
        _sOgB = ent(nc.semaphore("smOgB"))
        _sXg = ent(nc.semaphore("smXg"))
        _sCSg = ent(nc.semaphore("smCSg"))
        _sSlC = ent(nc.semaphore("smSlC"))
        _sSlD = ent(nc.semaphore("smSlD"))
        _sGq0 = ent(nc.semaphore("smGq0"))
        _sGq1 = ent(nc.semaphore("smGq1"))
        _sGq2 = ent(nc.semaphore("smGq2"))
        _sGq3 = ent(nc.semaphore("smGq3"))
        _sTpA = ent(nc.semaphore("smTpA"))
        _sTpB = ent(nc.semaphore("smTpB"))
        _sOut = ent(nc.semaphore("smOut"))
        _sCC = ent(nc.semaphore("smCC"))
        _sPE = ent(nc.semaphore("smPE"))
        _sACT = ent(nc.semaphore("smACT"))
        _sDVE = ent(nc.semaphore("smDVE"))
        block = ent(nc.Block())
        sC, sX, sCS = Ct(_sC), Ct(_sX), Ct(_sCS)
        sSl = [Ct(_sSlA), Ct(_sSlB)]
        sG = [Ct(_sGA), Ct(_sGB)]
        sScr = [Ct(_sScr), Ct(_sScrB)]
        sCc, sCo, sGc = Ct(_sCc), Ct(_sCo), Ct(_sGc)
        sPp = [Ct(_sPpA), Ct(_sPpB)]
        sGP = Ct(_sGP)
        sOg = [Ct(_sOgA), Ct(_sOgB)]
        sXg, sCSg = Ct(_sXg), Ct(_sCSg)
        sSl3 = [sSl[0], sSl[1], Ct(_sSlC), Ct(_sSlD)]
        sGq = [Ct(_sGq0), Ct(_sGq1), Ct(_sGq2), Ct(_sGq3)]
        sTp = [Ct(_sTpA), Ct(_sTpB)]
        sOut = [Ct(_sOut), Ct(_sOutB)]
        sCC = Ct(_sCC)
        sPE, sACT, sDVE = Ct(_sPE), Ct(_sACT), Ct(_sDVE)
        ENG_CT.update({"tensor": sPE, "vector": sDVE, "scalar": sACT, "gpsimd": sGP})

        CA1, SAv, GAC, CB1, SBv, GBC, GBIAS = (camod[:, 32 * i:32 * i + 32] for i in range(7))

        modpart = scrA[:, 384:408]
        routb = sig[:, 0:8]
        routT = psiT[0:8, 0:512]
        modcol = scrB[:, 0:192]
        routf = coefs[:, 0:8]

        scrUse = {"a": 0, "v": 0, "t": 0}

        def mark_scratch():
            scrUse["a"], scrUse["v"], scrUse["t"] = sACT.n, sDVE.n, sPE.n

        def wait_scratch(eng):
            if eng != "scalar":
                WAIT(eng, sACT, scrUse["a"])
            if eng != "vector":
                WAIT(eng, sDVE, scrUse["v"])
            if eng != "tensor":
                WAIT(eng, sPE, scrUse["t"])

        DMA("sync", sC, idfs[:], idf)
        DMA("sync", sC, idbs[:], idbI)
        DMA("sync", sC, onesb[:], onesbI)
        DMA("sync", sC, onesc[:], onescI)
        DMA("sync", sC, aAB[:], aABm)
        DMA("sync", sC, m3s[:], m3m)
        DMA("sync", sC, vmk[:], vmkm)
        DMA("sync", sC, biasC[:], biasM)
        DMA("sync", sC, scrB[0:32, 0:128], cvec.rearrange("o (a b) -> (o a) b", b=128))
        vC = sC.n
        DMA("gpsimd", sGc, gwb[:].rearrange("p (b e) -> p b e", e=8),
            gwT.rearrange("(b p) e -> p b e", p=128))
        DMA("gpsimd", sGc, icsb[:], icsM)
        DMA("gpsimd", sGc, ocsb[:], ocsM)
        DMA("gpsimd", sGc, aosb[:, 0:4096].rearrange("p (b f) -> p b f", f=128),
            wkT.rearrange("(b p) f -> p b f", p=128))
        DMA("gpsimd", sGc, aosb[:, 4096:8192].rearrange("p (b f) -> p b f", f=128),
            wvT.rearrange("(b p) f -> p b f", p=128))
        vGsmall = sGc.n
        DMA("sync", sX, xsb[:, 0:8192].rearrange("p (b t) -> p b t", t=TT),
            xT[0:2048, 0:TT].rearrange("(b p) t -> p b t", p=128))
        DMA("sync", sX, xsb[:, 14336:16384].rearrange("p (b t) -> p b t", t=TT),
            xT[3584:4096, 0:TT].rearrange("(b p) t -> p b t", p=128))
        DMA("sync", sCS, cossb[:], cosE[:, 0:TT])
        DMA("sync", sCS, sinsb[:], sinE[:, 0:TT])
        vCSl0 = sCS.n

        WAIT("scalar", sC, vC)
        vSilu = OP("scalar", lambda e: e.activation(out=scrB[0:32, 128:256], in_=scrB[0:32, 0:128], func=AF.Silu))
        WAIT("tensor", sACT, vSilu)
        WAIT("tensor", sC, vC)
        vTs = OP("tensor", lambda e: e.transpose(pQ1[:, 0:32], scrB[0:32, 128:256], idfs[0:32, 0:32]))
        WAIT("vector", sPE, vTs)
        vScol = OP("vector", lambda e: e.tensor_copy(out=scol32[:], in_=pQ1[:, 0:32]))

        mod_evac = 0
        peW = {0: 0, 1: 0}
        for jb in range(n_jb):
            j0 = jb * 24
            jn = min(24, jpc - j0)
            for i in range(ND):
                sl = i % 2
                xbuf = xsb[:, 8192 + 3072 * sl:8192 + 3072 * sl + 3072]
                WAIT("gpsimd", sPE, peW[sl])
                DMA("gpsimd", sG[sl], xbuf[:, 0:128 * jn],
                    wmodS[128 * i:128 * i + 128, 128 * j0:128 * (j0 + jn)])
                if jb == 0 and i == 0:
                    WAIT("tensor", sDVE, vScol)
                WAIT("tensor", sG[sl], sG[sl].n)
                for j in range(jn):
                    fn = (lambda e, xbuf=xbuf, j=j, i=i:
                          e.matmul(pS[:, (32 * j + i):(32 * j + i + 1)],
                                   xbuf[:, 128 * j:128 * j + 128],
                                   scol32[:, i:i + 1], start=True, stop=True))
                    if j == jn - 1:
                        vmm = OP("tensor", fn)
                    else:
                        MM(fn)
                peW[sl] = vmm
            for j in range(jn):
                if j == 0:
                    WAIT("vector", sPE, vmm)
                vred = OP("vector", lambda e, j=j: e.tensor_reduce(
                    out=modpart[:, j:j + 1], in_=pS[:, 32 * j:32 * j + 32], axis=AX.X, op=ALU.add))
            WAIT("tensor", sDVE, vred)
            vTp = OP("tensor", lambda e, jn=jn: e.transpose(pQ0[0:jn, 0:128], modpart[:, 0:jn], idfs[:]))
            WAIT("vector", sPE, vTp)
            WAIT("vector", sCc, mod_evac)
            vEv = OP("vector", lambda e, jn=jn: e.tensor_copy(out=scrA[0:jn, 256:384], in_=pQ0[0:jn, 0:128]))
            WAIT("sync", sDVE, vEv)
            DMA("sync", sCc, ccin[j0:j0 + jn, :], scrA[0:jn, 256:384])
            mod_evac = sCc.n

        WAIT("sync", sPE, vmm)
        DMA("sync", sX, xsb[:, 8192:14336].rearrange("p (b t) -> p b t", t=TT),
            xT[2048:3584, 0:TT].rearrange("(b p) t -> p b t", p=128))
        vXl0 = sX.n
        WAIT("gpsimd", sCc, mod_evac)
        em("gpsimd", lambda e: e.collective_compute(
            "AllGather", ALU.bypass, replica_groups=[list(range(n_cores))],
            ins=[ccin], outs=[ccout]).then_inc(sCC.h, 1))
        sCC.n += 1
        WAIT("sync", sCC, sCC.n)
        R = jpc * n_cores
        g0, tI = 0, 0
        vT2 = 0
        while g0 < R:
            gn = min(96, R - g0)
            WAIT("sync", sPE, vT2)
            DMA("sync", sCo, scrA[0:gn, 0:128], ccout[g0:g0 + gn, :])
            WAIT("tensor", sCo, sCo.n)
            vT2 = OP("tensor", lambda e, gn=gn, tI=tI: e.transpose(
                pQ1[0:128, 128 * tI:128 * tI + gn], scrA[0:gn, 0:128], idfs[0:gn, 0:gn]))
            WAIT("vector", sPE, vT2)
            vMc = OP("vector", lambda e, gn=gn, g0=g0, tI=tI: e.tensor_copy(
                out=modcol[:, g0:g0 + gn], in_=pQ1[:, 128 * tI:128 * tI + gn]))
            g0 += gn
            tI += 1
        OP("vector", lambda e: e.tensor_copy(out=SAv, in_=modcol[:, 0:32]))
        OP("vector", lambda e: e.tensor_scalar(out=CA1, in0=modcol[:, 32:64], scalar1=1.0, scalar2=None, op0=ALU.add))
        OP("vector", lambda e: e.tensor_copy(out=SBv, in_=modcol[:, 96:128]))
        OP("vector", lambda e: e.tensor_scalar(out=CB1, in0=modcol[:, 128:160], scalar1=1.0, scalar2=None, op0=ALU.add))
        for (ga_s, dst) in ((64, GAC), (160, GBC)):
            g = modcol[:, ga_s:ga_s + 32]
            OP("vector", lambda e, g=g: e.tensor_scalar(out=scrC[:, 0:32], in0=g, scalar1=0.0, scalar2=None, op0=ALU.is_ge))
            OP("vector", lambda e, g=g: e.tensor_scalar(out=scrC[:, 32:64], in0=g, scalar1=0.25, scalar2=None, op0=ALU.max))
            OP("vector", lambda e, g=g: e.tensor_scalar(out=scrC[:, 64:96], in0=g, scalar1=-0.25, scalar2=None, op0=ALU.min))
            OP("vector", lambda e: e.tensor_sub(scrC[:, 96:128], scrC[:, 32:64], scrC[:, 64:96]))
            OP("vector", lambda e: e.tensor_mul(scrC[:, 96:128], scrC[:, 96:128], scrC[:, 0:32]))
            OP("vector", lambda e, dst=dst: e.tensor_add(dst, scrC[:, 96:128], scrC[:, 64:96]))
        WAIT("vector", sC, vC)
        vMod = OP("vector", lambda e: e.tensor_mul(GBIAS, GBC, biasC[:]))
        mark_scratch()

        pp = [pp0, pp1]
        prep_state = {"prevWr": {0: 0, 1: 0}, "qidx": 0}
        prep_items = []
        scr_done = {}

        def mk_prep_item(wsrc, scr, do_sign, i, q, last):
            def emit():
                sl = prep_state["qidx"] % 2
                prep_state["qidx"] += 1
                WAIT("gpsimd", sScr[sl], prep_state["prevWr"][sl])
                DMA("gpsimd", sPp[sl], pp[sl][:], wsrc[128 * i:128 * i + 128, 1024 * q:1024 * q + 1024])
                WAIT("gpsimd", sPp[sl], sPp[sl].n)
                DMA("gpsimd", sScr[sl], scr[8 * q:8 * q + 8, :, i, :].rearrange("c p f -> p c f"),
                    pp[sl][:].rearrange("p (c f) -> p c f", f=128))
                prep_state["prevWr"][sl] = sScr[sl].n
                if last:
                    scr_done[id(scr)] = (sScr[0].n, sScr[1].n)
            return emit

        def mat_items(wsrc, scr, do_sign):
            return [mk_prep_item(wsrc, scr, do_sign, i, q, i == ND - 1 and q == 3)
                    for i in range(ND) for q in range(4)]

        prep_items.extend(mat_items(wfT, scrf, False))

        def weave(n):
            for _ in range(min(n, len(prep_items))):
                prep_items.pop(0)()

        def scr_gate(scr):
            v = scr_done.get(id(scr))
            if v is None:
                while prep_items and id(scr) not in scr_done:
                    prep_items.pop(0)()
                v = scr_done[id(scr)]
            return v

        def ln_stats(src_f32, mdst, rdst, extra_rows=None):
            wait_scratch("vector")
            wait_scratch("scalar")
            v = OP("vector", lambda e: e.tensor_copy(out=scrA[:], in_=src_f32[:, 0:512]))
            aq = OP("scalar", lambda e: e.activation(out=scrB[:], in_=src_f32[:, 0:512], func=AF.Square))
            vadd2 = 0
            for cix in range(1, ND):
                ch = src_f32[:, 512 * cix:512 * cix + 512]
                OP("vector", lambda e, ch=ch: e.tensor_add(scrA[:], scrA[:], ch))
                if vadd2:
                    WAIT("scalar", sDVE, vadd2)
                aq = OP("scalar", lambda e, ch=ch: e.activation(out=scrC[:], in_=ch, func=AF.Square))
                WAIT("vector", sACT, aq)
                vadd2 = OP("vector", lambda e: e.tensor_add(scrB[:], scrB[:], scrC[:]))
            WAIT("tensor", sDVE, vadd2)
            return ln_tail(mdst, rdst, extra_rows=extra_rows)

        def ln_tail(mdst, rdst, vdep=0, extra_rows=None):
            wait_scratch("tensor")
            WAIT("tensor", sDVE, vdep)
            OP("tensor", lambda e: e.matmul(pVo1[0:1, 0:512], onesc[:], scrA[:], start=True, stop=True))
            vmm = OP("tensor", lambda e: e.matmul(pVo0[0:1, 0:512], onesc[:], scrB[:], start=True, stop=True))
            WAIT("vector", sPE, vmm)
            OP("vector", lambda e: e.tensor_scalar(out=scrC[0:1, 0:512], in0=pVo1[0:1, 0:512],
                                                   scalar1=1.0 / D, scalar2=None, op0=ALU.mult))
            OP("vector", lambda e: e.tensor_scalar(out=scrB[0:1, 0:512], in0=pVo0[0:1, 0:512],
                                                   scalar1=1.0 / D, scalar2=None, op0=ALU.mult))
            OP("vector", lambda e: e.tensor_mul(scrA[0:1, 0:512], scrC[0:1, 0:512], scrC[0:1, 0:512]))
            v = OP("vector", lambda e: e.scalar_tensor_tensor(
                out=scrB[0:1, 0:512], in0=scrB[0:1, 0:512], scalar=EPS, in1=scrA[0:1, 0:512],
                op0=ALU.add, op1=ALU.subtract))
            WAIT("scalar", sDVE, v)
            aS = OP("scalar", lambda e: e.activation(out=scrA[0:1, 0:512], in_=scrB[0:1, 0:512], func=AF.Sqrt))
            WAIT("vector", sACT, aS)
            OP("vector", lambda e: e.reciprocal(out=scrB[0:1, 0:512], in_=scrA[0:1, 0:512]))
            OP("vector", lambda e, mdst=mdst: e.tensor_copy(out=mdst, in_=scrC[0:1, 0:512]))
            v = OP("vector", lambda e, rdst=rdst: e.tensor_copy(out=rdst, in_=scrB[0:1, 0:512]))
            if extra_rows is not None:
                lo, hi, own0 = extra_rows
                OP("vector", lambda e, lo=lo, hi=hi, own0=own0: e.tensor_copy(
                    out=mrow[0:1, own0:own0 + hi - lo], in_=scrC[0:1, lo:hi]))
                v = OP("vector", lambda e, lo=lo, hi=hi, own0=own0: e.tensor_copy(
                    out=rrow[0:1, own0:own0 + hi - lo], in_=scrB[0:1, lo:hi]))
            return v

        def ln_bcast(mr, rr, vrows=0):
            wait_scratch("tensor")
            WAIT("tensor", sDVE, vrows)
            OP("tensor", lambda e, mr=mr: e.matmul(pS[:, 0:512], onesb[:], mr, start=True, stop=True))
            return OP("tensor", lambda e, rr=rr: e.matmul(pS[:, 512:1024], onesb[:], rr, start=True, stop=True))

        def ln_apply(src_f32, dst_bf, c1col, shcol, vdep_pe):
            lastAs = []
            for cix in range(ND):
                ch = src_f32[:, 512 * cix:512 * cix + 512]
                dch = dst_bf[:, 512 * cix:512 * cix + 512]
                pa, pb = (scrA, scrB) if cix % 2 == 0 else (scrC, scrD)
                if cix == 0:
                    WAIT("vector", sPE, vdep_pe)
                    wait_scratch("vector")
                if cix >= 2:
                    WAIT("vector", sACT, lastAs[cix - 2])
                OP("vector", lambda e, ch=ch, pa=pa: e.tensor_sub(pa[:], ch, pS[:, 0:512]))
                vv = OP("vector", lambda e, pa=pa, pb=pb: e.tensor_mul(pb[:], pa[:], pS[:, 512:1024]))
                WAIT("scalar", sDVE, vv)
                lastAs.append(OP("scalar", lambda e, dch=dch, cix=cix, pb=pb: e.activation(
                    out=dch, in_=pb[:], func=AF.Identity,
                    bias=shcol[:, cix:cix + 1], scale=c1col[:, cix:cix + 1])))
            return lastAs

        vFree = [0, 0, 0, 0]
        lastVevac = 0
        lastHprev = 0
        for et in range(n_ext):
            t0 = et * TT
            if et > 0:
                WAIT("sync", sACT, lastHprev)
                DMA("sync", sX, xsb[:].rearrange("p (b t) -> p b t", t=TT),
                    xT[:, t0:t0 + TT].rearrange("(b p) t -> p b t", p=128))
                vXl = sX.n
                WAIT("sync", sDVE, sDVE.n)
                DMA("sync", sCS, cossb[:], cosE[:, t0:t0 + TT])
                DMA("sync", sCS, sinsb[:], sinE[:, t0:t0 + TT])
                vCSl = sCS.n
            else:
                vXl = vXl0
                vCSl = vCSl0
            WAIT("vector", sX, vXl)
            WAIT("scalar", sX, vXl)
            own_lo = max(0, 256 - t0)
            own_hi = min(512, 256 + core_t - t0)
            vRows = ln_stats(xsb, rowsb[0:1, 0:512], rowsb[0:1, 512:1024],
                             extra_rows=(own_lo, own_hi, t0 - 256 + own_lo))
            vB = ln_bcast(rowsb[0:1, 0:512], rowsb[0:1, 512:1024], vRows)
            lastHs = ln_apply(xsb, hsb, CA1, SAv, vB)
            lastH = lastHs[-1]
            lastHprev = lastH
            vGrp = {}
            if et == 0:
                WAIT("tensor", sGc, vGsmall)
            for which, wbase, pdst in (("k", 0, pQ0), ("v", 4096, pQ1)):
                waited = 0
                for i in range(ND):
                    if lastHs[i] > waited:
                        waited = lastHs[i]
                        WAIT("tensor", sACT, waited)
                    fn = (lambda e, wbase=wbase, i=i, pdst=pdst: e.matmul(
                        pdst[:], aosb[:, wbase + 128 * i:wbase + 128 * i + 128],
                        hsb[:, 512 * i:512 * i + 512], start=(i == 0), stop=(i == ND - 1)))
                    if i == ND - 1:
                        vg = OP("tensor", fn)
                    else:
                        MM(fn)
                vGrp[which] = vg
            WAIT("vector", sPE, vGrp["k"])
            WAIT("vector", sCS, vCSl)
            OP("vector", lambda e: e.tensor_mul(scrA[:], pQ0[:], cossb[:]))
            OP("vector", lambda e: e.tensor_mul(scrB[0:64, :], pQ0[64:128, :], sinsb[0:64, :]))
            OP("vector", lambda e: e.tensor_mul(scrB[64:128, :], pQ0[0:64, :], sinsb[64:128, :]))
            OP("vector", lambda e, t0=t0: e.tensor_sub(kres[0:64, t0:t0 + TT], scrA[0:64, :], scrB[0:64, :]))
            OP("vector", lambda e, t0=t0: e.tensor_add(kres[64:128, t0:t0 + TT], scrA[64:128, :], scrB[64:128, :]))
            WAIT("scalar", sPE, vGrp["v"])
            vVc = OP("scalar", lambda e: e.activation(out=sig[:, 0:512], in_=pQ1[:], func=AF.Copy))
            WAIT("tensor", sACT, vVc)
            WAIT("tensor", sDVE, lastVevac)
            for j in range(4):
                vTt = OP("tensor", lambda e, j=j: e.transpose(
                    pTb[:, 128 * j:128 * j + 128], sig[:, 128 * j:128 * j + 128], idbs[:]))
            WAIT("vector", sPE, vTt)
            for j in range(4):
                kb = 4 * et + j
                lastVevac = OP("vector", lambda e, j=j, kb=kb: e.tensor_scalar(
                    out=vtm[:, 129 * kb:129 * kb + 128], in0=pTb[:, 128 * j:128 * j + 128],
                    scalar1=vmk[:, kb:kb + 1], scalar2=None, op0=ALU.mult))
            if et == n_ext - 1:
                lastVevac = OP("vector", lambda e: e.tensor_copy(
                    out=vtm[:].rearrange("p (b c) -> p b c", c=129)[:, :, 128:129],
                    in_=vmk[:].rearrange("p (b c) -> p b c", c=1)))
            mark_scratch()
            weave(26)

        WSL3 = [wslA, wslB, wslC, wslD]
        mmg_state = {"hidx": 0}
        vSigACT = {}
        vTpPE = {}
        vOmm = {}
        vBlend = {}
        vEvac = {}
        uG = 0
        uTile = {}

        def emit_O(u):
            h_ = (u % 128) // 4
            qc_ = u % 4
            tl = uTile[u]
            lqb = 2 * tl + qc_ // 2
            slot = u % 2
            WAIT("tensor", sACT, vSigACT[u])
            if u >= 1:
                WAIT("tensor", sDVE, vBlend[u - 1])
            for grp, (pO, c0) in enumerate(((pVo0, 0), (pVo1, 2))):
                for cx in range(c0, c0 + 4):
                    kb = 2 * lqb + cx
                    fn = (lambda e, pO=pO, cx=cx, kb=kb, slot=slot, c0=c0: e.matmul(
                        pO[:, 0:129], sig[:, 768 * slot + 128 * cx:768 * slot + 128 * cx + 128],
                        vtm[:, 129 * kb:129 * kb + 129],
                        start=(cx == c0), stop=(cx == c0 + 3)))
                    if cx == c0 + 3:
                        vOmm[u] = OP("tensor", fn)
                    else:
                        MM(fn)

        def emit_blend(u):
            tl = uTile[u]
            qc_ = u % 4
            lqc = 4 * tl + qc_
            slot = u % 2
            sm = sums[:, 8 * slot:8 * slot + 8]
            cf = coefs[:, 4 * slot:4 * slot + 4]
            WAIT("vector", sPE, vOmm[u])
            OP("vector", lambda e, sm=sm: e.reciprocal(out=sm[:, 0:1], in_=pVo0[:, 128:129]))
            OP("vector", lambda e, sm=sm: e.reciprocal(out=sm[:, 1:2], in_=pVo1[:, 128:129]))
            OP("vector", lambda e, sm=sm, cf=cf, lqc=lqc: e.tensor_mul(
                cf[:, 0:2], sm[:, 0:2], aAB[:, 2 * lqc:2 * lqc + 2]))
            OP("vector", lambda e, cf=cf: e.tensor_scalar(
                out=scrC[:, 0:128], in0=pVo0[:, 0:128], scalar1=cf[:, 0:1], scalar2=None, op0=ALU.mult))
            vBlend[u] = OP("vector", lambda e, cf=cf, slot=slot: e.scalar_tensor_tensor(
                out=psiT[:, 128 * slot:128 * slot + 128], in0=pVo1[:, 0:128], scalar=cf[:, 1:2],
                in1=scrC[:, 0:128], op0=ALU.mult, op1=ALU.add))

        def emit_otrans(u):
            h_ = (u % 128) // 4
            qc_ = u % 4
            slot = u % 2
            pT = pTb if slot == 0 else pTb2
            WAIT("tensor", sDVE, vBlend[u])
            if u >= 2:
                WAIT("tensor", sACT, vEvac[u - 2][1])
            vTpPE[u] = OP("tensor", lambda e, pT=pT, slot=slot: e.transpose(
                pT[:, 0:128], psiT[:, 128 * slot:128 * slot + 128], idbs[:]))
            WAIT("scalar", sPE, vTpPE[u])
            vEvac[u] = (sACT, OP("scalar", lambda e, h_=h_, qc_=qc_, pT=pT: e.activation(
                out=aosb[:, 512 * h_ + 128 * qc_:512 * h_ + 128 * qc_ + 128],
                in_=pT[:, 0:128], func=AF.Copy)))

        def mm_group(wsrc, c, rhs_sb, pdst, extra_pre=None, rhs_ready=None, sign=False,
                     direct=False):
            if extra_pre:
                extra_pre()
            vh = []
            slots = []
            vsg = [0, 0]
            for half in range(2):
                sl = mmg_state["hidx"] % 4
                mmg_state["hidx"] += 1
                slots.append(sl)
                wdst = WSL3[sl][:].rearrange("p (b f) -> p b f", f=128)
                if direct:
                    WAIT("gpsimd", sPE, vFree[sl])
                    DMA("gpsimd", sGq[sl], wdst,
                        wsrc[2048 * half:2048 * half + 2048, 128 * c:128 * c + 128]
                        .rearrange("(b p) f -> p b f", p=128))
                    vh.append((sGq[sl], sGq[sl].n))
                else:
                    WAIT("sync", sPE, vFree[sl])
                    DMA("sync", sSl3[sl], wdst, wsrc[c, :, 16 * half:16 * half + 16, :])
                    vh.append((sSl3[sl], sSl3[sl].n))
                if sign:
                    WAITV("vector", vh[half])
                    WAIT("vector", sPE, vFree[sl])
                    OP("vector", lambda e, sl=sl: e.tensor_scalar(
                        out=WSL3[sl][:], in0=WSL3[sl][:], scalar1=1e38, scalar2=None, op0=ALU.mult))
                    vsg[half] = OP("vector", lambda e, sl=sl: e.tensor_scalar(
                        out=WSL3[sl][:], in0=WSL3[sl][:], scalar1=-1.0, scalar2=1.0, op0=ALU.max, op1=ALU.min))
            out = None
            waited = 0
            for half in range(2):
                sl = slots[half]
                wsl = WSL3[sl]
                WAITV("tensor", vh[half])
                if sign:
                    WAIT("tensor", sDVE, vsg[half])
                for b in range(16):
                    i = 16 * half + b
                    if rhs_ready is not None and rhs_ready[i] > waited:
                        waited = rhs_ready[i]
                        WAIT("tensor", sACT, waited)
                    fn = (lambda e, wsl=wsl, b=b, i=i, pdst=pdst: e.matmul(
                        pdst[:], wsl[:, 128 * b:128 * b + 128],
                        rhs_sb[:, 512 * i:512 * i + 512], start=(i == 0), stop=(i == ND - 1)))
                    if i == 15 or i == ND - 1:
                        out = OP("tensor", fn)
                        vFree[sl] = out
                    else:
                        MM(fn)
            return out

        vXl_pref = None
        vCSl_pref = None
        vStSlot = {0: 0, 1: 0}
        for tile in range(n_tiles):
            t0 = tile * TT
            e0 = t0 + 256
            if vXl_pref is None:
                WAIT("sync", sACT, lastHprev)
                DMA("sync", sX, xsb[:].rearrange("p (b t) -> p b t", t=TT),
                    xT[:, e0:e0 + TT].rearrange("(b p) t -> p b t", p=128))
                vXl = (sX, sX.n)
                WAIT("sync", sDVE, sDVE.n)
                DMA("sync", sCS, cossb[:], cosE[:, e0:e0 + TT])
                DMA("sync", sCS, sinsb[:], sinE[:, e0:e0 + TT])
                vCSl = (sCS, sCS.n)
            else:
                vXl, vCSl = vXl_pref, vCSl_pref
            WAITV("vector", vXl)
            WAITV("scalar", vXl)
            vB = ln_bcast(mrow[0:1, t0:t0 + 512], rrow[0:1, t0:t0 + 512])
            lastHs = ln_apply(xsb, hsb, CA1, SAv, vB)
            lastH = lastHs[-1]

            vRope = {}
            vLastScores = sPE.n
            vQmm = {}
            qsub_state = {}

            def q_load(h):
                if tile == 0:
                    weave(4)
                vh, slots = [], []
                for half in range(2):
                    sl = mmg_state["hidx"] % 4
                    mmg_state["hidx"] += 1
                    slots.append(sl)
                    WAIT("gpsimd", sPE, vFree[sl])
                    DMA("gpsimd", sGq[sl], WSL3[sl][:].rearrange("p (b f) -> p b f", f=128),
                        wqT[2048 * half:2048 * half + 2048, 128 * h:128 * h + 128]
                        .rearrange("(b p) f -> p b f", p=128))
                    vh.append((sGq[sl], sGq[sl].n))
                qsub_state[h] = (slots, vh)

            def q_sub(h, j, rhs_ready=None):
                slots, vh = qsub_state[h]
                half = j // 2
                sl = slots[half]
                wsl = WSL3[sl]
                if j == 0:
                    if h >= 2:
                        WAITV("tensor", vRope[h - 2])
                    WAIT("tensor", sACT, lastH)
                if j % 2 == 0:
                    WAITV("tensor", vh[half])
                out = None
                waited = 0
                for b in range(8):
                    i = 8 * j + b
                    if rhs_ready is not None and rhs_ready[i] > waited:
                        waited = rhs_ready[i]
                        WAIT("tensor", sACT, waited)
                    fn = (lambda e, wsl=wsl, i=i, h=h: e.matmul(
                        (pQ0 if h % 2 == 0 else pQ1)[:], wsl[:, 128 * (i % 16):128 * (i % 16) + 128],
                        hsb[:, 512 * i:512 * i + 512], start=(i == 0), stop=(i == ND - 1)))
                    if i in (15, ND - 1):
                        out = OP("tensor", fn)
                        vFree[sl] = out
                    else:
                        MM(fn)
                if j == 3:
                    vQmm[h] = out
                return out

            def emit_rope(h):
                qsl2 = h % 2
                WAIT("vector", sPE, vQmm[h])
                WAITV("vector", vCSl)
                if h >= 2:
                    WAIT("vector", sPE, vLastScores)
                pQh = pQ0 if qsl2 == 0 else pQ1
                qs = qslot[:, 512 * qsl2:512 * qsl2 + 512]
                OP("vector", lambda e, pQh=pQh: e.tensor_mul(scrA[:], pQh[:], cossb[:]))
                OP("vector", lambda e, pQh=pQh: e.tensor_mul(scrB[0:64, :], pQh[64:128, :], sinsb[0:64, :]))
                OP("vector", lambda e, pQh=pQh: e.tensor_mul(scrB[64:128, :], pQh[0:64, :], sinsb[64:128, :]))
                OP("vector", lambda e, qs=qs: e.tensor_sub(qs[0:64, :], scrA[0:64, :], scrB[0:64, :]))
                vRope[h] = (sDVE, OP("vector", lambda e, qs=qs: e.tensor_add(qs[64:128, :], scrA[64:128, :], scrB[64:128, :])))

            SUBLAG = 5
            q_load(0)
            for j4 in range(4):
                q_sub(0, j4, rhs_ready=lastHs)
            emit_rope(0)
            q_load(1)
            q_sub(1, 0)
            for h in range(H):
                qsl = h % 2
                for qc in range(4):
                    u = uG
                    uG += 1
                    uTile[u] = tile
                    lqc = 4 * tile + qc
                    lqb = 2 * tile + qc // 2
                    slot = u % 2
                    sm = sums[:, 8 * slot:8 * slot + 8]
                    cf = coefs[:, 4 * slot:4 * slot + 4]
                    qs = qslot[:, 512 * qsl:512 * qsl + 512]
                    s = 4 * h + qc + SUBLAG
                    if s < 4 * H:
                        h2, j2 = divmod(s, 4)
                        if j2 == 0 and h2 >= 2:
                            q_load(h2)
                        q_sub(h2, j2)
                        if j2 == 3:
                            emit_rope(h2)
                    WAITV("tensor", vRope[h])
                    if u >= 1:
                        WAIT("tensor", sACT, vSigACT[u - 1])
                    k0 = 256 * lqb
                    for cx in range(6):
                        vSc = OP("tensor", lambda e, qs=qs, qc=qc, k0=k0, cx=cx: e.matmul(
                            pS[:, 128 * cx:128 * cx + 128],
                            kres[:, k0 + 128 * cx:k0 + 128 * cx + 128],
                            qs[:, 128 * qc:128 * qc + 128], start=True, stop=True))
                    vLastScores = vSc
                    WAIT("scalar", sPE, vSc)
                    if u >= 2:
                        WAIT("scalar", sPE, vOmm[u - 2])
                    vSigACT[u] = OP("scalar", lambda e, slot=slot: e.activation(
                        out=sig[:, 768 * slot:768 * slot + 768],
                        in_=pS[:, 0:768], func=AF.Sigmoid, scale=SCALE))
                    if u >= 1 and (u - 1) not in vOmm:
                        emit_O(u - 1)
                        emit_blend(u - 1)
                    if u >= 2 and (u - 2) not in vTpPE:
                        emit_otrans(u - 2)
            uL = uG - 1
            if (uL - 1) not in vOmm:
                emit_O(uL - 1)
                emit_blend(uL - 1)
            emit_O(uL)
            emit_blend(uL)
            if (uL - 1) not in vTpPE:
                emit_otrans(uL - 1)
            emit_otrans(uL)
            vAttnDone = vEvac[uL]
            mark_scratch()

            vRopeLast = vRope[H - 1][1] if vRope[H - 1][0] is sDVE else None
            vX2 = {}
            vp2 = 0
            for c in range(ND):
                def pre_wo(c=c):
                    if c >= 2:
                        WAIT("tensor", sDVE, vX2[c - 2])
                    if c == 0:
                        WAITV("tensor", vAttnDone)
                if tile == 0:
                    weave(3)
                vW = mm_group(woT, c, aosb, pQ0 if c % 2 == 0 else pQ1, extra_pre=pre_wo, direct=True)
                WAIT("vector", sPE, vW)
                pQc = pQ0 if c % 2 == 0 else pQ1
                vX2[c] = OP("vector", lambda e, c=c, pQc=pQc: e.scalar_tensor_tensor(
                    out=xsb[:, 512 * c:512 * c + 512], in0=pQc[:], scalar=GAC[:, c:c + 1],
                    in1=xsb[:, 512 * c:512 * c + 512], op0=ALU.mult, op1=ALU.add))
                ch = xsb[:, 512 * c:512 * c + 512]
                if c == 0:
                    wait_scratch("vector")
                    OP("vector", lambda e, ch=ch: e.tensor_copy(out=scrA[:], in_=ch))
                    WAIT("scalar", sDVE, vX2[0])
                    aq = OP("scalar", lambda e, ch=ch: e.activation(out=scrB[:], in_=ch, func=AF.Square))
                else:
                    OP("vector", lambda e, ch=ch: e.tensor_add(scrA[:], scrA[:], ch))
                    tmp = scrC if c % 2 == 0 else scrD
                    WAIT("scalar", sDVE, vp2 if vp2 else vX2[c])
                    WAIT("scalar", sDVE, vX2[c])
                    aq = OP("scalar", lambda e, ch=ch, tmp=tmp: e.activation(out=tmp[:], in_=ch, func=AF.Square))
                    WAIT("vector", sACT, aq)
                    vp2 = OP("vector", lambda e, tmp=tmp: e.tensor_add(scrB[:], scrB[:], tmp[:]))

            vTail = ln_tail(rowsb[0:1, 0:512], rowsb[0:1, 512:1024], vp2)
            vB2 = ln_bcast(rowsb[0:1, 0:512], rowsb[0:1, 512:1024], vTail)
            lastH2s = ln_apply(xsb, hsb, CB1, SBv, vB2)
            lastH2 = lastH2s[-1]

            WAIT("tensor", sACT, vOsc0 if tile > 0 else 0)
            vRt = 0
            vLgs = {}
            for tc in range(4):
                if tc == 0:
                    if tile == 0:
                        WAIT("tensor", sGc, vGsmall)
                    waited = 0
                else:
                    WAIT("tensor", sACT, vExp)
                for i in range(ND):
                    if tc == 0 and lastH2s[i] > waited:
                        waited = lastH2s[i]
                        WAIT("tensor", sACT, waited)
                    fn = (lambda e, tc=tc, i=i: e.matmul(
                        pVo0[:, 0:8], hsb[:, 512 * i + 128 * tc:512 * i + 128 * tc + 128],
                        gwb[:, 8 * i:8 * i + 8], start=(i == 0), stop=(i == ND - 1)))
                    if i == ND - 1:
                        vLg = OP("tensor", fn)
                    else:
                        MM(fn)
                WAIT("vector", sPE, vLg)
                OP("vector", lambda e: e.tensor_reduce(out=nmx[:, 0:1], in_=pVo0[:, 0:8], axis=AX.X, op=ALU.max))
                OP("vector", lambda e: e.tensor_scalar(out=nmx[:, 1:2], in0=nmx[:, 0:1],
                                                       scalar1=-1.0, scalar2=None, op0=ALU.mult))
                WAIT("scalar", sDVE, sDVE.n)
                vExp = OP("scalar", lambda e: e.activation(
                    out=routf[:], in_=pVo0[:, 0:8], func=AF.Exp, bias=nmx[:, 1:2],
                    accum_out=nmx[:, 2:3]))
                if tc == 3:
                    vOsc0 = vExp
                WAIT("vector", sACT, vExp)
                OP("vector", lambda e: e.reciprocal(out=nmx[:, 3:4], in_=nmx[:, 2:3]))
                vRb = OP("vector", lambda e: e.tensor_scalar(
                    out=routb[:], in0=routf[:], scalar1=nmx[:, 3:4], scalar2=None, op0=ALU.mult))
                WAIT("tensor", sDVE, vRb)
                if vRt:
                    WAIT("tensor", sDVE, vRt)
                vTr = OP("tensor", lambda e: e.transpose(pTb[0:8, 512:640], routb[:], idbs[:]))
                WAIT("vector", sPE, vTr)
                vRt = OP("vector", lambda e, tc=tc: e.tensor_copy(
                    out=routT[:, 128 * tc:128 * tc + 128], in_=pTb[0:8, 512:640]))

            vXs = {}
            for c in range(ND):
                psl = c % 2
                if c == 0:
                    WAIT("tensor", sDVE, vRt)
                if c >= 2:
                    WAIT("tensor", sDVE, vXs[c - 2])
                vIs = OP("tensor", lambda e, c=c, psl=psl: e.matmul(
                    pS[:, 512 * psl:512 * psl + 512], icsb[:, 128 * c:128 * c + 128],
                    routT[:], start=True, stop=True))
                WAIT("vector", sPE, vIs)
                vXs[c] = OP("vector", lambda e, c=c, psl=psl: e.tensor_mul(
                    aosb[:, 512 * c:512 * c + 512], hsb[:, 512 * c:512 * c + 512],
                    pS[:, 512 * psl:512 * psl + 512]))

            vEp = {}
            vSt = {}
            for c in range(ND):
                if tile == 0 and c == 0:
                    vWf = scr_gate(scrf)
                    WAIT("sync", sScr[0], vWf[0])
                    WAIT("sync", sScr[1], vWf[1])
                def pre_ff(c=c):
                    if c >= 2:
                        WAIT("tensor", sDVE, vEp[c - 2])
                    if c == 0:
                        WAIT("tensor", sDVE, vXs[ND - 1])
                psl = c % 2
                vF = mm_group(scrf, c, aosb, pQ0 if psl == 0 else pQ1, extra_pre=pre_ff, sign=True)
                if c >= 2:
                    WAIT("tensor", sACT, vOsc)
                vOs = OP("tensor", lambda e, c=c, psl=psl: e.matmul(
                    pS[:, 512 * psl:512 * psl + 512], ocsb[:, 128 * c:128 * c + 128],
                    routT[:], start=True, stop=True))
                WAIT("scalar", sPE, vOs)
                if c >= 2:
                    WAIT("scalar", sDVE, vEp[c - 2])
                oscr = scrC if psl == 0 else scrB
                vOsc = OP("scalar", lambda e, psl=psl, oscr=oscr: e.activation(
                    out=oscr[:], in_=pS[:, 512 * psl:512 * psl + 512], func=AF.Copy))
                WAIT("vector", sPE, vF)
                WAIT("vector", sACT, vOsc)
                WAIT("vector", sOg[0], vStSlot[0])
                OP("vector", lambda e, c=c, psl=psl, oscr=oscr: e.scalar_tensor_tensor(
                    out=scrA[:], in0=(pQ0 if psl == 0 else pQ1)[:], scalar=GBC[:, c:c + 1],
                    in1=oscr[:], op0=ALU.mult, op1=ALU.mult))
                vEp[c] = OP("vector", lambda e, c=c: e.scalar_tensor_tensor(
                    out=ostg[:, 0:512], in0=scrA[:], scalar=GBIAS[:, c:c + 1],
                    in1=xsb[:, 512 * c:512 * c + 512], op0=ALU.add, op1=ALU.add))
                WAIT("gpsimd", sDVE, vEp[c])
                DMA("gpsimd", sOg[0], outT[128 * c:128 * c + 128, t0:t0 + TT], ostg[:, 0:512])
                vSt[c] = sOg[0].n
                vStSlot[0] = sOg[0].n
                if tile < n_tiles - 1:
                    e0n = e0 + TT
                    if c == 0:
                        if vRopeLast is not None:
                            WAIT("gpsimd", sDVE, vRopeLast)
                        DMA("gpsimd", sCSg, cossb[:], cosE[:, e0n:e0n + TT])
                        DMA("gpsimd", sCSg, sinsb[:], sinE[:, e0n:e0n + TT])
                        vCSl_pref = (sCSg, sCSg.n)
                    DMA("gpsimd", sXg, xsb[:, 512 * c:512 * c + 512],
                        xT[128 * c:128 * c + 128, e0n:e0n + TT])
                    vXl_pref = (sXg, sXg.n)
            mark_scratch()

        WAIT("gpsimd", sOg[0], sOg[0].n)

        @block.sync
        def _(e):
            for fn in sched["sync"]:
                fn(e)

        @block.tensor
        def _(e):
            for fn in sched["tensor"]:
                fn(e)

        @block.vector
        def _(e):
            for fn in sched["vector"]:
                fn(e)

        @block.scalar
        def _(e):
            for fn in sched["scalar"]:
                fn(e)

        @block.gpsimd
        def _(e):
            for fn in sched["gpsimd"]:
                fn(e)

    return nc



def _rope_cache(S):
    inv_freq = (1.0 / 10000.0 ** (np.arange(0, DH, 2, dtype=np.float32) / DH)).astype(np.float32)
    t = np.arange(S, dtype=np.float32)
    f = np.outer(t, inv_freq)
    cos = np.zeros((S, DH), dtype=np.float32)
    sin = np.zeros((S, DH), dtype=np.float32)
    cos[:, 0::2] = np.cos(f)
    cos[:, 1::2] = np.cos(f)
    sin[:, 0::2] = np.sin(f)
    sin[:, 1::2] = np.sin(f)
    return cos, sin


def make_in_maps(inputs, n_cores, core_t):
    ext = core_t + 512
    nqc = core_t // 128
    nqb = core_t // 256
    nek = ext // 128
    jpc = 24576 // (128 * n_cores)
    S = n_cores * core_t

    f32 = np.float32
    x = np.asarray(inputs["L0_attn_x"], f32)[0]
    assert x.shape == (S, D), x.shape
    xTfull = np.zeros((D, S + 512), f32)
    xTfull[:, 256:256 + S] = np.ascontiguousarray(x.T)
    wq = np.ascontiguousarray(np.asarray(inputs["L0_attn_Wq"], f32).T)
    wk = np.ascontiguousarray(np.asarray(inputs["L0_attn_Wk"], f32).T)
    wv = np.ascontiguousarray(np.asarray(inputs["L0_attn_Wv"], f32).T)
    wo = np.ascontiguousarray(np.asarray(inputs["L0_attn_Wo"], f32).T)
    wf = np.ascontiguousarray(np.asarray(inputs["L0_ffn_weight"], f32).T)
    wmod = np.asarray(inputs["L0_wmod"], f32)
    cvec = np.ascontiguousarray(np.asarray(inputs["L0_c"], f32).reshape(1, D))
    gw = np.ascontiguousarray(np.asarray(inputs["L0_ffn_gate_w"], f32).T)
    ics = np.ascontiguousarray(np.asarray(inputs["L0_ffn_in_channel_scale"], f32))
    ocs = np.ascontiguousarray(np.asarray(inputs["L0_ffn_out_channel_scale"], f32))
    biasM = np.ascontiguousarray(np.asarray(inputs["L0_ffn_bias"], f32).reshape(32, 128).T)

    cos, sin = _rope_cache(S)
    idfA = np.eye(128, dtype=f32)
    idbA = np.eye(128, dtype=np.float32).astype(ml_dtypes.bfloat16)
    onesbA = np.ones((1, 128), dtype=ml_dtypes.bfloat16)
    onescA = np.ones((128, 1), dtype=f32)
    alpha = np.linspace(0.0, 1.0, 256, dtype=f32)

    maps = []
    for r in range(n_cores):
        r0 = r * core_t
        pos = np.arange(r0 - 256, r0 + core_t + 256)
        posc = np.clip(pos, 0, S - 1)
        cosEa = np.ascontiguousarray(cos[posc].T)
        sinEa = np.ascontiguousarray(sin[posc].T)
        aABa = np.zeros((128, 2 * nqc), f32)
        for qc in range(nqc):
            gqb = (r0 // 256) + qc // 2
            half = qc % 2
            asl = alpha[128 * half:128 * half + 128]
            if gqb == 0:
                aABa[:, 2 * qc] = 0.0
                aABa[:, 2 * qc + 1] = 1.0
            else:
                aABa[:, 2 * qc] = 1.0 - asl
                aABa[:, 2 * qc + 1] = asl
        m3a = np.ones((128, nqb), f32)
        for qb in range(nqb):
            if (r0 // 256) + qb == S // 256 - 1:
                m3a[:, qb] = 0.0
        vma = np.ones((128, nek), f32)
        for kb in range(nek):
            glob = r0 - 256 + 128 * kb
            if glob < 0 or glob >= S:
                vma[:, kb] = 0.0
        maps.append({
            "xT": np.ascontiguousarray(xTfull[:, r0:r0 + ext]),
            "wqT": wq, "woT": wo, "wfT": wf, "wkT": wk, "wvT": wv,
            "wmodS": np.ascontiguousarray(wmod[:, 128 * jpc * r:128 * jpc * (r + 1)]),
            "cvec": cvec, "gwT": gw, "icsM": ics, "ocsM": ocs, "biasM": biasM,
            "cosE": cosEa.astype(ml_dtypes.bfloat16), "sinE": sinEa.astype(ml_dtypes.bfloat16), "aABm": aABa, "m3m": m3a,
            "vmkm": vma, "idf": idfA, "onesbI": onesbA, "onescI": onescA, "idbI": idbA,
        })
    return maps


def kernel(**inputs):
    n_cores, core_t = 8, 2048
    maps = make_in_maps(inputs, n_cores, core_t)
    nc = build_nc(n_cores, core_t)
    res = run_bass_kernel_spmd(nc, maps, core_ids=list(range(n_cores)))
    full = np.concatenate([np.asarray(r["outT"]) for r in res.results], axis=1)
    return np.ascontiguousarray(full.T).reshape(1, n_cores * core_t, D).astype(np.float32)


# version 23 (final)
# speedup vs baseline: 1.3502x; 1.0347x over previous
import numpy as np
from contextlib import ExitStack
import ml_dtypes
import concourse.bass as bass
import concourse.mybir as mybir
from concourse.bass_utils import run_bass_kernel_spmd

F32 = mybir.dt.float32
BF16 = mybir.dt.bfloat16
AF = mybir.ActivationFunctionType
ALU = mybir.AluOpType
AX = mybir.AxisListType

D = 4096
ND = 32
DH = 128
H = 32
SCALE = float(1.0 / np.sqrt(128.0))
EPS = 1e-5
TT = 512


class Ct:
    __slots__ = ("h", "n")
    def __init__(self, h):
        self.h = h
        self.n = 0


def build_nc(n_cores, core_t):
    ext = core_t + 512
    n_tiles = core_t // TT
    n_ext = ext // TT
    nqc = core_t // 128
    nqb = core_t // 256
    nek = ext // 128
    jpc = 24576 // (128 * n_cores)
    n_jb = (jpc + 23) // 24

    nc = bass.Bass("TRN2", target_bir_lowering=False, debug=False)

    ap = lambda t: t.ap()
    xT = ap(nc.dram_tensor("xT", [D, ext], F32, kind="ExternalInput"))
    wqT = ap(nc.dram_tensor("wqT", [D, D], F32, kind="ExternalInput"))
    woT = ap(nc.dram_tensor("woT", [D, D], F32, kind="ExternalInput"))
    wfT = ap(nc.dram_tensor("wfT", [D, D], F32, kind="ExternalInput"))
    wkT = ap(nc.dram_tensor("wkT", [D, DH], F32, kind="ExternalInput"))
    wvT = ap(nc.dram_tensor("wvT", [D, DH], F32, kind="ExternalInput"))
    wmodS = ap(nc.dram_tensor("wmodS", [D, 128 * jpc], F32, kind="ExternalInput"))
    cvec = ap(nc.dram_tensor("cvec", [1, D], F32, kind="ExternalInput"))
    gwT = ap(nc.dram_tensor("gwT", [D, 8], F32, kind="ExternalInput"))
    icsM = ap(nc.dram_tensor("icsM", [8, D], F32, kind="ExternalInput"))
    ocsM = ap(nc.dram_tensor("ocsM", [8, D], F32, kind="ExternalInput"))
    biasM = ap(nc.dram_tensor("biasM", [128, 32], F32, kind="ExternalInput"))
    cosE = ap(nc.dram_tensor("cosE", [128, ext], BF16, kind="ExternalInput"))
    sinE = ap(nc.dram_tensor("sinE", [128, ext], BF16, kind="ExternalInput"))
    aABm = ap(nc.dram_tensor("aABm", [128, 2 * nqc], F32, kind="ExternalInput"))
    m3m = ap(nc.dram_tensor("m3m", [128, nqb], F32, kind="ExternalInput"))
    vmkm = ap(nc.dram_tensor("vmkm", [128, nek], F32, kind="ExternalInput"))
    idf = ap(nc.dram_tensor("idf", [128, 128], F32, kind="ExternalInput"))
    onesbI = ap(nc.dram_tensor("onesbI", [1, 128], BF16, kind="ExternalInput"))
    onescI = ap(nc.dram_tensor("onescI", [128, 1], F32, kind="ExternalInput"))
    idbI = ap(nc.dram_tensor("idbI", [128, 128], BF16, kind="ExternalInput"))

    scrq = ap(nc.dram_tensor("scrq", [32, 128, 32, 128], BF16, kind="Internal"))
    scro = ap(nc.dram_tensor("scro", [32, 128, 32, 128], BF16, kind="Internal"))
    scrf = ap(nc.dram_tensor("scrf", [32, 128, 32, 128], BF16, kind="Internal"))
    ccin = ap(nc.dram_tensor("ccin", [jpc, 128], F32, kind="Internal"))
    ccout = ap(nc.dram_tensor("ccout", [jpc * n_cores, 128], F32, kind="Internal"))

    outT = ap(nc.dram_tensor("outT", [D, core_t], F32, kind="ExternalOutput"))

    sched = {k: [] for k in ("sync", "tensor", "vector", "scalar", "gpsimd")}

    def em(eng, fn):
        sched[eng].append(fn)

    def WAIT(eng, ct, v):
        if v > 0:
            em(eng, lambda e, h=ct.h, v=v: e.wait_ge(h, v))

    def WAITV(eng, tv):
        WAIT(eng, tv[0], tv[1])

    def DMA(eng, ct, out, in_, **kw):
        ct.n += 16
        em(eng, lambda e, o=out, i=in_, h=ct.h, kw=dict(kw): e.dma_start(out=o, in_=i, **kw).then_inc(h, 16))
        return ct.n

    def TDMA(ct, out, in_):
        ct.n += 16
        em("scalar", lambda e, o=out, i=in_, h=ct.h: e.dma_start(out=o, in_=i, transpose=True).then_inc(h, 16))
        return ct.n

    ENG_CT = {}

    def OP(eng, fn):
        ct = ENG_CT[eng]
        if eng != "tensor" and ct.n > 0:
            em(eng, lambda e, h=ct.h, v=ct.n: e.wait_ge(h, v))
        ct.n += 1
        em(eng, lambda e, fn=fn, h=ct.h: fn(e).then_inc(h, 1))
        return ct.n

    def MM(fn):
        em("tensor", lambda e, fn=fn: fn(e))

    with ExitStack() as _st:
        ent = _st.enter_context
        xsb = ent(nc.sbuf_tensor("xsb", [128, ND * TT], F32))
        hsb = ent(nc.sbuf_tensor("hsb", [128, ND * TT], BF16))
        aosb = ent(nc.sbuf_tensor("aosb", [128, ND * TT], BF16))
        kres = ent(nc.sbuf_tensor("kres", [128, ext], BF16))
        vtm = ent(nc.sbuf_tensor("vtm", [128, (ext // 128) * 129], BF16))
        cossb = ent(nc.sbuf_tensor("cossb", [128, TT], BF16))
        sinsb = ent(nc.sbuf_tensor("sinsb", [128, TT], BF16))
        qslot = ent(nc.sbuf_tensor("qslot", [128, 1024], BF16))
        sig = ent(nc.sbuf_tensor("sig", [128, 1536], BF16))
        psiT = ent(nc.sbuf_tensor("psiT", [128, 512], BF16, is_transpose_dest=True))
        scrA = ent(nc.sbuf_tensor("scrA", [128, 512], F32))
        scrB = ent(nc.sbuf_tensor("scrB", [128, 512], F32))
        scrC = ent(nc.sbuf_tensor("scrC", [128, 512], F32))
        scrD = ent(nc.sbuf_tensor("scrD", [128, 512], F32))
        mrow = ent(nc.sbuf_tensor("mrow", [1, core_t], BF16))
        rrow = ent(nc.sbuf_tensor("rrow", [1, core_t], BF16))
        rowsb = ent(nc.sbuf_tensor("rowsb", [1, 1024], BF16))
        ostg = ent(nc.sbuf_tensor("ostg", [128, 512], F32))
        wslA = ent(nc.sbuf_tensor("wslA", [128, 2048], BF16))
        wslB = ent(nc.sbuf_tensor("wslB", [128, 2048], BF16))
        pp0 = ent(nc.sbuf_tensor("pp0", [128, 1024], BF16))
        pp1 = ent(nc.sbuf_tensor("pp1", [128, 1024], BF16))
        wslC = ent(nc.sbuf_tensor("wslC", [128, 2048], BF16))
        wslD = ent(nc.sbuf_tensor("wslD", [128, 2048], BF16))
        icsb = ent(nc.sbuf_tensor("icsb", [8, D], BF16))
        ocsb = ent(nc.sbuf_tensor("ocsb", [8, D], BF16))
        gwb = ent(nc.sbuf_tensor("gwb", [128, 256], BF16))
        idfs = ent(nc.sbuf_tensor("idfs", [128, 128], F32))
        idbs = ent(nc.sbuf_tensor("idbs", [128, 128], BF16))
        onesb = ent(nc.sbuf_tensor("onesb", [1, 128], BF16))
        onesc = ent(nc.sbuf_tensor("onesc", [128, 1], F32))
        aAB = ent(nc.sbuf_tensor("aAB", [128, 2 * nqc], F32))
        m3s = ent(nc.sbuf_tensor("m3s", [128, nqb], F32))
        vmk = ent(nc.sbuf_tensor("vmk", [128, nek], F32))
        biasC = ent(nc.sbuf_tensor("biasC", [128, 32], F32))
        scol32 = ent(nc.sbuf_tensor("scol32", [128, 32], F32))
        camod = ent(nc.sbuf_tensor("camod", [128, 224], F32))
        sums = ent(nc.sbuf_tensor("sums", [128, 16], F32))
        coefs = ent(nc.sbuf_tensor("coefs", [128, 8], F32))
        nmx = ent(nc.sbuf_tensor("nmx", [128, 4], F32))
        pQ0 = ent(nc.psum_tensor("pQ0", [128, 512], F32))
        pQ1 = ent(nc.psum_tensor("pQ1", [128, 512], F32))
        pS = ent(nc.psum_tensor("pS", [128, 1024], F32))
        pVo0 = ent(nc.psum_tensor("pVo0", [128, 512], F32))
        pVo1 = ent(nc.psum_tensor("pVo1", [128, 512], F32))
        pTb = ent(nc.psum_tensor("pTb", [128, 1024], BF16))
        pTb2 = ent(nc.psum_tensor("pTb2", [128, 1024], BF16))
        _sC = ent(nc.semaphore("smC"))
        _sX = ent(nc.semaphore("smX"))
        _sCS = ent(nc.semaphore("smCS"))
        _sSlA = ent(nc.semaphore("smSlA"))
        _sSlB = ent(nc.semaphore("smSlB"))
        _sGA = ent(nc.semaphore("smGA"))
        _sGB = ent(nc.semaphore("smGB"))
        _sScr = ent(nc.semaphore("smScr"))
        _sScrB = ent(nc.semaphore("smScrB"))
        _sCc = ent(nc.semaphore("smCc"))
        _sCo = ent(nc.semaphore("smCo"))
        _sOutB = ent(nc.semaphore("smOutB"))
        _sGc = ent(nc.semaphore("smGc"))
        _sPpA = ent(nc.semaphore("smPpA"))
        _sPpB = ent(nc.semaphore("smPpB"))
        _sGP = ent(nc.semaphore("smGP"))
        _sOgA = ent(nc.semaphore("smOgA"))
        _sOgB = ent(nc.semaphore("smOgB"))
        _sXg = ent(nc.semaphore("smXg"))
        _sCSg = ent(nc.semaphore("smCSg"))
        _sSlC = ent(nc.semaphore("smSlC"))
        _sSlD = ent(nc.semaphore("smSlD"))
        _sGq0 = ent(nc.semaphore("smGq0"))
        _sGq1 = ent(nc.semaphore("smGq1"))
        _sGq2 = ent(nc.semaphore("smGq2"))
        _sGq3 = ent(nc.semaphore("smGq3"))
        _sTpA = ent(nc.semaphore("smTpA"))
        _sTpB = ent(nc.semaphore("smTpB"))
        _sOut = ent(nc.semaphore("smOut"))
        _sCC = ent(nc.semaphore("smCC"))
        _sPE = ent(nc.semaphore("smPE"))
        _sACT = ent(nc.semaphore("smACT"))
        _sDVE = ent(nc.semaphore("smDVE"))
        block = ent(nc.Block())
        sC, sX, sCS = Ct(_sC), Ct(_sX), Ct(_sCS)
        sSl = [Ct(_sSlA), Ct(_sSlB)]
        sG = [Ct(_sGA), Ct(_sGB)]
        sScr = [Ct(_sScr), Ct(_sScrB)]
        sCc, sCo, sGc = Ct(_sCc), Ct(_sCo), Ct(_sGc)
        sPp = [Ct(_sPpA), Ct(_sPpB)]
        sGP = Ct(_sGP)
        sOg = [Ct(_sOgA), Ct(_sOgB)]
        sXg, sCSg = Ct(_sXg), Ct(_sCSg)
        sSl3 = [sSl[0], sSl[1], Ct(_sSlC), Ct(_sSlD)]
        sGq = [Ct(_sGq0), Ct(_sGq1), Ct(_sGq2), Ct(_sGq3)]
        sTp = [Ct(_sTpA), Ct(_sTpB)]
        sOut = [Ct(_sOut), Ct(_sOutB)]
        sCC = Ct(_sCC)
        sPE, sACT, sDVE = Ct(_sPE), Ct(_sACT), Ct(_sDVE)
        ENG_CT.update({"tensor": sPE, "vector": sDVE, "scalar": sACT, "gpsimd": sGP})

        CA1, SAv, GAC, CB1, SBv, GBC, GBIAS = (camod[:, 32 * i:32 * i + 32] for i in range(7))

        modpart = scrA[:, 384:408]
        routb = sig[:, 0:8]
        routT = psiT[0:8, 0:512]
        modcol = scrB[:, 0:192]
        routf = coefs[:, 0:8]

        scrUse = {"a": 0, "v": 0, "t": 0}

        def mark_scratch():
            scrUse["a"], scrUse["v"], scrUse["t"] = sACT.n, sDVE.n, sPE.n

        def wait_scratch(eng):
            if eng != "scalar":
                WAIT(eng, sACT, scrUse["a"])
            if eng != "vector":
                WAIT(eng, sDVE, scrUse["v"])
            if eng != "tensor":
                WAIT(eng, sPE, scrUse["t"])

        DMA("sync", sC, idfs[:], idf)
        DMA("sync", sC, idbs[:], idbI)
        DMA("sync", sC, onesb[:], onesbI)
        DMA("sync", sC, onesc[:], onescI)
        DMA("sync", sC, aAB[:], aABm)
        DMA("sync", sC, m3s[:], m3m)
        DMA("sync", sC, vmk[:], vmkm)
        DMA("sync", sC, biasC[:], biasM)
        DMA("sync", sC, scrB[0:32, 0:128], cvec.rearrange("o (a b) -> (o a) b", b=128))
        vC = sC.n
        DMA("gpsimd", sGc, gwb[:].rearrange("p (b e) -> p b e", e=8),
            gwT.rearrange("(b p) e -> p b e", p=128))
        DMA("gpsimd", sGc, icsb[:], icsM)
        DMA("gpsimd", sGc, ocsb[:], ocsM)
        DMA("gpsimd", sGc, aosb[:, 0:4096].rearrange("p (b f) -> p b f", f=128),
            wkT.rearrange("(b p) f -> p b f", p=128))
        DMA("gpsimd", sGc, aosb[:, 4096:8192].rearrange("p (b f) -> p b f", f=128),
            wvT.rearrange("(b p) f -> p b f", p=128))
        vGsmall = sGc.n
        DMA("sync", sX, xsb[:, 0:8192].rearrange("p (b t) -> p b t", t=TT),
            xT[0:2048, 0:TT].rearrange("(b p) t -> p b t", p=128))
        DMA("sync", sX, xsb[:, 14336:16384].rearrange("p (b t) -> p b t", t=TT),
            xT[3584:4096, 0:TT].rearrange("(b p) t -> p b t", p=128))
        DMA("sync", sCS, cossb[:], cosE[:, 0:TT])
        DMA("sync", sCS, sinsb[:], sinE[:, 0:TT])
        vCSl0 = sCS.n

        WAIT("scalar", sC, vC)
        vSilu = OP("scalar", lambda e: e.activation(out=scrB[0:32, 128:256], in_=scrB[0:32, 0:128], func=AF.Silu))
        WAIT("tensor", sACT, vSilu)
        WAIT("tensor", sC, vC)
        vTs = OP("tensor", lambda e: e.transpose(pQ1[:, 0:32], scrB[0:32, 128:256], idfs[0:32, 0:32]))
        WAIT("vector", sPE, vTs)
        vScol = OP("vector", lambda e: e.tensor_copy(out=scol32[:], in_=pQ1[:, 0:32]))

        mod_evac = 0
        peW = {0: 0, 1: 0}
        for jb in range(n_jb):
            j0 = jb * 24
            jn = min(24, jpc - j0)
            for i in range(ND):
                sl = i % 2
                xbuf = xsb[:, 8192 + 3072 * sl:8192 + 3072 * sl + 3072]
                WAIT("gpsimd", sPE, peW[sl])
                DMA("gpsimd", sG[sl], xbuf[:, 0:128 * jn],
                    wmodS[128 * i:128 * i + 128, 128 * j0:128 * (j0 + jn)])
                if jb == 0 and i == 0:
                    WAIT("tensor", sDVE, vScol)
                WAIT("tensor", sG[sl], sG[sl].n)
                for j in range(jn):
                    fn = (lambda e, xbuf=xbuf, j=j, i=i:
                          e.matmul(pS[:, (32 * j + i):(32 * j + i + 1)],
                                   xbuf[:, 128 * j:128 * j + 128],
                                   scol32[:, i:i + 1], start=True, stop=True))
                    if j == jn - 1:
                        vmm = OP("tensor", fn)
                    else:
                        MM(fn)
                peW[sl] = vmm
            for j in range(jn):
                if j == 0:
                    WAIT("vector", sPE, vmm)
                vred = OP("vector", lambda e, j=j: e.tensor_reduce(
                    out=modpart[:, j:j + 1], in_=pS[:, 32 * j:32 * j + 32], axis=AX.X, op=ALU.add))
            WAIT("tensor", sDVE, vred)
            vTp = OP("tensor", lambda e, jn=jn: e.transpose(pQ0[0:jn, 0:128], modpart[:, 0:jn], idfs[:]))
            WAIT("vector", sPE, vTp)
            WAIT("vector", sCc, mod_evac)
            vEv = OP("vector", lambda e, jn=jn: e.tensor_copy(out=scrA[0:jn, 256:384], in_=pQ0[0:jn, 0:128]))
            WAIT("sync", sDVE, vEv)
            DMA("sync", sCc, ccin[j0:j0 + jn, :], scrA[0:jn, 256:384])
            mod_evac = sCc.n

        WAIT("sync", sPE, vmm)
        DMA("sync", sX, xsb[:, 8192:14336].rearrange("p (b t) -> p b t", t=TT),
            xT[2048:3584, 0:TT].rearrange("(b p) t -> p b t", p=128))
        vXl0 = sX.n
        WAIT("gpsimd", sCc, mod_evac)
        em("gpsimd", lambda e: e.collective_compute(
            "AllGather", ALU.bypass, replica_groups=[list(range(n_cores))],
            ins=[ccin], outs=[ccout]).then_inc(sCC.h, 1))
        sCC.n += 1
        WAIT("sync", sCC, sCC.n)
        R = jpc * n_cores
        g0, tI = 0, 0
        vT2 = 0
        while g0 < R:
            gn = min(96, R - g0)
            WAIT("sync", sPE, vT2)
            DMA("sync", sCo, scrA[0:gn, 0:128], ccout[g0:g0 + gn, :])
            WAIT("tensor", sCo, sCo.n)
            vT2 = OP("tensor", lambda e, gn=gn, tI=tI: e.transpose(
                pQ1[0:128, 128 * tI:128 * tI + gn], scrA[0:gn, 0:128], idfs[0:gn, 0:gn]))
            WAIT("vector", sPE, vT2)
            vMc = OP("vector", lambda e, gn=gn, g0=g0, tI=tI: e.tensor_copy(
                out=modcol[:, g0:g0 + gn], in_=pQ1[:, 128 * tI:128 * tI + gn]))
            g0 += gn
            tI += 1
        OP("vector", lambda e: e.tensor_copy(out=SAv, in_=modcol[:, 0:32]))
        OP("vector", lambda e: e.tensor_scalar(out=CA1, in0=modcol[:, 32:64], scalar1=1.0, scalar2=None, op0=ALU.add))
        OP("vector", lambda e: e.tensor_copy(out=SBv, in_=modcol[:, 96:128]))
        OP("vector", lambda e: e.tensor_scalar(out=CB1, in0=modcol[:, 128:160], scalar1=1.0, scalar2=None, op0=ALU.add))
        for (ga_s, dst) in ((64, GAC), (160, GBC)):
            g = modcol[:, ga_s:ga_s + 32]
            OP("vector", lambda e, g=g: e.tensor_scalar(out=scrC[:, 0:32], in0=g, scalar1=0.0, scalar2=None, op0=ALU.is_ge))
            OP("vector", lambda e, g=g: e.tensor_scalar(out=scrC[:, 32:64], in0=g, scalar1=0.25, scalar2=None, op0=ALU.max))
            OP("vector", lambda e, g=g: e.tensor_scalar(out=scrC[:, 64:96], in0=g, scalar1=-0.25, scalar2=None, op0=ALU.min))
            OP("vector", lambda e: e.tensor_sub(scrC[:, 96:128], scrC[:, 32:64], scrC[:, 64:96]))
            OP("vector", lambda e: e.tensor_mul(scrC[:, 96:128], scrC[:, 96:128], scrC[:, 0:32]))
            OP("vector", lambda e, dst=dst: e.tensor_add(dst, scrC[:, 96:128], scrC[:, 64:96]))
        WAIT("vector", sC, vC)
        vMod = OP("vector", lambda e: e.tensor_mul(GBIAS, GBC, biasC[:]))
        mark_scratch()

        pp = [pp0, pp1]
        prep_state = {"prevWr": {0: 0, 1: 0}, "qidx": 0}
        prep_items = []
        scr_done = {}

        def mk_prep_item(wsrc, scr, do_sign, i, q, last):
            def emit():
                sl = prep_state["qidx"] % 2
                prep_state["qidx"] += 1
                WAIT("gpsimd", sScr[sl], prep_state["prevWr"][sl])
                DMA("gpsimd", sPp[sl], pp[sl][:], wsrc[128 * i:128 * i + 128, 1024 * q:1024 * q + 1024])
                WAIT("gpsimd", sPp[sl], sPp[sl].n)
                DMA("gpsimd", sScr[sl], scr[8 * q:8 * q + 8, :, i, :].rearrange("c p f -> p c f"),
                    pp[sl][:].rearrange("p (c f) -> p c f", f=128))
                prep_state["prevWr"][sl] = sScr[sl].n
                if last:
                    scr_done[id(scr)] = (sScr[0].n, sScr[1].n)
            return emit

        def mat_items(wsrc, scr, do_sign):
            return [mk_prep_item(wsrc, scr, do_sign, i, q, i == ND - 1 and q == 3)
                    for i in range(ND) for q in range(4)]

        prep_items.extend(mat_items(wfT, scrf, False))

        def weave(n):
            for _ in range(min(n, len(prep_items))):
                prep_items.pop(0)()

        def scr_gate(scr):
            v = scr_done.get(id(scr))
            if v is None:
                while prep_items and id(scr) not in scr_done:
                    prep_items.pop(0)()
                v = scr_done[id(scr)]
            return v

        def ln_stats(src_f32, mdst, rdst, extra_rows=None):
            wait_scratch("vector")
            wait_scratch("scalar")
            v = OP("vector", lambda e: e.tensor_copy(out=scrA[:], in_=src_f32[:, 0:512]))
            aq = OP("scalar", lambda e: e.activation(out=scrB[:], in_=src_f32[:, 0:512], func=AF.Square))
            vadd2 = 0
            for cix in range(1, ND):
                ch = src_f32[:, 512 * cix:512 * cix + 512]
                OP("vector", lambda e, ch=ch: e.tensor_add(scrA[:], scrA[:], ch))
                if vadd2:
                    WAIT("scalar", sDVE, vadd2)
                aq = OP("scalar", lambda e, ch=ch: e.activation(out=scrC[:], in_=ch, func=AF.Square))
                WAIT("vector", sACT, aq)
                vadd2 = OP("vector", lambda e: e.tensor_add(scrB[:], scrB[:], scrC[:]))
            WAIT("tensor", sDVE, vadd2)
            return ln_tail(mdst, rdst, extra_rows=extra_rows)

        def ln_tail(mdst, rdst, vdep=0, extra_rows=None):
            wait_scratch("tensor")
            WAIT("tensor", sDVE, vdep)
            OP("tensor", lambda e: e.matmul(pVo1[0:1, 0:512], onesc[:], scrA[:], start=True, stop=True))
            vmm = OP("tensor", lambda e: e.matmul(pVo0[0:1, 0:512], onesc[:], scrB[:], start=True, stop=True))
            WAIT("vector", sPE, vmm)
            OP("vector", lambda e: e.tensor_scalar(out=scrC[0:1, 0:512], in0=pVo1[0:1, 0:512],
                                                   scalar1=1.0 / D, scalar2=None, op0=ALU.mult))
            OP("vector", lambda e: e.tensor_scalar(out=scrB[0:1, 0:512], in0=pVo0[0:1, 0:512],
                                                   scalar1=1.0 / D, scalar2=None, op0=ALU.mult))
            OP("vector", lambda e: e.tensor_mul(scrA[0:1, 0:512], scrC[0:1, 0:512], scrC[0:1, 0:512]))
            v = OP("vector", lambda e: e.scalar_tensor_tensor(
                out=scrB[0:1, 0:512], in0=scrB[0:1, 0:512], scalar=EPS, in1=scrA[0:1, 0:512],
                op0=ALU.add, op1=ALU.subtract))
            WAIT("scalar", sDVE, v)
            aS = OP("scalar", lambda e: e.activation(out=scrA[0:1, 0:512], in_=scrB[0:1, 0:512], func=AF.Sqrt))
            WAIT("vector", sACT, aS)
            OP("vector", lambda e: e.reciprocal(out=scrB[0:1, 0:512], in_=scrA[0:1, 0:512]))
            OP("vector", lambda e, mdst=mdst: e.tensor_copy(out=mdst, in_=scrC[0:1, 0:512]))
            v = OP("vector", lambda e, rdst=rdst: e.tensor_copy(out=rdst, in_=scrB[0:1, 0:512]))
            if extra_rows is not None:
                lo, hi, own0 = extra_rows
                OP("vector", lambda e, lo=lo, hi=hi, own0=own0: e.tensor_copy(
                    out=mrow[0:1, own0:own0 + hi - lo], in_=scrC[0:1, lo:hi]))
                v = OP("vector", lambda e, lo=lo, hi=hi, own0=own0: e.tensor_copy(
                    out=rrow[0:1, own0:own0 + hi - lo], in_=scrB[0:1, lo:hi]))
            return v

        def ln_bcast(mr, rr, vrows=0):
            wait_scratch("tensor")
            WAIT("tensor", sDVE, vrows)
            OP("tensor", lambda e, mr=mr: e.matmul(pS[:, 0:512], onesb[:], mr, start=True, stop=True))
            return OP("tensor", lambda e, rr=rr: e.matmul(pS[:, 512:1024], onesb[:], rr, start=True, stop=True))

        def ln_apply(src_f32, dst_bf, c1col, shcol, vdep_pe):
            lastAs = []
            for cix in range(ND):
                ch = src_f32[:, 512 * cix:512 * cix + 512]
                dch = dst_bf[:, 512 * cix:512 * cix + 512]
                pa, pb = (scrA, scrB) if cix % 2 == 0 else (scrC, scrD)
                if cix == 0:
                    WAIT("vector", sPE, vdep_pe)
                    wait_scratch("vector")
                if cix >= 2:
                    WAIT("vector", sACT, lastAs[cix - 2])
                OP("vector", lambda e, ch=ch, pa=pa: e.tensor_sub(pa[:], ch, pS[:, 0:512]))
                vv = OP("vector", lambda e, pa=pa, pb=pb: e.tensor_mul(pb[:], pa[:], pS[:, 512:1024]))
                WAIT("scalar", sDVE, vv)
                lastAs.append(OP("scalar", lambda e, dch=dch, cix=cix, pb=pb: e.activation(
                    out=dch, in_=pb[:], func=AF.Identity,
                    bias=shcol[:, cix:cix + 1], scale=c1col[:, cix:cix + 1])))
            return lastAs

        vFree = [0, 0, 0, 0]
        lastVevac = 0
        lastHprev = 0
        for et in range(n_ext):
            t0 = et * TT
            if et > 0:
                WAIT("sync", sACT, lastHprev)
                DMA("sync", sX, xsb[:].rearrange("p (b t) -> p b t", t=TT),
                    xT[:, t0:t0 + TT].rearrange("(b p) t -> p b t", p=128))
                vXl = sX.n
                WAIT("sync", sDVE, sDVE.n)
                DMA("sync", sCS, cossb[:], cosE[:, t0:t0 + TT])
                DMA("sync", sCS, sinsb[:], sinE[:, t0:t0 + TT])
                vCSl = sCS.n
            else:
                vXl = vXl0
                vCSl = vCSl0
            WAIT("vector", sX, vXl)
            WAIT("scalar", sX, vXl)
            own_lo = max(0, 256 - t0)
            own_hi = min(512, 256 + core_t - t0)
            vRows = ln_stats(xsb, rowsb[0:1, 0:512], rowsb[0:1, 512:1024],
                             extra_rows=(own_lo, own_hi, t0 - 256 + own_lo))
            vB = ln_bcast(rowsb[0:1, 0:512], rowsb[0:1, 512:1024], vRows)
            lastHs = ln_apply(xsb, hsb, CA1, SAv, vB)
            lastH = lastHs[-1]
            lastHprev = lastH
            vGrp = {}
            if et == 0:
                WAIT("tensor", sGc, vGsmall)
            for which, wbase, pdst in (("k", 0, pQ0), ("v", 4096, pQ1)):
                waited = 0
                for i in range(ND):
                    if lastHs[i] > waited:
                        waited = lastHs[i]
                        WAIT("tensor", sACT, waited)
                    fn = (lambda e, wbase=wbase, i=i, pdst=pdst: e.matmul(
                        pdst[:], aosb[:, wbase + 128 * i:wbase + 128 * i + 128],
                        hsb[:, 512 * i:512 * i + 512], start=(i == 0), stop=(i == ND - 1)))
                    if i == ND - 1:
                        vg = OP("tensor", fn)
                    else:
                        MM(fn)
                vGrp[which] = vg
            WAIT("vector", sPE, vGrp["k"])
            WAIT("vector", sCS, vCSl)
            OP("vector", lambda e: e.tensor_mul(scrA[:], pQ0[:], cossb[:]))
            OP("vector", lambda e: e.tensor_mul(scrB[0:64, :], pQ0[64:128, :], sinsb[0:64, :]))
            OP("vector", lambda e: e.tensor_mul(scrB[64:128, :], pQ0[0:64, :], sinsb[64:128, :]))
            OP("vector", lambda e, t0=t0: e.tensor_sub(kres[0:64, t0:t0 + TT], scrA[0:64, :], scrB[0:64, :]))
            OP("vector", lambda e, t0=t0: e.tensor_add(kres[64:128, t0:t0 + TT], scrA[64:128, :], scrB[64:128, :]))
            WAIT("scalar", sPE, vGrp["v"])
            vVc = OP("scalar", lambda e: e.activation(out=sig[:, 0:512], in_=pQ1[:], func=AF.Copy))
            WAIT("tensor", sACT, vVc)
            WAIT("tensor", sDVE, lastVevac)
            for j in range(4):
                vTt = OP("tensor", lambda e, j=j: e.transpose(
                    pTb[:, 128 * j:128 * j + 128], sig[:, 128 * j:128 * j + 128], idbs[:]))
            WAIT("vector", sPE, vTt)
            for j in range(4):
                kb = 4 * et + j
                lastVevac = OP("vector", lambda e, j=j, kb=kb: e.tensor_scalar(
                    out=vtm[:, 129 * kb:129 * kb + 128], in0=pTb[:, 128 * j:128 * j + 128],
                    scalar1=vmk[:, kb:kb + 1], scalar2=None, op0=ALU.mult))
            if et == n_ext - 1:
                lastVevac = OP("vector", lambda e: e.tensor_copy(
                    out=vtm[:].rearrange("p (b c) -> p b c", c=129)[:, :, 128:129],
                    in_=vmk[:].rearrange("p (b c) -> p b c", c=1)))
            mark_scratch()
            weave(26)

        WSL3 = [wslA, wslB, wslC, wslD]
        mmg_state = {"hidx": 0}
        vSigACT = {}
        vTpPE = {}
        vOmm = {}
        vBlend = {}
        vEvac = {}
        uG = 0
        uTile = {}

        def emit_O(u):
            h_ = (u % 128) // 4
            qc_ = u % 4
            tl = uTile[u]
            lqb = 2 * tl + qc_ // 2
            slot = u % 2
            WAIT("tensor", sACT, vSigACT[u])
            if u >= 1:
                WAIT("tensor", sDVE, vBlend[u - 1])
            for grp, (pO, c0) in enumerate(((pVo0, 0), (pVo1, 2))):
                for cx in range(c0, c0 + 4):
                    kb = 2 * lqb + cx
                    fn = (lambda e, pO=pO, cx=cx, kb=kb, slot=slot, c0=c0: e.matmul(
                        pO[:, 0:129], sig[:, 768 * slot + 128 * cx:768 * slot + 128 * cx + 128],
                        vtm[:, 129 * kb:129 * kb + 129],
                        start=(cx == c0), stop=(cx == c0 + 3)))
                    if cx == c0 + 3:
                        vOmm[u] = OP("tensor", fn)
                    else:
                        MM(fn)

        def emit_blend(u):
            tl = uTile[u]
            qc_ = u % 4
            lqc = 4 * tl + qc_
            slot = u % 2
            sm = sums[:, 8 * slot:8 * slot + 8]
            cf = coefs[:, 4 * slot:4 * slot + 4]
            WAIT("vector", sPE, vOmm[u])
            OP("vector", lambda e, sm=sm: e.reciprocal(out=sm[:, 0:1], in_=pVo0[:, 128:129]))
            OP("vector", lambda e, sm=sm: e.reciprocal(out=sm[:, 1:2], in_=pVo1[:, 128:129]))
            OP("vector", lambda e, sm=sm, cf=cf, lqc=lqc: e.tensor_mul(
                cf[:, 0:2], sm[:, 0:2], aAB[:, 2 * lqc:2 * lqc + 2]))
            OP("vector", lambda e, cf=cf: e.tensor_scalar(
                out=scrC[:, 0:128], in0=pVo0[:, 0:128], scalar1=cf[:, 0:1], scalar2=None, op0=ALU.mult))
            vBlend[u] = OP("vector", lambda e, cf=cf, slot=slot: e.scalar_tensor_tensor(
                out=psiT[:, 128 * slot:128 * slot + 128], in0=pVo1[:, 0:128], scalar=cf[:, 1:2],
                in1=scrC[:, 0:128], op0=ALU.mult, op1=ALU.add))

        def emit_otrans(u):
            h_ = (u % 128) // 4
            qc_ = u % 4
            slot = u % 2
            pT = pTb if slot == 0 else pTb2
            WAIT("tensor", sDVE, vBlend[u])
            if u >= 2:
                WAIT("tensor", sACT, vEvac[u - 2][1])
            vTpPE[u] = OP("tensor", lambda e, pT=pT, slot=slot: e.transpose(
                pT[:, 0:128], psiT[:, 128 * slot:128 * slot + 128], idbs[:]))
            WAIT("scalar", sPE, vTpPE[u])
            vEvac[u] = (sACT, OP("scalar", lambda e, h_=h_, qc_=qc_, pT=pT: e.activation(
                out=aosb[:, 512 * h_ + 128 * qc_:512 * h_ + 128 * qc_ + 128],
                in_=pT[:, 0:128], func=AF.Copy)))

        def mm_group(wsrc, c, rhs_sb, pdst, extra_pre=None, rhs_ready=None, sign=False,
                     direct=False):
            if extra_pre:
                extra_pre()
            vh = []
            slots = []
            vsg = [0, 0]
            for half in range(2):
                sl = mmg_state["hidx"] % 4
                mmg_state["hidx"] += 1
                slots.append(sl)
                wdst = WSL3[sl][:].rearrange("p (b f) -> p b f", f=128)
                if direct:
                    WAIT("gpsimd", sPE, vFree[sl])
                    DMA("gpsimd", sGq[sl], wdst,
                        wsrc[2048 * half:2048 * half + 2048, 128 * c:128 * c + 128]
                        .rearrange("(b p) f -> p b f", p=128))
                    vh.append((sGq[sl], sGq[sl].n))
                else:
                    WAIT("sync", sPE, vFree[sl])
                    DMA("sync", sSl3[sl], wdst, wsrc[c, :, 16 * half:16 * half + 16, :])
                    vh.append((sSl3[sl], sSl3[sl].n))
                if sign:
                    WAITV("vector", vh[half])
                    WAIT("vector", sPE, vFree[sl])
                    OP("vector", lambda e, sl=sl: e.tensor_scalar(
                        out=WSL3[sl][:], in0=WSL3[sl][:], scalar1=1e38, scalar2=None, op0=ALU.mult))
                    vsg[half] = OP("vector", lambda e, sl=sl: e.tensor_scalar(
                        out=WSL3[sl][:], in0=WSL3[sl][:], scalar1=-1.0, scalar2=1.0, op0=ALU.max, op1=ALU.min))
            out = None
            waited = 0
            for half in range(2):
                sl = slots[half]
                wsl = WSL3[sl]
                WAITV("tensor", vh[half])
                if sign:
                    WAIT("tensor", sDVE, vsg[half])
                for b in range(16):
                    i = 16 * half + b
                    if rhs_ready is not None and rhs_ready[i] > waited:
                        waited = rhs_ready[i]
                        WAIT("tensor", sACT, waited)
                    fn = (lambda e, wsl=wsl, b=b, i=i, pdst=pdst: e.matmul(
                        pdst[:], wsl[:, 128 * b:128 * b + 128],
                        rhs_sb[:, 512 * i:512 * i + 512], start=(i == 0), stop=(i == ND - 1)))
                    if i == 15 or i == ND - 1:
                        out = OP("tensor", fn)
                        vFree[sl] = out
                    else:
                        MM(fn)
            return out

        vXl_pref = None
        vCSl_pref = None
        vStSlot = {0: 0, 1: 0}
        for tile in range(n_tiles):
            t0 = tile * TT
            e0 = t0 + 256
            if vXl_pref is None:
                WAIT("sync", sACT, lastHprev)
                DMA("sync", sX, xsb[:].rearrange("p (b t) -> p b t", t=TT),
                    xT[:, e0:e0 + TT].rearrange("(b p) t -> p b t", p=128))
                vXl = (sX, sX.n)
                WAIT("sync", sDVE, sDVE.n)
                DMA("sync", sCS, cossb[:], cosE[:, e0:e0 + TT])
                DMA("sync", sCS, sinsb[:], sinE[:, e0:e0 + TT])
                vCSl = (sCS, sCS.n)
            else:
                vXl, vCSl = vXl_pref, vCSl_pref
            WAITV("vector", vXl)
            WAITV("scalar", vXl)
            vB = ln_bcast(mrow[0:1, t0:t0 + 512], rrow[0:1, t0:t0 + 512])
            lastHs = ln_apply(xsb, hsb, CA1, SAv, vB)
            lastH = lastHs[-1]

            vRope = {}
            vLastScores = sPE.n
            vQmm = {}
            qsub_state = {}

            def q_load(h):
                if tile == 0:
                    weave(4)
                vh, slots = [], []
                for half in range(2):
                    sl = mmg_state["hidx"] % 4
                    mmg_state["hidx"] += 1
                    slots.append(sl)
                    WAIT("gpsimd", sPE, vFree[sl])
                    DMA("gpsimd", sGq[sl], WSL3[sl][:].rearrange("p (b f) -> p b f", f=128),
                        wqT[2048 * half:2048 * half + 2048, 128 * h:128 * h + 128]
                        .rearrange("(b p) f -> p b f", p=128))
                    vh.append((sGq[sl], sGq[sl].n))
                qsub_state[h] = (slots, vh)

            def q_sub(h, j, rhs_ready=None):
                slots, vh = qsub_state[h]
                half = j // 2
                sl = slots[half]
                wsl = WSL3[sl]
                if j == 0:
                    if h >= 2:
                        WAITV("tensor", vRope[h - 2])
                    WAIT("tensor", sACT, lastH)
                if j % 2 == 0:
                    WAITV("tensor", vh[half])
                out = None
                waited = 0
                for b in range(8):
                    i = 8 * j + b
                    if rhs_ready is not None and rhs_ready[i] > waited:
                        waited = rhs_ready[i]
                        WAIT("tensor", sACT, waited)
                    fn = (lambda e, wsl=wsl, i=i, h=h: e.matmul(
                        (pQ0 if h % 2 == 0 else pQ1)[:], wsl[:, 128 * (i % 16):128 * (i % 16) + 128],
                        hsb[:, 512 * i:512 * i + 512], start=(i == 0), stop=(i == ND - 1)))
                    if i in (15, ND - 1):
                        out = OP("tensor", fn)
                        vFree[sl] = out
                    else:
                        MM(fn)
                if j == 3:
                    vQmm[h] = out
                return out

            def emit_rope(h):
                qsl2 = h % 2
                WAIT("vector", sPE, vQmm[h])
                WAITV("vector", vCSl)
                if h >= 2:
                    WAIT("vector", sPE, vLastScores)
                pQh = pQ0 if qsl2 == 0 else pQ1
                qs = qslot[:, 512 * qsl2:512 * qsl2 + 512]
                OP("vector", lambda e, pQh=pQh: e.tensor_mul(scrA[:], pQh[:], cossb[:]))
                OP("vector", lambda e, pQh=pQh: e.tensor_mul(scrB[0:64, :], pQh[64:128, :], sinsb[0:64, :]))
                OP("vector", lambda e, pQh=pQh: e.tensor_mul(scrB[64:128, :], pQh[0:64, :], sinsb[64:128, :]))
                OP("vector", lambda e, qs=qs: e.tensor_sub(qs[0:64, :], scrA[0:64, :], scrB[0:64, :]))
                vRope[h] = (sDVE, OP("vector", lambda e, qs=qs: e.tensor_add(qs[64:128, :], scrA[64:128, :], scrB[64:128, :])))

            SUBLAG = 5
            q_load(0)
            for j4 in range(4):
                q_sub(0, j4, rhs_ready=lastHs)
            emit_rope(0)
            q_load(1)
            q_sub(1, 0)
            for h in range(H):
                qsl = h % 2
                for qc in range(4):
                    u = uG
                    uG += 1
                    uTile[u] = tile
                    lqc = 4 * tile + qc
                    lqb = 2 * tile + qc // 2
                    slot = u % 2
                    sm = sums[:, 8 * slot:8 * slot + 8]
                    cf = coefs[:, 4 * slot:4 * slot + 4]
                    qs = qslot[:, 512 * qsl:512 * qsl + 512]
                    WAITV("tensor", vRope[h])
                    if u >= 1:
                        WAIT("tensor", sACT, vSigACT[u - 1])
                    k0 = 256 * lqb
                    for cx in range(6):
                        vSc = OP("tensor", lambda e, qs=qs, qc=qc, k0=k0, cx=cx: e.matmul(
                            pS[:, 128 * cx:128 * cx + 128],
                            kres[:, k0 + 128 * cx:k0 + 128 * cx + 128],
                            qs[:, 128 * qc:128 * qc + 128], start=True, stop=True))
                    vLastScores = vSc
                    WAIT("scalar", sPE, vSc)
                    if u >= 2:
                        WAIT("scalar", sPE, vOmm[u - 2])
                    vSigACT[u] = OP("scalar", lambda e, slot=slot: e.activation(
                        out=sig[:, 768 * slot:768 * slot + 768],
                        in_=pS[:, 0:768], func=AF.Sigmoid, scale=SCALE))
                    if u >= 1 and (u - 1) not in vOmm:
                        emit_O(u - 1)
                        emit_blend(u - 1)
                    if u >= 2 and (u - 2) not in vTpPE:
                        emit_otrans(u - 2)
                    s = 4 * h + qc + SUBLAG
                    if s < 4 * H:
                        h2, j2 = divmod(s, 4)
                        if j2 == 0 and h2 >= 2:
                            q_load(h2)
                        q_sub(h2, j2)
                        if j2 == 3:
                            emit_rope(h2)
            uL = uG - 1
            if (uL - 1) not in vOmm:
                emit_O(uL - 1)
                emit_blend(uL - 1)
            emit_O(uL)
            emit_blend(uL)
            if (uL - 1) not in vTpPE:
                emit_otrans(uL - 1)
            emit_otrans(uL)
            vAttnDone = vEvac[uL]
            mark_scratch()

            vRopeLast = vRope[H - 1][1] if vRope[H - 1][0] is sDVE else None
            vX2 = {}
            vp2 = 0
            for c in range(ND):
                def pre_wo(c=c):
                    if c >= 2:
                        WAIT("tensor", sDVE, vX2[c - 2])
                    if c == 0:
                        WAITV("tensor", vAttnDone)
                if tile == 0:
                    weave(3)
                vW = mm_group(woT, c, aosb, pQ0 if c % 2 == 0 else pQ1, extra_pre=pre_wo, direct=True)
                WAIT("vector", sPE, vW)
                pQc = pQ0 if c % 2 == 0 else pQ1
                vX2[c] = OP("vector", lambda e, c=c, pQc=pQc: e.scalar_tensor_tensor(
                    out=xsb[:, 512 * c:512 * c + 512], in0=pQc[:], scalar=GAC[:, c:c + 1],
                    in1=xsb[:, 512 * c:512 * c + 512], op0=ALU.mult, op1=ALU.add))
                ch = xsb[:, 512 * c:512 * c + 512]
                if c == 0:
                    wait_scratch("vector")
                    OP("vector", lambda e, ch=ch: e.tensor_copy(out=scrA[:], in_=ch))
                    WAIT("scalar", sDVE, vX2[0])
                    aq = OP("scalar", lambda e, ch=ch: e.activation(out=scrB[:], in_=ch, func=AF.Square))
                else:
                    OP("vector", lambda e, ch=ch: e.tensor_add(scrA[:], scrA[:], ch))
                    tmp = scrC if c % 2 == 0 else scrD
                    WAIT("scalar", sDVE, vp2 if vp2 else vX2[c])
                    WAIT("scalar", sDVE, vX2[c])
                    aq = OP("scalar", lambda e, ch=ch, tmp=tmp: e.activation(out=tmp[:], in_=ch, func=AF.Square))
                    WAIT("vector", sACT, aq)
                    vp2 = OP("vector", lambda e, tmp=tmp: e.tensor_add(scrB[:], scrB[:], tmp[:]))

            vTail = ln_tail(rowsb[0:1, 0:512], rowsb[0:1, 512:1024], vp2)
            vB2 = ln_bcast(rowsb[0:1, 0:512], rowsb[0:1, 512:1024], vTail)
            lastH2s = ln_apply(xsb, hsb, CB1, SBv, vB2)
            lastH2 = lastH2s[-1]

            WAIT("tensor", sACT, vOsc0 if tile > 0 else 0)
            vRt = 0
            vLgs = {}
            for tc in range(4):
                if tc == 0:
                    if tile == 0:
                        WAIT("tensor", sGc, vGsmall)
                    waited = 0
                else:
                    WAIT("tensor", sACT, vExp)
                for i in range(ND):
                    if tc == 0 and lastH2s[i] > waited:
                        waited = lastH2s[i]
                        WAIT("tensor", sACT, waited)
                    fn = (lambda e, tc=tc, i=i: e.matmul(
                        pVo0[:, 0:8], hsb[:, 512 * i + 128 * tc:512 * i + 128 * tc + 128],
                        gwb[:, 8 * i:8 * i + 8], start=(i == 0), stop=(i == ND - 1)))
                    if i == ND - 1:
                        vLg = OP("tensor", fn)
                    else:
                        MM(fn)
                WAIT("vector", sPE, vLg)
                OP("vector", lambda e: e.tensor_reduce(out=nmx[:, 0:1], in_=pVo0[:, 0:8], axis=AX.X, op=ALU.max))
                OP("vector", lambda e: e.tensor_scalar(out=nmx[:, 1:2], in0=nmx[:, 0:1],
                                                       scalar1=-1.0, scalar2=None, op0=ALU.mult))
                WAIT("scalar", sDVE, sDVE.n)
                vExp = OP("scalar", lambda e: e.activation(
                    out=routf[:], in_=pVo0[:, 0:8], func=AF.Exp, bias=nmx[:, 1:2],
                    accum_out=nmx[:, 2:3]))
                if tc == 3:
                    vOsc0 = vExp
                WAIT("vector", sACT, vExp)
                OP("vector", lambda e: e.reciprocal(out=nmx[:, 3:4], in_=nmx[:, 2:3]))
                vRb = OP("vector", lambda e: e.tensor_scalar(
                    out=routb[:], in0=routf[:], scalar1=nmx[:, 3:4], scalar2=None, op0=ALU.mult))
                WAIT("tensor", sDVE, vRb)
                if vRt:
                    WAIT("tensor", sDVE, vRt)
                vTr = OP("tensor", lambda e: e.transpose(pTb[0:8, 512:640], routb[:], idbs[:]))
                WAIT("vector", sPE, vTr)
                vRt = OP("vector", lambda e, tc=tc: e.tensor_copy(
                    out=routT[:, 128 * tc:128 * tc + 128], in_=pTb[0:8, 512:640]))

            vXs = {}
            for c in range(ND):
                psl = c % 2
                if c == 0:
                    WAIT("tensor", sDVE, vRt)
                if c >= 2:
                    WAIT("tensor", sDVE, vXs[c - 2])
                vIs = OP("tensor", lambda e, c=c, psl=psl: e.matmul(
                    pS[:, 512 * psl:512 * psl + 512], icsb[:, 128 * c:128 * c + 128],
                    routT[:], start=True, stop=True))
                WAIT("vector", sPE, vIs)
                vXs[c] = OP("vector", lambda e, c=c, psl=psl: e.tensor_mul(
                    aosb[:, 512 * c:512 * c + 512], hsb[:, 512 * c:512 * c + 512],
                    pS[:, 512 * psl:512 * psl + 512]))

            vEp = {}
            vSt = {}
            for c in range(ND):
                if tile == 0 and c == 0:
                    vWf = scr_gate(scrf)
                    WAIT("sync", sScr[0], vWf[0])
                    WAIT("sync", sScr[1], vWf[1])
                def pre_ff(c=c):
                    if c >= 2:
                        WAIT("tensor", sDVE, vEp[c - 2])
                    if c == 0:
                        WAIT("tensor", sDVE, vXs[ND - 1])
                psl = c % 2
                vF = mm_group(scrf, c, aosb, pQ0 if psl == 0 else pQ1, extra_pre=pre_ff, sign=True)
                if c >= 2:
                    WAIT("tensor", sACT, vOsc)
                vOs = OP("tensor", lambda e, c=c, psl=psl: e.matmul(
                    pS[:, 512 * psl:512 * psl + 512], ocsb[:, 128 * c:128 * c + 128],
                    routT[:], start=True, stop=True))
                WAIT("scalar", sPE, vOs)
                if c >= 2:
                    WAIT("scalar", sDVE, vEp[c - 2])
                oscr = scrC if psl == 0 else scrB
                vOsc = OP("scalar", lambda e, psl=psl, oscr=oscr: e.activation(
                    out=oscr[:], in_=pS[:, 512 * psl:512 * psl + 512], func=AF.Copy))
                WAIT("vector", sPE, vF)
                WAIT("vector", sACT, vOsc)
                WAIT("vector", sOg[0], vStSlot[0])
                OP("vector", lambda e, c=c, psl=psl, oscr=oscr: e.scalar_tensor_tensor(
                    out=scrA[:], in0=(pQ0 if psl == 0 else pQ1)[:], scalar=GBC[:, c:c + 1],
                    in1=oscr[:], op0=ALU.mult, op1=ALU.mult))
                vEp[c] = OP("vector", lambda e, c=c: e.scalar_tensor_tensor(
                    out=ostg[:, 0:512], in0=scrA[:], scalar=GBIAS[:, c:c + 1],
                    in1=xsb[:, 512 * c:512 * c + 512], op0=ALU.add, op1=ALU.add))
                WAIT("gpsimd", sDVE, vEp[c])
                DMA("gpsimd", sOg[0], outT[128 * c:128 * c + 128, t0:t0 + TT], ostg[:, 0:512])
                vSt[c] = sOg[0].n
                vStSlot[0] = sOg[0].n
                if tile < n_tiles - 1:
                    e0n = e0 + TT
                    if c == 0:
                        if vRopeLast is not None:
                            WAIT("gpsimd", sDVE, vRopeLast)
                        DMA("gpsimd", sCSg, cossb[:], cosE[:, e0n:e0n + TT])
                        DMA("gpsimd", sCSg, sinsb[:], sinE[:, e0n:e0n + TT])
                        vCSl_pref = (sCSg, sCSg.n)
                    DMA("gpsimd", sXg, xsb[:, 512 * c:512 * c + 512],
                        xT[128 * c:128 * c + 128, e0n:e0n + TT])
                    vXl_pref = (sXg, sXg.n)
            mark_scratch()

        WAIT("gpsimd", sOg[0], sOg[0].n)

        @block.sync
        def _(e):
            for fn in sched["sync"]:
                fn(e)

        @block.tensor
        def _(e):
            for fn in sched["tensor"]:
                fn(e)

        @block.vector
        def _(e):
            for fn in sched["vector"]:
                fn(e)

        @block.scalar
        def _(e):
            for fn in sched["scalar"]:
                fn(e)

        @block.gpsimd
        def _(e):
            for fn in sched["gpsimd"]:
                fn(e)

    return nc



def _rope_cache(S):
    inv_freq = (1.0 / 10000.0 ** (np.arange(0, DH, 2, dtype=np.float32) / DH)).astype(np.float32)
    t = np.arange(S, dtype=np.float32)
    f = np.outer(t, inv_freq)
    cos = np.zeros((S, DH), dtype=np.float32)
    sin = np.zeros((S, DH), dtype=np.float32)
    cos[:, 0::2] = np.cos(f)
    cos[:, 1::2] = np.cos(f)
    sin[:, 0::2] = np.sin(f)
    sin[:, 1::2] = np.sin(f)
    return cos, sin


def make_in_maps(inputs, n_cores, core_t):
    ext = core_t + 512
    nqc = core_t // 128
    nqb = core_t // 256
    nek = ext // 128
    jpc = 24576 // (128 * n_cores)
    S = n_cores * core_t

    f32 = np.float32
    x = np.asarray(inputs["L0_attn_x"], f32)[0]
    assert x.shape == (S, D), x.shape
    xTfull = np.zeros((D, S + 512), f32)
    xTfull[:, 256:256 + S] = np.ascontiguousarray(x.T)
    wq = np.ascontiguousarray(np.asarray(inputs["L0_attn_Wq"], f32).T)
    wk = np.ascontiguousarray(np.asarray(inputs["L0_attn_Wk"], f32).T)
    wv = np.ascontiguousarray(np.asarray(inputs["L0_attn_Wv"], f32).T)
    wo = np.ascontiguousarray(np.asarray(inputs["L0_attn_Wo"], f32).T)
    wf = np.ascontiguousarray(np.asarray(inputs["L0_ffn_weight"], f32).T)
    wmod = np.asarray(inputs["L0_wmod"], f32)
    cvec = np.ascontiguousarray(np.asarray(inputs["L0_c"], f32).reshape(1, D))
    gw = np.ascontiguousarray(np.asarray(inputs["L0_ffn_gate_w"], f32).T)
    ics = np.ascontiguousarray(np.asarray(inputs["L0_ffn_in_channel_scale"], f32))
    ocs = np.ascontiguousarray(np.asarray(inputs["L0_ffn_out_channel_scale"], f32))
    biasM = np.ascontiguousarray(np.asarray(inputs["L0_ffn_bias"], f32).reshape(32, 128).T)

    cos, sin = _rope_cache(S)
    idfA = np.eye(128, dtype=f32)
    idbA = np.eye(128, dtype=np.float32).astype(ml_dtypes.bfloat16)
    onesbA = np.ones((1, 128), dtype=ml_dtypes.bfloat16)
    onescA = np.ones((128, 1), dtype=f32)
    alpha = np.linspace(0.0, 1.0, 256, dtype=f32)

    maps = []
    for r in range(n_cores):
        r0 = r * core_t
        pos = np.arange(r0 - 256, r0 + core_t + 256)
        posc = np.clip(pos, 0, S - 1)
        cosEa = np.ascontiguousarray(cos[posc].T)
        sinEa = np.ascontiguousarray(sin[posc].T)
        aABa = np.zeros((128, 2 * nqc), f32)
        for qc in range(nqc):
            gqb = (r0 // 256) + qc // 2
            half = qc % 2
            asl = alpha[128 * half:128 * half + 128]
            if gqb == 0:
                aABa[:, 2 * qc] = 0.0
                aABa[:, 2 * qc + 1] = 1.0
            else:
                aABa[:, 2 * qc] = 1.0 - asl
                aABa[:, 2 * qc + 1] = asl
        m3a = np.ones((128, nqb), f32)
        for qb in range(nqb):
            if (r0 // 256) + qb == S // 256 - 1:
                m3a[:, qb] = 0.0
        vma = np.ones((128, nek), f32)
        for kb in range(nek):
            glob = r0 - 256 + 128 * kb
            if glob < 0 or glob >= S:
                vma[:, kb] = 0.0
        maps.append({
            "xT": np.ascontiguousarray(xTfull[:, r0:r0 + ext]),
            "wqT": wq, "woT": wo, "wfT": wf, "wkT": wk, "wvT": wv,
            "wmodS": np.ascontiguousarray(wmod[:, 128 * jpc * r:128 * jpc * (r + 1)]),
            "cvec": cvec, "gwT": gw, "icsM": ics, "ocsM": ocs, "biasM": biasM,
            "cosE": cosEa.astype(ml_dtypes.bfloat16), "sinE": sinEa.astype(ml_dtypes.bfloat16), "aABm": aABa, "m3m": m3a,
            "vmkm": vma, "idf": idfA, "onesbI": onesbA, "onescI": onescA, "idbI": idbA,
        })
    return maps


def kernel(**inputs):
    n_cores, core_t = 8, 2048
    maps = make_in_maps(inputs, n_cores, core_t)
    nc = build_nc(n_cores, core_t)
    res = run_bass_kernel_spmd(nc, maps, core_ids=list(range(n_cores)))
    full = np.concatenate([np.asarray(r["outT"]) for r in res.results], axis=1)
    return np.ascontiguousarray(full.T).reshape(1, n_cores * core_t, D).astype(np.float32)
